# version 05
# baseline (speedup 1.0000x reference)
import numpy as np
from contextlib import ExitStack

import concourse.bass as bass
import concourse.mybir as mybir
from concourse.bass_utils import run_bass_kernel_spmd

F32 = mybir.dt.float32
BF16 = mybir.dt.bfloat16
AF = mybir.ActivationFunctionType
ALU = mybir.AluOpType

T = 2048
D = 1024
LAT = 256
DEPTH = 6
CH = 512
NCH = T // CH
DI = 2730
NKT2 = 22
WIN = 256
VSW = 66 * 16
ENGS = ("tensor", "vector", "scalar", "gpsimd", "sync")


def _perm():
    p = np.zeros(D, np.int64)
    for r in range(512):
        h, d = r // 32, r % 32
        p[r] = 64 * h + d
        p[512 + r] = 64 * h + 32 + d
    return p


PERM = _perm()


def _tile_major(w):
    K, M = w.shape
    return np.ascontiguousarray(
        w.reshape(K // 128, 128, M // 128, 128).transpose(2, 1, 0, 3))


def _slab(w):
    K, M = w.shape
    return np.ascontiguousarray(w.reshape(K // 128, 128, M))


def _host_tables():
    t = np.arange(T, dtype=np.float64)
    i = np.arange(32, dtype=np.float64)
    invf = 1.0 / 10000.0 ** (i / 32.0)
    ang = np.outer(invf, t)
    mcos = np.tile(np.cos(ang), (4, 1)).astype(np.float32)
    msin = np.tile(np.sin(ang), (4, 1)).astype(np.float32)

    half = np.arange(0, 64, 2, dtype=np.float64)
    invf2 = 1.0 / 10000.0 ** (half / 64.0)
    freqs = np.outer(invf2, t)
    base = (half + 0.4 * 64) / (1.4 * 64)
    power = (t - T // 2) / (WIN // 2)
    scale = base[:, None] ** power[None, :]
    xc, xs = np.cos(freqs), np.sin(freqs)
    tabc = np.tile(xc * scale, (4, 1)).astype(np.float32)
    tabs = np.tile(xs * scale, (4, 1)).astype(np.float32)
    tabci = np.tile(xc / scale, (4, 1)).astype(np.float32)
    tabsi = np.tile(xs / scale, (4, 1)).astype(np.float32)

    NEG = np.float32(-1e30)
    kp = np.arange(128)[:, None]
    qf = np.arange(512)[None, :]
    mtri = np.concatenate(
        [((128 * r + kp) > qf) * NEG for r in range(4)], axis=1).astype(np.float32)
    qr = np.arange(256)[None, :]
    win = np.zeros((128, 1024), np.float32)
    for kt in range(4):
        kk = 128 * kt + kp
        win[:, 256 * kt:256 * kt + 256] = ((kk < qr) | (kk > qr + 256)) * NEG
    win0 = np.zeros((128, 512), np.float32)
    for kt in range(2):
        kk = 128 * kt + kp
        win0[:, 256 * kt:256 * kt + 256] = (kk > qr) * NEG

    onesk = np.ones((128, 128), np.float32)
    bo32 = np.zeros((128, 128), np.float32)
    for b in range(4):
        bo32[32 * b:32 * b + 32, 32 * b:32 * b + 32] = 1.0
    return dict(mcos=mcos, msin=msin, tabc=tabc, tabs=tabs, tabci=tabci,
                tabsi=tabsi, mtri=mtri, win=win, win0=win0, onesk=onesk,
                bo32=bo32)


def _prep(ins):
    o = {}
    o.update(_host_tables())
    o["wq_t"] = _tile_major(np.asarray(ins["L0_attn_W_q"])[:, PERM])
    o["wdkv_t"] = _tile_major(np.asarray(ins["L0_attn_W_dkv"]))
    o["wuk_t"] = _tile_major(np.asarray(ins["L0_attn_W_uk"])[:, PERM])
    o["wuv_s"] = _slab(np.asarray(ins["L0_attn_W_uv"]))
    o["wo_t"] = _tile_major(np.asarray(ins["L0_attn_W_o"]))
    o["gam_c"] = np.ascontiguousarray(np.asarray(ins["L0_attn_ln_gamma"]).reshape(2, 128).T)
    o["bet_c"] = np.ascontiguousarray(np.asarray(ins["L0_attn_ln_beta"]).reshape(2, 128).T)

    wqkv = np.asarray(ins["L0_ffn_Wqkv"])
    qk = np.concatenate([wqkv[:, :, 0:D][:, :, PERM],
                         wqkv[:, :, D:2 * D][:, :, PERM]], axis=2)
    o["wqkv_qk_t"] = np.stack([_tile_major(qk[l]) for l in range(DEPTH)])
    o["wqkv_v_s"] = np.stack([_slab(wqkv[l][:, 2 * D:3 * D]) for l in range(DEPTH)])
    o["wo2_t"] = np.stack([_tile_major(np.asarray(ins["L0_ffn_Wo"])[l]) for l in range(DEPTH)])

    w1 = np.asarray(ins["L0_ffn_W1"])
    b1 = np.asarray(ins["L0_ffn_b1"])
    w1p = np.zeros((DEPTH, D, 44 * 128), np.float32)
    b1a = np.zeros((DEPTH, 128, NKT2), np.float32)
    b1g = np.zeros((DEPTH, 128, NKT2), np.float32)
    for i in range(NKT2):
        c0 = 128 * i
        n = min(128, DI - c0)
        w1p[:, :, 256 * i:256 * i + n] = w1[:, :, c0:c0 + n]
        w1p[:, :, 256 * i + 128:256 * i + 128 + n] = w1[:, :, DI + c0:DI + c0 + n]
        b1a[:, :n, i] = b1[:, c0:c0 + n]
        b1g[:, :n, i] = b1[:, DI + c0:DI + c0 + n]
    w1t = np.stack([_tile_major(w1p[l]) for l in range(DEPTH)])
    o["w1_t"] = np.ascontiguousarray(
        w1t.reshape(DEPTH, NKT2, 2, 128, 8, 128).transpose(0, 1, 3, 2, 4, 5))
    o["b1a_c"] = b1a
    o["b1g_c"] = b1g
    w2 = np.asarray(ins["L0_ffn_W2"])
    w2p = np.zeros((DEPTH, NKT2 * 128, D), np.float32)
    w2p[:, :DI, :] = w2
    o["w2_t"] = np.stack([_tile_major(w2p[l]) for l in range(DEPTH)])
    o["b2_c"] = np.ascontiguousarray(np.asarray(ins["L0_ffn_b2"]).reshape(DEPTH, 8, 128).transpose(0, 2, 1))
    o["g_c"] = np.ascontiguousarray(np.asarray(ins["L0_ffn_ln_g"]).reshape(DEPTH, 8, 128).transpose(0, 2, 1))
    o["bln_c"] = np.ascontiguousarray(np.asarray(ins["L0_ffn_ln_b"]).reshape(DEPTH, 8, 128).transpose(0, 2, 1))
    qs = np.asarray(ins["L0_ffn_q_scale"])
    ks = np.asarray(ins["L0_ffn_k_scale"])
    o["qsk_c"] = np.stack([
        np.stack([np.tile(qs[l, 0:32], 4), np.tile(qs[l, 32:64], 4),
                  np.tile(ks[l, 0:32], 4), np.tile(ks[l, 32:64], 4)], axis=1)
        for l in range(DEPTH)]).astype(np.float32)
    return o


class K:
    def __init__(self, nc, es):
        self.nc = nc
        self.ops = {e: [] for e in ENGS}
        self.sems = {}
        self.cnt = {}
        self.lastw = {e: {} for e in ENGS}
        self.bufev = {}
        self.es = es
        self.trace = {e: [] for e in ENGS}
        for s in ("pe", "dv", "ac"):
            self.sems[s] = es.enter_context(nc.semaphore(s))
            self.cnt[s] = 0

    def emit(self, eng, fn):
        self.ops[eng].append(fn)

    def wait(self, eng, sem, val):
        if val is None or val <= 0 or self.lastw[eng].get(sem, 0) >= val:
            return
        self.lastw[eng][sem] = val
        s = self.sems[sem]
        self.trace[eng].append(("w", sem, val))
        self.emit(eng, lambda e, s=s, v=val: e.wait_ge(s, v))

    def op(self, eng, fn, sem=None, dn=1):
        if sem is None:
            self.emit(eng, fn)
            return None
        if sem in ("dv", "ac"):
            self.wait(eng, sem, self.cnt[sem])
        self.cnt[sem] += dn
        s = self.sems[sem]
        self.trace[eng].append(("i", sem, dn))
        self.emit(eng, lambda e, fn=fn, s=s, d=dn: fn(e).then_inc(s, d))
        return self.cnt[sem]

    def check_deadlock(self):
        pos = {e: 0 for e in ENGS}
        cur = {s: 0 for s in self.cnt}
        tr = self.trace
        progress = True
        while progress:
            progress = False
            for e in ENGS:
                while pos[e] < len(tr[e]):
                    kind, sem, val = tr[e][pos[e]]
                    if kind == "i":
                        cur[sem] += val
                        pos[e] += 1
                        progress = True
                    else:
                        if cur[sem] >= val:
                            pos[e] += 1
                            progress = True
                        else:
                            break
        stuck = {e: (pos[e], len(tr[e])) for e in ENGS if pos[e] < len(tr[e])}
        if stuck:
            for e, (p, n) in stuck.items():
                print("DEADLOCK", e, "at", p, "/", n, "on", tr[e][p],
                      "cur=", {s: cur[s] for s in cur})
            return False
        return True

    def chan(self, name):
        if name not in self.sems:
            self.sems[name] = self.es.enter_context(self.nc.semaphore(name))
            self.cnt[name] = 0
        return name

    def dma(self, eng, out, in_, sem, war=None):
        self.chan(sem)
        if war is not None:
            for (s2, v2) in self.bufev.get(war, ()):
                self.wait(eng, s2, v2)
        return self.op(eng, lambda e, o=out, i=in_: e.dma_start(o, i), sem, 16)

    def use(self, key, sem, val):
        self.bufev[key] = ((sem, val),)

    def barrier(self):
        snap = dict(self.cnt)
        for eng in ENGS:
            for s, v in snap.items():
                self.wait(eng, s, v)

    def flush(self):
        nonempty = [e for e in ENGS if self.ops[e]]
        if not nonempty:
            return
        with self.nc.Block() as block:
            for eng in nonempty:
                lst = self.ops[eng]

                def body(e, lst=lst):
                    for f in lst:
                        f(e)
                getattr(block, eng)(body)
        self.ops = {e: [] for e in ENGS}


class FMB:
    def __init__(self, t, w):
        self.t = t
        self.w = w

    def sl(self, p, o, n, r0=0, r1=128):
        return self.t[r0:r1, p * self.w + o: p * self.w + o + n]


def build_nc(taps=(), stop=None):
    nc = bass.Bass("TRN2", target_bir_lowering=False)
    top = ExitStack()
    k = K(nc, top)

    dr = {}
    dr["xT"] = nc.dram_tensor("xT", [D, T], F32, kind="ExternalInput")
    dr["y_out"] = nc.dram_tensor("y_out", [D, T], F32, kind="ExternalOutput")
    dr["xo_sp"] = nc.dram_tensor("xo_sp", [D, T], F32, kind="Internal")
    dr["y0_sp"] = nc.dram_tensor("y0_sp", [D, T], F32, kind="Internal")
    shapes = dict(
        wq_t=[8, 128, 8, 128], wdkv_t=[2, 128, 8, 128], wuk_t=[8, 128, 2, 128],
        wuv_s=[2, 128, D], wo_t=[8, 128, 8, 128], gam_c=[128, 2], bet_c=[128, 2],
        wqkv_qk_t=[DEPTH, 16, 128, 8, 128], wqkv_v_s=[DEPTH, 8, 128, D],
        wo2_t=[DEPTH, 8, 128, 8, 128], w1_t=[DEPTH, NKT2, 128, 2, 8, 128],
        b1a_c=[DEPTH, 128, NKT2], b1g_c=[DEPTH, 128, NKT2],
        w2_t=[DEPTH, 8, 128, NKT2, 128], b2_c=[DEPTH, 128, 8],
        g_c=[DEPTH, 128, 8], bln_c=[DEPTH, 128, 8], qsk_c=[DEPTH, 128, 4],
        mcos=[128, T], msin=[128, T], tabc=[128, T], tabs=[128, T],
        tabci=[128, T], tabsi=[128, T], mtri=[128, 4 * 512], win=[128, 1024],
        win0=[128, 512], onesk=[128, 128], bo32=[128, 128],
    )
    for n, s in shapes.items():
        dr[n] = nc.dram_tensor(n, s, F32, kind="ExternalInput")
    tap_out = {}
    for name in taps:
        tap_out[name] = nc.dram_tensor("tap_" + name, [D, T], F32, kind="ExternalOutput")

    tapstage = [None]
    if tap_out:
        tapstage[0] = top.enter_context(nc.sbuf_tensor("tapstage", [128, T], F32))

    def tapT(name, tens, w):
        if name not in tap_out:
            return
        dt = tap_out[name]
        k.barrier()
        dvv = k.op("vector", lambda v, w=w: v.tensor_copy(tapstage[0][:, 0:w], tens[:, 0:w]), "dv")
        k.wait("sync", "dv", dvv)
        wsv = k.dma("sync", dt[0:128, 0:w], tapstage[0][:, 0:w], "tap")
        k.wait("vector", "tap", wsv)
        k.barrier()

    def tap(name, fmb, col0=0, w=T):
        if name not in tap_out:
            return
        dt = tap_out[name]
        k.barrier()
        for p in range(8):
            dvv = k.op("vector", lambda v, p=p, w=w, c0=col0:
                       v.tensor_copy(tapstage[0][:, 0:w], fmb.sl(p, c0, w)), "dv")
            k.wait("sync", "dv", dvv)
            wsv = k.dma("sync", dt[128 * p:128 * p + 128, 0:w], tapstage[0][:, 0:w], "tap")
            k.wait("vector", "tap", wsv)
        k.barrier()

    psA = top.enter_context(nc.psum_tensor("psA", [128, 512], F32))
    psB = top.enter_context(nc.psum_tensor("psB", [128, 512], F32))
    psS0 = top.enter_context(nc.psum_tensor("psS0", [128, 1024], F32))
    psS1 = top.enter_context(nc.psum_tensor("psS1", [128, 1024], F32))
    psV0 = top.enter_context(nc.psum_tensor("psV0", [128, 512], F32))
    psV1 = top.enter_context(nc.psum_tensor("psV1", [128, 512], F32))
    psG = [psA, psB]
    psS = [psS0, psS1]
    psV = [psV0, psV1]

    onesk = top.enter_context(nc.sbuf_tensor("onesk_b", [128, 128], BF16))
    bo32 = top.enter_context(nc.sbuf_tensor("bo32_b", [128, 128], BF16))
    tabs4 = top.enter_context(nc.sbuf_tensor("tabs4", [128, 4 * T], BF16))
    mask_b = top.enter_context(nc.sbuf_tensor("mask_b", [128, 2048], BF16))
    smalls = top.enter_context(nc.sbuf_tensor("smalls", [128, 80], F32))

    epsc = top.enter_context(nc.sbuf_tensor("epsc", [128, 1], F32))
    onesf = top.enter_context(nc.sbuf_tensor("onesf", [33, 64], F32))
    k.dma("gpsimd", onesk[:, :], dr["onesk"][:, :], "cst")
    k.dma("gpsimd", bo32[:, :], dr["bo32"][:, :], "cst")
    k.dma("sync", smalls[:, 0:2], dr["gam_c"][:, :], "sml")
    k.dma("sync", smalls[:, 2:4], dr["bet_c"][:, :], "sml")
    k.op("vector", lambda v: v.memset(epsc[:, :], 1e-5), "dv")
    k.op("vector", lambda v: v.memset(onesf[:, :], 1.0), "dv")

    gemm_ev = {}

    def gemm_wait_bank(b):
        k.wait("tensor", "dv", gemm_ev.get(("dv", b), 0))
        k.wait("tensor", "ac", gemm_ev.get(("ac", b), 0))

    def mark(kind, b, v):
        gemm_ev[(kind, b)] = max(gemm_ev.get((kind, b), 0), v)

    def ln_from_sums(mu_ap, rstd_ap, st1, st2, n, wdt):
        k.op("vector", lambda v: v.tensor_scalar(mu_ap, psG[0][:, 0:wdt],
                                                 1.0 / n, None, ALU.mult), "dv")
        dva = k.op("vector", lambda v: v.tensor_scalar(st1[:, 0:wdt], psG[1][:, 0:wdt],
                                                       1.0 / n, None, ALU.mult), "dv")
        mark("dv", 0, dva)
        mark("dv", 1, dva)
        k.op("vector", lambda v: v.tensor_mul(st2[:, 0:wdt], mu_ap, mu_ap), "dv")
        dvv = k.op("vector", lambda v: v.tensor_sub(st1[:, 0:wdt], st1[:, 0:wdt],
                                                    st2[:, 0:wdt]), "dv")
        k.wait("scalar", "dv", dvv)
        k.op("scalar", lambda s: s.activation(st2[:, 0:wdt], st1[:, 0:wdt], AF.Ln,
                                              bias=epsc[:, 0:1]), "ac")
        return k.op("scalar", lambda s: s.activation(rstd_ap, st2[:, 0:wdt],
                                                     AF.Exp, scale=-0.5), "ac")

    def pair_norm_evac(obc_ap0, obc_ap1, denf, rcpd, st1, jw):
        k.wait("scalar", "pe", k.cnt["pe"])
        k.op("scalar", lambda s: s.copy(denf[0:1, 0:jw], psV[0][64:65, 0:jw]), "ac")
        acv = k.op("scalar", lambda s: s.copy(denf[32:33, 0:jw],
                                              psV[1][64:65, 0:jw]), "ac")
        k.wait("tensor", "ac", acv)
        gemm_wait_bank(1)
        k.op("tensor", lambda t: t.matmul(psG[1][0:64, 0:jw], onesf[0:1, 0:64],
                                          denf[0:1, 0:jw], start=True, stop=True), "pe")
        pev = k.op("tensor", lambda t: t.matmul(psG[1][64:128, 0:jw], onesf[32:33, 0:64],
                                                denf[32:33, 0:jw], start=True, stop=True,
                                                tile_position=(32, 64)), "pe")
        k.wait("scalar", "pe", pev)
        k.op("scalar", lambda s: s.activation(st1[:, 0:jw], psG[1][:, 0:jw], AF.Ln), "ac")
        acv = k.op("scalar", lambda s: s.activation(rcpd[:, 0:jw], st1[:, 0:jw],
                                                    AF.Exp, scale=-1.0), "ac")
        mark("ac", 1, acv)
        k.wait("vector", "ac", acv)
        k.op("vector", lambda v: v.tensor_mul(obc_ap0, psV[0][0:64, 0:jw],
                                              rcpd[0:64, 0:jw]), "dv")
        dvv = k.op("vector", lambda v: v.tensor_mul(obc_ap1, psV[1][0:64, 0:jw],
                                                    rcpd[64:128, 0:jw]), "dv")
        return dvv

    def emit_mla():
        with ExitStack() as ph:
            sb = lambda n_, s_, d_: ph.enter_context(nc.sbuf_tensor(n_, s_, d_))
            qp = FMB(sb("qp", [128, 8 * T], BF16), T)
            kp = FMB(sb("kp", [128, 8 * T], BF16), T)
            vsl = sb("vsl", [128, 16 * VSW], BF16)
            latnb = FMB(sb("latnb", [128, 2 * T], BF16), T)
            wqb = [sb("wqb0", [128, 8 * 128], BF16), sb("wqb1", [128, 8 * 128], BF16)]

            k.dma("gpsimd", tabs4[:, 0:T], dr["mcos"][:, :], "cst")
            k.dma("gpsimd", tabs4[:, T:2 * T], dr["msin"][:, :], "cst")
            k.dma("gpsimd", mask_b[:, 0:4 * 512], dr["mtri"][:, :], "cst")
            vset = k.op("vector", lambda v: v.memset(vsl[:, :], 1.0), "dv")
            k.barrier()

            with ExitStack() as p2:
                s2 = lambda n_, s_, d_: p2.enter_context(nc.sbuf_tensor(n_, s_, d_))
                xbc = FMB(s2("xbc", [128, 8 * CH], BF16), CH)
                sqc = FMB(s2("sqc", [128, 2 * CH], BF16), CH)
                xfc = FMB(s2("xfc", [128, 8 * CH], F32), CH)
                xnc = FMB(s2("xnc", [128, 8 * CH], BF16), CH)
                mu = s2("mu", [128, CH], F32)
                rstd = s2("rstd", [128, CH], F32)
                st1 = s2("st1", [128, CH], F32)
                st2 = s2("st2", [128, CH], F32)
                tA = s2("tA", [128, CH], F32)
                tB = s2("tB", [128, CH], F32)
                qrp = FMB(s2("qrp", [128, 2 * CH], BF16), CH)
                latfc = FMB(s2("latfc", [128, 2 * CH], F32), CH)

                def rope_pair(pr, dstf, dstoff, c, co, so):
                    for (dp, sA, sB, sub) in ((pr, 0, 1, True), (pr + 4, 1, 0, False)):
                        k.op("vector", lambda v, sA=sA, co=co, c=c:
                             v.tensor_mul(tA[:, :], qrp.sl(sA, 0, CH),
                                          tabs4[:, co + c * CH:co + c * CH + CH]), "dv")
                        k.op("vector", lambda v, sB=sB, so=so, c=c:
                             v.tensor_mul(tB[:, :], qrp.sl(sB, 0, CH),
                                          tabs4[:, so + c * CH:so + c * CH + CH]), "dv")
                        if sub:
                            dvv = k.op("vector", lambda v, dp=dp:
                                       v.tensor_sub(dstf.sl(dp, dstoff, CH), tA[:, :], tB[:, :]), "dv")
                        else:
                            dvv = k.op("vector", lambda v, dp=dp:
                                       v.tensor_add(dstf.sl(dp, dstoff, CH), tA[:, :], tB[:, :]), "dv")
                    return dvv

                def stats_pass(src_fn, nf, do_square, sq_fn=None):
                    lastA = {}
                    sqv = {}
                    for p in range(nf):
                        if do_square:
                            k.wait("scalar", "pe", lastA.get(p - 2, 0))
                            sqv[p] = k.op("scalar", lambda s, p=p:
                                          s.activation(sqc.sl(p % 2, 0, CH), src_fn(p),
                                                       AF.Square), "ac")
                        k.op("tensor", lambda t, p=p, s=(p == 0), e=(p == nf - 1):
                             t.matmul(psG[0][:, :], onesk[:, :], src_fn(p),
                                      start=s, stop=e), "pe")
                        if do_square:
                            k.wait("tensor", "ac", sqv[p])
                            src2 = lambda p=p: sqc.sl(p % 2, 0, CH)
                        else:
                            src2 = lambda p=p: sq_fn(p)
                        pev = k.op("tensor", lambda t, p=p, s=(p == 0), e=(p == nf - 1), s2=src2:
                                   t.matmul(psG[1][:, :], onesk[:, :], s2(),
                                            start=s, stop=e), "pe")
                        lastA[p] = pev
                    return pev

                for c in range(NCH):
                    xbv = None
                    xfv = None
                    for p in range(8):
                        xbv = k.dma("gpsimd", xbc.sl(p, 0, CH),
                                    dr["xT"][128 * p:128 * p + 128, c * CH:c * CH + CH], "xbc")
                        xfv = k.dma("sync", xfc.sl(p, 0, CH),
                                    dr["xT"][128 * p:128 * p + 128, c * CH:c * CH + CH], "xfc")
                    k.wait("scalar", "xbc", xbv)
                    k.wait("tensor", "xbc", xbv)
                    gemm_wait_bank(0)
                    gemm_wait_bank(1)
                    pev = stats_pass(lambda p: xbc.sl(p, 0, CH), 8, True)
                    k.wait("vector", "pe", pev)
                    acv = ln_from_sums(mu[:, :], rstd[:, :], st1, st2, D, CH)
                    k.wait("vector", "xfc", xfv)
                    k.wait("vector", "ac", acv)
                    if c == 3:
                        tapT("mu_dbg", mu, CH)
                        tapT("rstd_dbg", rstd, CH)
                    xnv = None
                    for p in range(8):
                        k.op("vector", lambda v, p=p:
                             v.tensor_sub(st2[:, :], xfc.sl(p, 0, CH), mu[:, :]), "dv")
                        xnv = k.op("vector", lambda v, p=p:
                                   v.tensor_mul(xnc.sl(p, 0, CH), st2[:, :],
                                                rstd[:, :]), "dv")
                    k.wait("tensor", "dv", xnv)

                    import os as _o
                    for pr in range(0 if _o.environ.get("SKIP_Q") else 4):
                        for half, p in ((0, pr), (1, pr + 4)):
                            wv = k.dma("gpsimd", wqb[half][:, :],
                                       dr["wq_t"][p].rearrange("p k j -> p (k j)"),
                                       "wqb%d" % half, war="wqb%d" % half)
                            k.wait("tensor", "wqb%d" % half, wv)
                            gemm_wait_bank(half)
                            pev = None
                            for kk in range(8):
                                pev = k.op("tensor", lambda t, h2=half, kk=kk,
                                           s=(kk == 0), e=(kk == 7):
                                           t.matmul(psG[h2][:, :], wqb[h2][:, 128 * kk:128 * kk + 128],
                                                    xnc.sl(kk, 0, CH), start=s, stop=e), "pe")
                            k.use("wqb%d" % half, "pe", pev)
                            k.wait("vector", "pe", pev)
                            dvv = k.op("vector", lambda v, h2=half:
                                       v.tensor_copy(qrp.sl(h2, 0, CH), psG[h2][:, :]), "dv")
                            mark("dv", half, dvv)
                        rope_pair(pr, qp, c * CH, c, 0, T)

                    LP = int(_o.environ.get("LATPART", "5"))
                    for lp in range(0 if _o.environ.get("SKIP_LAT") else 2):
                        wv = k.dma("gpsimd", wqb[lp][:, :],
                                   dr["wdkv_t"][lp].rearrange("p k j -> p (k j)"),
                                   "wqb%d" % lp, war="wqb%d" % lp)
                        k.wait("tensor", "wqb%d" % lp, wv)
                        gemm_wait_bank(lp)
                        pev = None
                        for kk in range(8):
                            pev = k.op("tensor", lambda t, lp=lp, kk=kk,
                                       s=(kk == 0), e=(kk == 7):
                                       t.matmul(psG[lp][:, :], wqb[lp][:, 128 * kk:128 * kk + 128],
                                                xnc.sl(kk, 0, CH), start=s, stop=e), "pe")
                        k.use("wqb%d" % lp, "pe", pev)
                        if LP >= 2:
                            k.wait("vector", "pe", pev)
                            dvv = k.op("vector", lambda v, lp=lp:
                                       v.tensor_copy(latfc.sl(lp, 0, CH), psG[lp][:, :]), "dv")
                            mark("dv", lp, dvv)
                        if LP >= 3:
                            k.wait("scalar", "dv", dvv)
                            acv = k.op("scalar", lambda s, lp=lp:
                                       s.copy(qrp.sl(lp, 0, CH), latfc.sl(lp, 0, CH)), "ac")
                    if not _o.environ.get("SKIP_LAT") and LP >= 4:
                        k.wait("tensor", "ac", k.cnt["ac"])
                        gemm_wait_bank(0)
                        gemm_wait_bank(1)
                        pev = stats_pass(lambda p: qrp.sl(p, 0, CH), 2, True)
                        k.wait("vector", "pe", pev)
                        acv = ln_from_sums(mu[:, :], rstd[:, :], st1, st2, LAT, CH)
                        if LP >= 5:
                            k.wait("vector", "ac", acv)
                            lnv = None
                            for lp in range(2):
                                k.op("vector", lambda v, lp=lp:
                                     v.tensor_sub(st2[:, :], latfc.sl(lp, 0, CH), mu[:, :]), "dv")
                                k.op("vector", lambda v, lp=lp:
                                     v.tensor_mul(latfc.sl(lp, 0, CH), st2[:, :], rstd[:, :]), "dv")
                                lnv = k.op("vector", lambda v, lp=lp, c=c:
                                           v.tensor_scalar(latnb.sl(lp, c * CH, CH), latfc.sl(lp, 0, CH),
                                                           smalls[:, lp:lp + 1], smalls[:, 2 + lp:3 + lp],
                                                           ALU.mult, ALU.add), "dv")
                            k.wait("tensor", "dv", lnv)

                    for pr in range(0 if (_o.environ.get("SKIP_K") or _o.environ.get("SKIP_LAT")) else 4):
                        for half, p in ((0, pr), (1, pr + 4)):
                            wv = k.dma("gpsimd", wqb[half][:, 0:2 * 128],
                                       dr["wuk_t"][p].rearrange("p k j -> p (k j)"),
                                       "wqb%d" % half, war="wqb%d" % half)
                            k.wait("tensor", "wqb%d" % half, wv)
                            gemm_wait_bank(half)
                            pev = None
                            for kk in range(2):
                                pev = k.op("tensor", lambda t, h2=half, kk=kk, c=c,
                                           s=(kk == 0), e=(kk == 1):
                                           t.matmul(psG[h2][:, :], wqb[h2][:, 128 * kk:128 * kk + 128],
                                                    latnb.sl(kk, c * CH, CH), start=s, stop=e), "pe")
                            k.use("wqb%d" % half, "pe", pev)
                            k.wait("vector", "pe", pev)
                            dvv = k.op("vector", lambda v, h2=half:
                                       v.tensor_copy(qrp.sl(h2, 0, CH), psG[h2][:, :]), "dv")
                            mark("dv", half, dvv)
                        rope_pair(pr, kp, c * CH, c, 0, T)

                    if _o.environ.get("SKIP_V") or _o.environ.get("SKIP_LAT"):
                        k.barrier()
                        continue
                    for lk in range(2):
                        k.dma("gpsimd", wqb[lk][:, 0:512], dr["wuv_s"][lk][:, 0:512],
                              "wqb%d" % lk, war="wqb%d" % lk)
                        wvv = k.dma("gpsimd", wqb[lk][:, 512:1024], dr["wuv_s"][lk][:, 512:1024],
                                    "wqb%d" % lk)
                        k.wait("tensor", "wqb%d" % lk, wvv)
                    k.wait("tensor", "dv", k.cnt["dv"])
                    for ti in range(4):
                        tt = 4 * c + ti
                        for jh in range(2):
                            b = (ti * 2 + jh) % 2
                            gemm_wait_bank(b)
                            pev = None
                            for lk in range(2):
                                pev = k.op("tensor", lambda t, lk=lk, ti=ti, jh=jh, b=b, c=c,
                                           s=(lk == 0), e=(lk == 1):
                                           t.matmul(psG[b][:, :],
                                                    latnb.sl(lk, c * CH + ti * 128, 128),
                                                    wqb[lk][:, jh * 512:jh * 512 + 512],
                                                    start=s, stop=e), "pe")
                            k.wait("vector", "pe", pev)
                            k.wait("vector", "dv", vset)
                            dst = bass.AP(vsl, tt * VSW + jh * 8 * 66,
                                          [[16 * VSW, 128], [66, 8], [1, 64]])
                            src = bass.AP(psG[b], 0, [[512, 128], [64, 8], [1, 64]])
                            dvv = k.op("vector", lambda v, d=dst, s3=src: v.tensor_copy(d, s3), "dv")
                            mark("dv", b, dvv)
                    k.use("wqb0", "pe", k.cnt["pe"])
                    k.use("wqb1", "pe", k.cnt["pe"])
                    k.barrier()
                tap("xn_mla", xnc, 0, CH)
                k.flush()
            k.barrier()
            tap("qp", qp)
            tap("kp", kp)
            tapT("vsl_dbg", vsl, 2048)
            tapT("latnb_dbg", latnb.t, 2048)

            import os as _os
            if (_os.environ.get("KSTOP") or "") == "mlaproj":
                k.barrier()
                k.flush()
                return
            with ExitStack() as p3:
                s3 = lambda n_, s_, d_: p3.enter_context(nc.sbuf_tensor(n_, s_, d_))
                ptb = [s3("pt%d" % i, [128, 512], BF16) for i in range(4)]
                stg = [s3("stg%d" % i, [128, 512], F32) for i in range(2)]
                obc = FMB(s3("obc", [128, 8 * 512], BF16), 512)
                rcpd = s3("rcpd", [128, 512], F32)
                denf = s3("denf", [33, 512], F32)
                st1 = s3("st1b", [128, 512], F32)
                st2 = s3("st2b", [128, CH], F32)
                xfc = FMB(s3("xfc2", [128, 8 * CH], F32), CH)
                xoc = FMB(s3("xoc", [128, 8 * CH], F32), CH)
                xob = FMB(s3("xob", [128, 4 * CH], BF16), CH)
                mu = s3("mu2", [128, CH], F32)
                rstd = s3("rstd2", [128, CH], F32)
                wqb2 = [s3("wqc0", [128, 8 * 128], BF16), s3("wqc1", [128, 8 * 128], BF16)]

                AP_ = int(os.environ.get("ATTPART", "5"))
                for qc in range(NCH):
                    nkt = 4 * qc + 4
                    for gq in range(4):
                        for sp in range(2):
                            h0 = 4 * gq + 2 * sp
                            heads = ((0, h0, 64 * sp), (1, h0 + 1, 64 * sp + 32))
                            expv = {0: 0, 1: 0}
                            for kt in range(nkt):
                                par = kt % 2
                                for hi, h, a in heads:
                                    sl = psS[hi][:, par * 512:par * 512 + 512]
                                    k.wait("tensor", "ac", expv[hi])
                                    k.op("tensor", lambda t, gq=gq, a=a, kt=kt, qc=qc, sl=sl:
                                         t.matmul(sl, kp.sl(gq, kt * 128, 128, a, a + 32),
                                                  qp.sl(gq, qc * CH, CH, a, a + 32),
                                                  start=True, stop=False, tile_position=(a, 0)), "pe")
                                    pev = k.op("tensor", lambda t, gq=gq, a=a, kt=kt, qc=qc, sl=sl:
                                               t.matmul(sl, kp.sl(gq + 4, kt * 128, 128, a, a + 32),
                                                        qp.sl(gq + 4, qc * CH, CH, a, a + 32),
                                                        start=False, stop=True,
                                                        tile_position=(a, 0)), "pe")
                                    r = kt - 4 * qc
                                    pt = ptb[2 * hi + par]
                                    if r >= 0:
                                        k.wait("vector", "pe", pev)
                                        k.wait("vector", "ac", gemm_ev.get(("stg", hi), 0))
                                        mdv = k.op("vector", lambda v, sl=sl, r=r, hi=hi:
                                                   v.tensor_add(stg[hi][:, :], sl,
                                                                mask_b[:, r * 512:r * 512 + 512]), "dv")
                                        k.wait("scalar", "dv", mdv)
                                        acv = k.op("scalar", lambda s, hi=hi, pt=pt:
                                                   s.activation(pt[:, :], stg[hi][:, :],
                                                                AF.Exp, scale=0.125), "ac")
                                        gemm_ev[("stg", hi)] = acv
                                        expv[hi] = acv
                                        if AP_ < 2:
                                            continue
                                        k.wait("tensor", "ac", acv)
                                        if kt == 0:
                                            k.wait("tensor", "dv", gemm_ev.get(("pnorm", 0), 0))
                                        vap2 = bass.AP(vsl, kt * VSW + 66 * h,
                                                       [[16 * VSW, 128], [1, 65]])
                                        k.op("tensor", lambda t, va=vap2, pt=pt, hi=hi,
                                             s=(kt == 0), e=(kt == nkt - 1):
                                             t.matmul(psV[hi][0:65, :], va, pt[:, :],
                                                      start=s, stop=e), "pe")
                                        continue
                                    k.wait("scalar", "pe", pev)
                                    acv = k.op("scalar", lambda s, sl=sl, pt=pt:
                                               s.activation(pt[:, :], sl, AF.Exp, scale=0.125), "ac")
                                    expv[hi] = acv
                                    if AP_ < 2:
                                        continue
                                    k.wait("tensor", "ac", acv)
                                    if kt == 0:
                                        k.wait("tensor", "dv", gemm_ev.get(("pnorm", 0), 0))
                                    vap2 = bass.AP(vsl, kt * VSW + 66 * h,
                                                   [[16 * VSW, 128], [1, 65]])
                                    k.op("tensor", lambda t, va=vap2, pt=pt, hi=hi,
                                         s=(kt == 0), e=(kt == nkt - 1):
                                         t.matmul(psV[hi][0:65, :], va, pt[:, :],
                                                  start=s, stop=e), "pe")
                            pev = k.cnt["pe"]
                            k.wait("vector", "pe", pev)
                            if AP_ < 3:
                                continue
                            p = h0 // 2
                            dvv = pair_norm_evac(obc.sl(p, 0, 512, 0, 64),
                                                 obc.sl(p, 0, 512, 64, 128),
                                                 denf, rcpd, st1, 512)
                            k.wait("tensor", "dv", dvv)
                    obv = k.cnt["dv"]
                    if qc == NCH - 1:
                        tap("obc_mla", obc, 0, 512)
                    if AP_ < 4:
                        continue

                    xfv = None
                    for p in range(8):
                        xfv = k.dma("sync", xfc.sl(p, 0, CH),
                                    dr["xT"][128 * p:128 * p + 128, qc * CH:qc * CH + CH], "xfc")
                    for p in range(8):
                        wv = k.dma("gpsimd", wqb2[p % 2][:, :],
                                   dr["wo_t"][p].rearrange("p k j -> p (k j)"),
                                   "wqc%d" % (p % 2), war="wqc%d" % (p % 2))
                        k.wait("tensor", "wqc%d" % (p % 2), wv)
                        k.wait("tensor", "dv", obv)
                        b = p % 2
                        gemm_wait_bank(b)
                        pev = None
                        for kk in range(8):
                            pev = k.op("tensor", lambda t, b=b, kk=kk, s=(kk == 0), e=(kk == 7):
                                       t.matmul(psG[b][:, :], wqb2[b][:, 128 * kk:128 * kk + 128],
                                                obc.sl(kk, 0, 512), start=s, stop=e), "pe")
                        k.use("wqc%d" % b, "pe", pev)
                        k.wait("vector", "pe", pev)
                        k.wait("vector", "xfc", xfv)
                        dvv = k.op("vector", lambda v, p=p, b=b:
                                   v.tensor_add(xoc.sl(p, 0, CH), psG[b][:, :],
                                                xfc.sl(p, 0, CH)), "dv")
                        mark("dv", b, dvv)
                        k.wait("sync", "dv", dvv)
                        k.dma("sync", dr["xo_sp"][128 * p:128 * p + 128, qc * CH:qc * CH + CH],
                              xoc.sl(p, 0, CH), "spill")
                    xonv = k.cnt["dv"]
                    spillv = k.cnt["spill"]
                    gemm_wait_bank(0)
                    gemm_wait_bank(1)
                    lastA = {}
                    for p in range(8):
                        k.wait("scalar", "dv", xonv)
                        k.wait("scalar", "pe", lastA.get(p - 2, 0))
                        cpv = k.op("scalar", lambda s, p=p:
                                   s.activation(xob.sl(p % 2, 0, CH), xoc.sl(p, 0, CH),
                                                AF.Identity), "ac")
                        k.wait("tensor", "ac", cpv)
                        k.op("tensor", lambda t, p=p, s=(p == 0), e=(p == 7):
                             t.matmul(psG[0][:, :], onesk[:, :], xob.sl(p % 2, 0, CH),
                                      start=s, stop=e), "pe")
                        sqv = k.op("scalar", lambda s, p=p:
                                   s.activation(xob.sl(2 + p % 2, 0, CH), xob.sl(p % 2, 0, CH),
                                                AF.Square), "ac")
                        k.wait("tensor", "ac", sqv)
                        pev = k.op("tensor", lambda t, p=p, s=(p == 0), e=(p == 7):
                                   t.matmul(psG[1][:, :], onesk[:, :], xob.sl(2 + p % 2, 0, CH),
                                            start=s, stop=e), "pe")
                        lastA[p] = pev
                    k.wait("vector", "pe", pev)
                    acv = ln_from_sums(mu[:, :], rstd[:, :], st1, st2, D, CH)
                    k.wait("vector", "ac", acv)
                    k.wait("vector", "spill", spillv)
                    dvv = None
                    for p in range(8):
                        k.op("vector", lambda v, p=p:
                             v.tensor_sub(st2[:, :], xoc.sl(p, 0, CH), mu[:, :]), "dv")
                        dvv = k.op("vector", lambda v, p=p:
                                   v.tensor_mul(xoc.sl(p, 0, CH), st2[:, :],
                                                rstd[:, :]), "dv")
                        k.wait("sync", "dv", dvv)
                        k.dma("sync", dr["y0_sp"][128 * p:128 * p + 128, qc * CH:qc * CH + CH],
                              xoc.sl(p, 0, CH), "spill")
                    k.barrier()
                k.barrier()
                k.flush()
            tap("qp2", qp)
            tap("kp2", kp)
            tapT("vsl2_dbg", vsl, 2048)
            k.flush()

    def emit_layer(l, yfm):
        k.dma("sync", smalls[:, 4:8], dr["qsk_c"][l], "sml")
        k.dma("sync", smalls[:, 8:16], dr["g_c"][l], "sml")
        k.dma("sync", smalls[:, 16:24], dr["bln_c"][l], "sml")
        k.dma("sync", smalls[:, 24:32], dr["b2_c"][l], "sml")
        k.dma("sync", smalls[:, 32:54], dr["b1a_c"][l], "sml")
        k.dma("sync", smalls[:, 54:76], dr["b1g_c"][l], "sml")
        if l == 0:
            k.dma("gpsimd", tabs4[:, 0:T], dr["tabc"][:, :], "cst")
            k.dma("gpsimd", tabs4[:, T:2 * T], dr["tabs"][:, :], "cst")
            k.dma("gpsimd", tabs4[:, 2 * T:3 * T], dr["tabci"][:, :], "cst")
            k.dma("gpsimd", tabs4[:, 3 * T:4 * T], dr["tabsi"][:, :], "cst")
            k.dma("gpsimd", mask_b[:, 0:1024], dr["win"][:, :], "cst")
            k.dma("gpsimd", mask_b[:, 1024:1536], dr["win0"][:, :], "cst")
        k.barrier()

        with ExitStack() as ph:
            sb = lambda n_, s_, d_: ph.enter_context(nc.sbuf_tensor("%s_L%d" % (n_, l), s_, d_))
            kpb = FMB(sb("kpb", [128, 8 * 768], BF16), 768)
            vbl = sb("vbl", [128, 6 * VSW], BF16)
            ybc = FMB(sb("ybc", [128, 8 * 513], BF16), 513)
            qpc = FMB(sb("qpc", [128, 8 * CH], BF16), CH)
            qrp = FMB(sb("qrp2", [128, 2 * CH], BF16), CH)
            qsq = FMB(sb("qsq", [128, 2 * CH], BF16), CH)
            rsqf = sb("rsqf", [128, CH], F32)
            tA = sb("tA2", [128, CH], F32)
            tB = sb("tB2", [128, CH], F32)
            tC = sb("tC2", [128, CH], F32)
            st1 = sb("st1c", [128, CH], F32)
            wqb = [sb("wqd0", [128, 8 * 128], BF16), sb("wqd1", [128, 8 * 128], BF16)]
            wvb = FMB(sb("wvb", [128, 8 * 1024], BF16), 1024)
            ptb2 = [[sb("ptA0", [128, 1024], BF16), sb("ptA1", [128, 1024], BF16)],
                    [sb("ptB0", [128, 1024], BF16), sb("ptB1", [128, 1024], BF16)]]
            stg2 = [sb("stgA", [128, 1024], F32), sb("stgB", [128, 1024], F32)]
            obc = FMB(sb("obc2", [128, 8 * 512], BF16), 512)
            rcpd = sb("rcpd2", [128, 256], F32)
            denf = sb("denf2", [33, 256], F32)

            vset = k.op("vector", lambda v: v.memset(vbl[:, :], 1.0), "dv")

            for c in range(NCH):
                if c > 0:
                    for p in range(8):
                        k.op("vector", lambda v, p=p:
                             v.tensor_copy(kpb.sl(p, 0, 256), kpb.sl(p, 512, 256)), "dv")
                    for s4 in range(2):
                        k.op("vector", lambda v, s4=s4:
                             v.tensor_copy(vbl[:, s4 * VSW:(s4 + 1) * VSW],
                                           vbl[:, (4 + s4) * VSW:(5 + s4) * VSW]), "dv")
                for p in range(8):
                    if c == 0:
                        k.op("vector", lambda v, p=p: v.memset(ybc.sl(p, 0, 1), 0.0), "dv")
                    else:
                        k.op("vector", lambda v, p=p:
                             v.tensor_copy(ybc.sl(p, 0, 1), ybc.sl(p, 512, 1)), "dv")
                ycv = None
                for p in range(8):
                    ycv = k.op("vector", lambda v, p=p, c=c:
                               v.tensor_copy(ybc.sl(p, 1, 512), yfm.sl(p, 1 + c * CH, 512)), "dv")
                k.wait("tensor", "dv", ycv)

                def rhs_sh(kk, off, n):
                    return ybc.sl(kk, off + (0 if kk >= 4 else 1), n)

                qrp_rd = {0: 0, 1: 0}
                rope_dv = 0
                for (ofb, co, so, c1, c2) in ((0, 0, T, 4, 5), (8, 2 * T, 3 * T, 6, 7)):
                    for pr in range(4):
                        for half, p in ((0, pr), (1, pr + 4)):
                            wv = k.dma("gpsimd", wqb[half][:, :],
                                       dr["wqkv_qk_t"][l, ofb + p].rearrange("p k j -> p (k j)"),
                                       "wqd%d" % half, war="wqd%d" % half)
                            k.wait("tensor", "wqd%d" % half, wv)
                            gemm_wait_bank(half)
                            pev = None
                            for kk in range(8):
                                pev = k.op("tensor", lambda t, h2=half, kk=kk,
                                           s=(kk == 0), e=(kk == 7):
                                           t.matmul(psG[h2][:, :],
                                                    wqb[h2][:, 128 * kk:128 * kk + 128],
                                                    rhs_sh(kk, 0, CH), start=s, stop=e), "pe")
                            k.use("wqd%d" % half, "pe", pev)
                            k.wait("scalar", "pe", pev)
                            k.wait("scalar", "dv", qrp_rd[half])
                            k.op("scalar", lambda s, h2=half:
                                 s.copy(qrp.sl(h2, 0, CH), psG[h2][:, :]), "ac")
                            acv = k.op("scalar", lambda s, h2=half:
                                       s.activation(qsq.sl(h2, 0, CH), qrp.sl(h2, 0, CH),
                                                    AF.Square), "ac")
                            mark("ac", half, acv)
                        k.wait("tensor", "ac", k.cnt["ac"])
                        gemm_wait_bank(0)
                        k.op("tensor", lambda t: t.matmul(psG[0][:, :], bo32[:, :],
                                                          qsq.sl(0, 0, CH),
                                                          start=True, stop=False), "pe")
                        pev = k.op("tensor", lambda t: t.matmul(psG[0][:, :], bo32[:, :],
                                                                qsq.sl(1, 0, CH),
                                                                start=False, stop=True), "pe")
                        k.wait("scalar", "pe", pev)
                        k.wait("scalar", "dv", rope_dv)
                        k.op("scalar", lambda s: s.activation(st1[:, :], psG[0][:, :], AF.Ln), "ac")
                        acv = k.op("scalar", lambda s: s.activation(tC[:, :], st1[:, :],
                                                                    AF.Exp, scale=-0.5), "ac")
                        mark("ac", 0, acv)
                        k.wait("vector", "ac", acv)
                        k.op("vector", lambda v: v.tensor_scalar_min(rsqf[:, :], tC[:, :],
                                                                     1e12), "dv")
                        qrp_rd[0] = k.op("vector", lambda v, c1=c1: v.scalar_tensor_tensor(
                            tA[:, :], qrp.sl(0, 0, CH), smalls[:, c1:c1 + 1], rsqf[:, :],
                            ALU.mult, ALU.mult), "dv")
                        dvv = k.op("vector", lambda v, c2=c2: v.scalar_tensor_tensor(
                            tB[:, :], qrp.sl(1, 0, CH), smalls[:, c2:c2 + 1], rsqf[:, :],
                            ALU.mult, ALU.mult), "dv")
                        qrp_rd[1] = dvv
                        for (dp, tX, tY, sub) in ((pr, tA, tB, True), (pr + 4, tB, tA, False)):
                            k.op("vector", lambda v, tX=tX, co=co, c=c:
                                 v.tensor_mul(tC[:, :], tX[:, :],
                                              tabs4[:, co + c * CH:co + c * CH + CH]), "dv")
                            k.op("vector", lambda v, tY=tY, so=so, c=c:
                                 v.tensor_mul(st1[:, :], tY[:, :],
                                              tabs4[:, so + c * CH:so + c * CH + CH]), "dv")
                            if ofb == 0:
                                outap = qpc.sl(dp, 0, CH)
                            else:
                                outap = kpb.sl(dp, 256, CH)
                            if sub:
                                k.op("vector", lambda v, o=outap:
                                     v.tensor_sub(o, tC[:, :], st1[:, :]), "dv")
                            else:
                                k.op("vector", lambda v, o=outap:
                                     v.tensor_add(o, tC[:, :], st1[:, :]), "dv")
                        rope_dv = k.cnt["dv"]

                wvv = None
                for kk in range(8):
                    wvv = k.dma("gpsimd", wvb.sl(kk, 0, 1024),
                                dr["wqkv_v_s"][l, kk][:, :], "wvb",
                                war="wvb" if kk == 0 else None)
                k.wait("tensor", "wvb", wvv)
                k.wait("tensor", "dv", k.cnt["dv"])
                for ti in range(4):
                    for jh in range(2):
                        b = (ti * 2 + jh) % 2
                        gemm_wait_bank(b)
                        pev = None
                        for kk in range(8):
                            pev = k.op("tensor", lambda t, kk=kk, ti=ti, jh=jh, b=b,
                                       s=(kk == 0), e=(kk == 7):
                                       t.matmul(psG[b][:, :], rhs_sh(kk, ti * 128, 128),
                                                wvb.sl(kk, jh * 512, 512), start=s, stop=e), "pe")
                        k.wait("vector", "pe", pev)
                        k.wait("vector", "dv", vset)
                        dst = bass.AP(vbl, (2 + ti) * VSW + jh * 8 * 66,
                                      [[6 * VSW, 128], [66, 8], [1, 64]])
                        src = bass.AP(psG[b], 0, [[512, 128], [64, 8], [1, 64]])
                        dvv = k.op("vector", lambda v, d=dst, s5=src: v.tensor_copy(d, s5), "dv")
                        mark("dv", b, dvv)
                k.use("wvb", "pe", k.cnt["pe"])

                pairs = []
                for w in (2 * c, 2 * c + 1):
                    kts = [2, 3] if (w == 0 and c == 0) else [0, 1, 2, 3]
                    qoff = (w - 2 * c) * 256
                    for gq in range(4):
                        for sp in range(2):
                            pairs.append((w, tuple(kts), qoff, gq, sp))

                def emit_av(rec):
                    idx, heads, kts, qoff, w = rec
                    par = idx % 2
                    nk = len(kts)
                    for hi, h, a in heads:
                        k.wait("tensor", "ac", gemm_ev[("exp", (par, hi))])
                        k.wait("tensor", "dv", gemm_ev.get(("pnorm", 0), 0))
                        pev = None
                        for ki, kt in enumerate(kts):
                            slot = 2 * (w - 2 * c) + kt
                            vap2 = bass.AP(vbl, slot * VSW + 66 * h,
                                           [[6 * VSW, 128], [1, 65]])
                            pev = k.op("tensor", lambda t, va=vap2, hi=hi, ki=ki, par=par,
                                       s=(ki == 0), e=(ki == nk - 1):
                                       t.matmul(psV[hi][0:65, 0:256], va,
                                                ptb2[par][hi][:, ki * 256:ki * 256 + 256],
                                                start=s, stop=e), "pe")
                        gemm_ev[("ptfree", (par, hi))] = pev
                    k.wait("vector", "pe", k.cnt["pe"])
                    p2h = heads[0][1] // 2
                    dvv = pair_norm_evac(obc.sl(p2h, qoff, 256, 0, 64),
                                         obc.sl(p2h, qoff, 256, 64, 128),
                                         denf, rcpd, st1, 256)
                    gemm_ev[("pnorm", 0)] = dvv

                prev = None
                for idx, (w, kts, qoff, gq, sp) in enumerate(pairs):
                    nk = len(kts)
                    par = idx % 2
                    h0 = 4 * gq + 2 * sp
                    heads = ((0, h0, 64 * sp), (1, h0 + 1, 64 * sp + 32))
                    for hi, h, a in heads:
                        k.wait("tensor", "dv", gemm_ev.get(("maskdv", hi), 0))
                        pev = None
                        for ki, kt in enumerate(kts):
                            k.op("tensor", lambda t, gq=gq, a=a, kt=kt, ki=ki, hi=hi, qoff=qoff, w=w, c=c:
                                 t.matmul(psS[hi][:, ki * 256:ki * 256 + 256],
                                          kpb.sl(gq, (w - 2 * c) * 256 + kt * 128, 128, a, a + 32),
                                          qpc.sl(gq, qoff, 256, a, a + 32),
                                          start=True, stop=False, tile_position=(a, 0)), "pe")
                            pev = k.op("tensor", lambda t, gq=gq, a=a, kt=kt, ki=ki, hi=hi, qoff=qoff, w=w, c=c:
                                       t.matmul(psS[hi][:, ki * 256:ki * 256 + 256],
                                                kpb.sl(gq + 4, (w - 2 * c) * 256 + kt * 128, 128, a, a + 32),
                                                qpc.sl(gq + 4, qoff, 256, a, a + 32),
                                                start=False, stop=True,
                                                tile_position=(a, 0)), "pe")
                        k.wait("vector", "pe", pev)
                        moff = 1024 if (w == 0) else 0
                        k.wait("vector", "ac", gemm_ev.get(("stg2", hi), 0))
                        mdv = k.op("vector", lambda v, hi=hi, nk=nk, moff=moff:
                                   v.tensor_add(stg2[hi][:, 0:nk * 256], psS[hi][:, 0:nk * 256],
                                                mask_b[:, moff:moff + nk * 256]), "dv")
                        gemm_ev[("maskdv", hi)] = mdv
                        k.wait("scalar", "dv", mdv)
                        k.wait("scalar", "pe", gemm_ev.get(("ptfree", (par, hi)), 0))
                        acv = k.op("scalar", lambda s, hi=hi, nk=nk, par=par:
                                   s.activation(ptb2[par][hi][:, 0:nk * 256], stg2[hi][:, 0:nk * 256],
                                                AF.Exp, scale=8.0), "ac")
                        gemm_ev[("exp", (par, hi))] = acv
                        gemm_ev[("stg2", hi)] = acv
                    if prev is not None:
                        emit_av(prev)
                    prev = (idx, heads, kts, qoff, w)
                emit_av(prev)
                obv = k.cnt["dv"]

                for p in range(8):
                    wv = k.dma("gpsimd", wqb[p % 2][:, :],
                               dr["wo2_t"][l, p].rearrange("p k j -> p (k j)"),
                               "wqd%d" % (p % 2), war="wqd%d" % (p % 2))
                    k.wait("tensor", "wqd%d" % (p % 2), wv)
                    k.wait("tensor", "dv", obv)
                    b = p % 2
                    gemm_wait_bank(b)
                    pev = None
                    for kk in range(8):
                        pev = k.op("tensor", lambda t, b=b, kk=kk, s=(kk == 0), e=(kk == 7):
                                   t.matmul(psG[b][:, :], wqb[b][:, 128 * kk:128 * kk + 128],
                                            obc.sl(kk, 0, 512), start=s, stop=e), "pe")
                    k.use("wqd%d" % b, "pe", pev)
                    k.wait("vector", "pe", pev)
                    dvv = k.op("vector", lambda v, p=p, b=b, c=c:
                               v.tensor_add(yfm.sl(p, 1 + c * CH, CH), psG[b][:, :],
                                            yfm.sl(p, 1 + c * CH, CH)), "dv")
                    mark("dv", b, dvv)
                k.barrier()
                k.flush()
            k.barrier()
        k.flush()

        with ExitStack() as ph:
            sb = lambda n_, s_, d_: ph.enter_context(nc.sbuf_tensor("%s_F%d" % (n_, l), s_, d_))
            ybc1 = FMB(sb("ybc1", [128, 8 * 1025], BF16), 1025)
            sqp = FMB(sb("sqp", [128, 2 * 513], BF16), 513)
            ybk = sb("ybk", [128, 8], F32)
            xnbc = FMB(sb("xnbc", [128, 8 * 1024], BF16), 1024)
            gg = FMB(sb("gg", [128, NKT2 * 1024], BF16), 1024)
            w1b = [sb("w1b0", [128, 2 * 1024], BF16), sb("w1b1", [128, 2 * 1024], BF16)]
            w2b = [sb("w2b0", [128, NKT2 * 128], BF16), sb("w2b1", [128, NKT2 * 128], BF16)]
            gel = [sb("gel0", [128, 512], BF16), sb("gel1", [128, 512], BF16)]
            mu = sb("mu3", [128, 1024], F32)
            rstd = sb("rstd3", [128, 1024], F32)
            st1 = sb("st1d", [128, 512], F32)
            st2 = sb("st2d", [128, 512], F32)
            st3 = sb("st3d", [128, 512], F32)

            for F in range(2):
                F0 = F * 1024
                for p in range(8):
                    if F == 0:
                        k.op("vector", lambda v, p=p: v.memset(ybc1.sl(p, 0, 1), 0.0), "dv")
                    else:
                        k.op("vector", lambda v, p=p:
                             v.tensor_copy(ybc1.sl(p, 0, 1), ybc1.sl(p, 1024, 1)), "dv")
                ycv = None
                for p in range(8):
                    ycv = k.op("vector", lambda v, p=p, F0=F0:
                               v.tensor_copy(ybc1.sl(p, 1, 1024), yfm.sl(p, F0 + 1, 1024)), "dv")
                if F == 0:
                    for p in range(8):
                        ycv = k.op("vector", lambda v, p=p:
                                   v.tensor_copy(ybk[:, p:p + 1], yfm.sl(p, 1024, 1)), "dv")
                k.wait("tensor", "dv", ycv)
                k.wait("scalar", "dv", ycv)

                def rhs_sh1(kk, off, n):
                    return ybc1.sl(kk, off + (0 if kk >= 4 else 1), n)

                for jc in range(2):
                    gemm_wait_bank(0)
                    gemm_wait_bank(1)
                    lastA = {}
                    sqv = {}
                    for p in range(8):
                        k.wait("scalar", "pe", lastA.get(p - 2, 0))
                        sqv[p] = k.op("scalar", lambda s, p=p, jc=jc:
                                      s.activation(sqp.sl(p % 2, 0, 513),
                                                   ybc1.sl(p, jc * 512, 513), AF.Square), "ac")
                        k.op("tensor", lambda t, p=p, jc=jc, s=(p == 0), e=(p == 7):
                             t.matmul(psG[0][:, :], onesk[:, :], rhs_sh1(p, jc * 512, 512),
                                      start=s, stop=e), "pe")
                        k.wait("tensor", "ac", sqv[p])
                        pev = k.op("tensor", lambda t, p=p, jc=jc, s=(p == 0), e=(p == 7):
                                   t.matmul(psG[1][:, :], onesk[:, :],
                                            sqp.sl(p % 2, (0 if p >= 4 else 1), 512),
                                            start=s, stop=e), "pe")
                        lastA[p] = pev
                    k.wait("vector", "pe", pev)
                    acv = ln_from_sums(mu[:, jc * 512:jc * 512 + 512],
                                       rstd[:, jc * 512:jc * 512 + 512], st1, st2, D, 512)
                k.wait("vector", "ac", acv)
                xnv = [0, 0]
                for jc in range(2):
                    for p in range(8):
                        src = yfm.sl(p, F0 + jc * 512 + (0 if p >= 4 else 1), 512)
                        k.op("vector", lambda v, p=p, jc=jc, src=src:
                             v.tensor_sub(st3[:, :], src, mu[:, jc * 512:jc * 512 + 512]), "dv")
                        if p >= 4 and F == 1 and jc == 0:
                            k.op("vector", lambda v, p=p:
                                 v.tensor_sub(st3[:, 0:1], ybk[:, p:p + 1], mu[:, 0:1]), "dv")
                        k.op("vector", lambda v, jc=jc:
                             v.tensor_mul(st1[:, :], st3[:, :],
                                          rstd[:, jc * 512:jc * 512 + 512]), "dv")
                        xnv[jc] = k.op("vector", lambda v, p=p, jc=jc:
                                       v.tensor_scalar(xnbc.sl(p, jc * 512, 512), st1[:, :],
                                                       smalls[:, 8 + p:9 + p],
                                                       smalls[:, 16 + p:17 + p],
                                                       ALU.mult, ALU.add), "dv")

                psFF = [psG[0], psG[1], psV[0], psV[1]]
                for i in range(NKT2):
                    wv = k.dma("gpsimd", w1b[i % 2][:, :],
                               dr["w1_t"][l, i].rearrange("p o k j -> p (o k j)"),
                               "w1b%d" % (i % 2), war="w1b%d" % (i % 2))
                    k.wait("tensor", "w1b%d" % (i % 2), wv)
                    for jc in range(2):
                        k.wait("tensor", "dv", xnv[jc])
                        ba = 2 * (i % 2)
                        bg = ba + 1
                        gemm_wait_bank(ba)
                        gemm_wait_bank(bg)
                        pev = None
                        for kk in range(8):
                            k.op("tensor", lambda t, i=i, kk=kk, jc=jc, ba=ba, s=(kk == 0), e=(kk == 7):
                                 t.matmul(psFF[ba][:, :], w1b[i % 2][:, 128 * kk:128 * kk + 128],
                                          xnbc.sl(kk, jc * 512, 512), start=s, stop=e), "pe")
                            pev = k.op("tensor", lambda t, i=i, kk=kk, jc=jc, bg=bg,
                                       s=(kk == 0), e=(kk == 7):
                                       t.matmul(psFF[bg][:, :],
                                                w1b[i % 2][:, 1024 + 128 * kk:1024 + 128 * kk + 128],
                                                xnbc.sl(kk, jc * 512, 512), start=s, stop=e), "pe")
                        k.use("w1b%d" % (i % 2), "pe", pev)
                        k.wait("scalar", "pe", pev)
                        k.wait("scalar", "dv", gemm_ev.get(("gel", jc), 0))
                        gcv = k.op("scalar", lambda s, i=i, jc=jc, bg=bg:
                                   s.activation(gel[jc][:, :], psFF[bg][:, :], AF.Gelu,
                                                bias=smalls[:, 54 + i:55 + i]), "ac")
                        mark("ac", bg, gcv)
                        k.wait("vector", "ac", gcv)
                        dvv = k.op("vector", lambda v, i=i, jc=jc, ba=ba:
                                   v.scalar_tensor_tensor(gg.sl(i, jc * 512, 512), psFF[ba][:, :],
                                                          smalls[:, 32 + i:33 + i], gel[jc][:, :],
                                                          ALU.add, ALU.mult), "dv")
                        mark("dv", ba, dvv)
                        gemm_ev[("gel", jc)] = dvv
                ggv = k.cnt["dv"]

                for p in range(8):
                    wv = k.dma("gpsimd", w2b[p % 2][:, :],
                               dr["w2_t"][l, p].rearrange("p k j -> p (k j)"),
                               "w2b%d" % (p % 2), war="w2b%d" % (p % 2))
                    k.wait("tensor", "w2b%d" % (p % 2), wv)
                    k.wait("tensor", "dv", ggv)
                    for jc in range(2):
                        b = (2 * p + jc) % 4
                        gemm_wait_bank(b)
                        pev = None
                        for kk in range(NKT2):
                            pev = k.op("tensor", lambda t, p=p, kk=kk, jc=jc, b=b,
                                       s=(kk == 0), e=(kk == NKT2 - 1):
                                       t.matmul(psFF[b][:, :], w2b[p % 2][:, 128 * kk:128 * kk + 128],
                                                gg.sl(kk, jc * 512, 512), start=s, stop=e), "pe")
                        k.wait("vector", "pe", pev)
                        dvv = k.op("vector", lambda v, p=p, jc=jc, b=b, F0=F0:
                                   v.scalar_tensor_tensor(yfm.sl(p, F0 + 1 + jc * 512, 512),
                                                          psFF[b][:, :], smalls[:, 24 + p:25 + p],
                                                          yfm.sl(p, F0 + 1 + jc * 512, 512),
                                                          ALU.add, ALU.add), "dv")
                        mark("dv", b, dvv)
                        k.use("w2b%d" % (p % 2), "pe", pev)
                k.barrier()
                k.flush()
            k.barrier()
        k.flush()

    import os
    stop = stop or os.environ.get("KSTOP") or None
    emit_mla()

    if os.environ.get("NOYS"):
        with ExitStack() as ph:
            dum = FMB(ph.enter_context(nc.sbuf_tensor("dumout", [128, 8 * CH], F32)), CH)
            for c in range(NCH):
                dvv = None
                for p in range(8):
                    dvv = k.op("vector", lambda v, p=p: v.memset(dum.sl(p, 0, CH), 0.0), "dv")
                k.wait("sync", "dv", dvv)
                for p in range(8):
                    k.dma("sync", dr["y_out"][128 * p:128 * p + 128, c * CH:c * CH + CH],
                          dum.sl(p, 0, CH), "out")
            k.barrier()
            k.flush()
        return nc, top, k, tap_out

    with ExitStack() as rest:
        yfm = FMB(rest.enter_context(nc.sbuf_tensor("ystream", [128, 8 * (T + 1)], F32)),
                  T + 1)
        ylv = None
        for p in range(8):
            k.op("vector", lambda v, p=p: v.memset(yfm.sl(p, 0, 1), 0.0), "dv")
            ylv = k.dma("sync", yfm.sl(p, 1, T), dr["y0_sp"][128 * p:128 * p + 128, :], "y0ld")
        k.barrier()
        tap("y0", yfm, 1)

        nlay = DEPTH
        if stop in ("mla", "mlaproj"):
            nlay = 0
        elif stop and stop.startswith("l"):
            nlay = int(stop[1:]) + 1
        for l in range(nlay):
            emit_layer(l, yfm)
            tap("y_l%d" % l, yfm, 1)

        with ExitStack() as ph:
            sb = lambda n_, s_, d_: ph.enter_context(nc.sbuf_tensor(n_, s_, d_))
            xfc = FMB(sb("xfc3", [128, 8 * CH], F32), CH)
            outc = FMB(sb("outc", [128, 8 * CH], F32), CH)
            for c in range(NCH):
                xfv = None
                for p in range(8):
                    xfv = k.dma("sync", xfc.sl(p, 0, CH),
                                dr["xo_sp"][128 * p:128 * p + 128, c * CH:c * CH + CH], "xfc",
                                war="xfcbuf" if p == 0 else None)
                k.wait("vector", "xfc", xfv)
                dvv = None
                for p in range(8):
                    dvv = k.op("vector", lambda v, p=p, c=c:
                               v.tensor_add(outc.sl(p, 0, CH), xfc.sl(p, 0, CH),
                                            yfm.sl(p, 1 + c * CH, CH)), "dv")
                k.use("xfcbuf", "dv", dvv)

                k.wait("sync", "dv", dvv)
                for p in range(8):
                    k.dma("sync", dr["y_out"][128 * p:128 * p + 128, c * CH:c * CH + CH],
                          outc.sl(p, 0, CH), "out")
                k.wait("vector", "out", k.cnt["out"])
            k.barrier()
            k.flush()

    return nc, top, k, tap_out


def kernel(**ins):
    prep = _prep(ins)
    x = np.asarray(ins["L0_attn_x"])
    nc, top, k, _ = build_nc()
    with top:
        in_maps = []
        for b in range(8):
            m = {"xT": np.ascontiguousarray(x[b].T)}
            m.update(prep)
            in_maps.append(m)
        res = run_bass_kernel_spmd(nc, in_maps, core_ids=list(range(8)))
    return np.stack([np.ascontiguousarray(r["y_out"].T) for r in res.results]).astype(np.float32)


# version 06
# speedup vs baseline: 1.0222x; 1.0222x over previous
import numpy as np
from contextlib import ExitStack

import concourse.bass as bass
import concourse.mybir as mybir
from concourse.bass_utils import run_bass_kernel_spmd

F32 = mybir.dt.float32
BF16 = mybir.dt.bfloat16
AF = mybir.ActivationFunctionType
ALU = mybir.AluOpType

T = 2048
D = 1024
LAT = 256
DEPTH = 6
CH = 512
NCH = T // CH
DI = 2730
NKT2 = 22
WIN = 256
VSW = 66 * 16
ENGS = ("tensor", "vector", "scalar", "gpsimd", "sync")


def _perm():
    p = np.zeros(D, np.int64)
    for r in range(512):
        h, d = r // 32, r % 32
        p[r] = 64 * h + d
        p[512 + r] = 64 * h + 32 + d
    return p


PERM = _perm()


def _tile_major(w):
    K, M = w.shape
    return np.ascontiguousarray(
        w.reshape(K // 128, 128, M // 128, 128).transpose(2, 1, 0, 3))


def _slab(w):
    K, M = w.shape
    return np.ascontiguousarray(w.reshape(K // 128, 128, M))


def _host_tables():
    t = np.arange(T, dtype=np.float64)
    i = np.arange(32, dtype=np.float64)
    invf = 1.0 / 10000.0 ** (i / 32.0)
    ang = np.outer(invf, t)
    mcos = np.tile(np.cos(ang), (4, 1)).astype(np.float32)
    msin = np.tile(np.sin(ang), (4, 1)).astype(np.float32)

    half = np.arange(0, 64, 2, dtype=np.float64)
    invf2 = 1.0 / 10000.0 ** (half / 64.0)
    freqs = np.outer(invf2, t)
    base = (half + 0.4 * 64) / (1.4 * 64)
    power = (t - T // 2) / (WIN // 2)
    scale = base[:, None] ** power[None, :]
    xc, xs = np.cos(freqs), np.sin(freqs)
    tabc = np.tile(xc * scale, (4, 1)).astype(np.float32)
    tabs = np.tile(xs * scale, (4, 1)).astype(np.float32)
    tabci = np.tile(xc / scale, (4, 1)).astype(np.float32)
    tabsi = np.tile(xs / scale, (4, 1)).astype(np.float32)

    NEG = np.float32(-1e30)
    kp = np.arange(128)[:, None]
    qf = np.arange(512)[None, :]
    mtri = np.concatenate(
        [((128 * r + kp) > qf) * NEG for r in range(4)], axis=1).astype(np.float32)
    qr = np.arange(256)[None, :]
    win = np.zeros((128, 1024), np.float32)
    for kt in range(4):
        kk = 128 * kt + kp
        win[:, 256 * kt:256 * kt + 256] = ((kk < qr) | (kk > qr + 256)) * NEG
    win0 = np.zeros((128, 512), np.float32)
    for kt in range(2):
        kk = 128 * kt + kp
        win0[:, 256 * kt:256 * kt + 256] = (kk > qr) * NEG

    onesk = np.ones((128, 128), np.float32)
    bo32 = np.zeros((128, 128), np.float32)
    for b in range(4):
        bo32[32 * b:32 * b + 32, 32 * b:32 * b + 32] = 1.0
    return dict(mcos=mcos, msin=msin, tabc=tabc, tabs=tabs, tabci=tabci,
                tabsi=tabsi, mtri=mtri, win=win, win0=win0, onesk=onesk,
                bo32=bo32)


def _prep(ins):
    o = {}
    o.update(_host_tables())
    o["wq_t"] = _tile_major(np.asarray(ins["L0_attn_W_q"])[:, PERM])
    o["wdkv_t"] = _tile_major(np.asarray(ins["L0_attn_W_dkv"]))
    o["wuk_t"] = _tile_major(np.asarray(ins["L0_attn_W_uk"])[:, PERM])
    o["wuv_s"] = _slab(np.asarray(ins["L0_attn_W_uv"]))
    o["wo_t"] = _tile_major(np.asarray(ins["L0_attn_W_o"]))
    o["gam_c"] = np.ascontiguousarray(np.asarray(ins["L0_attn_ln_gamma"]).reshape(2, 128).T)
    o["bet_c"] = np.ascontiguousarray(np.asarray(ins["L0_attn_ln_beta"]).reshape(2, 128).T)

    wqkv = np.asarray(ins["L0_ffn_Wqkv"])
    qk = np.concatenate([wqkv[:, :, 0:D][:, :, PERM],
                         wqkv[:, :, D:2 * D][:, :, PERM]], axis=2)
    o["wqkv_qk_t"] = np.stack([_tile_major(qk[l]) for l in range(DEPTH)])
    o["wqkv_v_s"] = np.stack([_slab(wqkv[l][:, 2 * D:3 * D]) for l in range(DEPTH)])
    o["wo2_t"] = np.stack([_tile_major(np.asarray(ins["L0_ffn_Wo"])[l]) for l in range(DEPTH)])

    w1 = np.asarray(ins["L0_ffn_W1"])
    b1 = np.asarray(ins["L0_ffn_b1"])
    w1p = np.zeros((DEPTH, D, 44 * 128), np.float32)
    b1a = np.zeros((DEPTH, 128, NKT2), np.float32)
    b1g = np.zeros((DEPTH, 128, NKT2), np.float32)
    for i in range(NKT2):
        c0 = 128 * i
        n = min(128, DI - c0)
        w1p[:, :, 256 * i:256 * i + n] = w1[:, :, c0:c0 + n]
        w1p[:, :, 256 * i + 128:256 * i + 128 + n] = w1[:, :, DI + c0:DI + c0 + n]
        b1a[:, :n, i] = b1[:, c0:c0 + n]
        b1g[:, :n, i] = b1[:, DI + c0:DI + c0 + n]
    w1t = np.stack([_tile_major(w1p[l]) for l in range(DEPTH)])
    o["w1_t"] = np.ascontiguousarray(
        w1t.reshape(DEPTH, NKT2, 2, 128, 8, 128).transpose(0, 1, 3, 2, 4, 5))
    o["b1a_c"] = b1a
    o["b1g_c"] = b1g
    w2 = np.asarray(ins["L0_ffn_W2"])
    w2p = np.zeros((DEPTH, NKT2 * 128, D), np.float32)
    w2p[:, :DI, :] = w2
    o["w2_t"] = np.stack([_tile_major(w2p[l]) for l in range(DEPTH)])
    o["b2_c"] = np.ascontiguousarray(np.asarray(ins["L0_ffn_b2"]).reshape(DEPTH, 8, 128).transpose(0, 2, 1))
    o["g_c"] = np.ascontiguousarray(np.asarray(ins["L0_ffn_ln_g"]).reshape(DEPTH, 8, 128).transpose(0, 2, 1))
    o["bln_c"] = np.ascontiguousarray(np.asarray(ins["L0_ffn_ln_b"]).reshape(DEPTH, 8, 128).transpose(0, 2, 1))
    qs = np.asarray(ins["L0_ffn_q_scale"])
    ks = np.asarray(ins["L0_ffn_k_scale"])
    o["qsk_c"] = np.stack([
        np.stack([np.tile(qs[l, 0:32], 4), np.tile(qs[l, 32:64], 4),
                  np.tile(ks[l, 0:32], 4), np.tile(ks[l, 32:64], 4)], axis=1)
        for l in range(DEPTH)]).astype(np.float32)
    return o


class K:
    def __init__(self, nc, es):
        self.nc = nc
        self.ops = {e: [] for e in ENGS}
        self.sems = {}
        self.cnt = {}
        self.lastw = {e: {} for e in ENGS}
        self.bufev = {}
        self.es = es
        self.trace = {e: [] for e in ENGS}
        for s in ("pe", "dv", "ac"):
            self.sems[s] = es.enter_context(nc.semaphore(s))
            self.cnt[s] = 0

    def emit(self, eng, fn):
        self.ops[eng].append(fn)

    def wait(self, eng, sem, val):
        if val is None or val <= 0 or self.lastw[eng].get(sem, 0) >= val:
            return
        self.lastw[eng][sem] = val
        s = self.sems[sem]
        self.trace[eng].append(("w", sem, val))
        self.emit(eng, lambda e, s=s, v=val: e.wait_ge(s, v))

    def op(self, eng, fn, sem=None, dn=1):
        if sem is None:
            self.emit(eng, fn)
            return None
        if sem in ("dv", "ac"):
            self.wait(eng, sem, self.cnt[sem])
        self.cnt[sem] += dn
        s = self.sems[sem]
        self.trace[eng].append(("i", sem, dn))
        self.emit(eng, lambda e, fn=fn, s=s, d=dn: fn(e).then_inc(s, d))
        return self.cnt[sem]

    def check_deadlock(self):
        pos = {e: 0 for e in ENGS}
        cur = {s: 0 for s in self.cnt}
        tr = self.trace
        progress = True
        while progress:
            progress = False
            for e in ENGS:
                while pos[e] < len(tr[e]):
                    kind, sem, val = tr[e][pos[e]]
                    if kind == "i":
                        cur[sem] += val
                        pos[e] += 1
                        progress = True
                    else:
                        if cur[sem] >= val:
                            pos[e] += 1
                            progress = True
                        else:
                            break
        stuck = {e: (pos[e], len(tr[e])) for e in ENGS if pos[e] < len(tr[e])}
        if stuck:
            for e, (p, n) in stuck.items():
                print("DEADLOCK", e, "at", p, "/", n, "on", tr[e][p],
                      "cur=", {s: cur[s] for s in cur})
            return False
        return True

    def chan(self, name):
        if name not in self.sems:
            self.sems[name] = self.es.enter_context(self.nc.semaphore(name))
            self.cnt[name] = 0
        return name

    def dma(self, eng, out, in_, sem, war=None):
        self.chan(sem)
        if war is not None:
            for (s2, v2) in self.bufev.get(war, ()):
                self.wait(eng, s2, v2)
        return self.op(eng, lambda e, o=out, i=in_: e.dma_start(o, i), sem, 16)

    def use(self, key, sem, val):
        self.bufev[key] = ((sem, val),)

    def barrier(self):
        snap = dict(self.cnt)
        for eng in ENGS:
            for s, v in snap.items():
                self.wait(eng, s, v)

    def flush(self):
        nonempty = [e for e in ENGS if self.ops[e]]
        if not nonempty:
            return
        with self.nc.Block() as block:
            for eng in nonempty:
                lst = self.ops[eng]

                def body(e, lst=lst):
                    for f in lst:
                        f(e)
                getattr(block, eng)(body)
        self.ops = {e: [] for e in ENGS}


class FMB:
    def __init__(self, t, w):
        self.t = t
        self.w = w

    def sl(self, p, o, n, r0=0, r1=128):
        return self.t[r0:r1, p * self.w + o: p * self.w + o + n]


def build_nc(taps=(), stop=None):
    nc = bass.Bass("TRN2", target_bir_lowering=False)
    top = ExitStack()
    k = K(nc, top)

    dr = {}
    dr["xT"] = nc.dram_tensor("xT", [D, T], F32, kind="ExternalInput")
    dr["y_out"] = nc.dram_tensor("y_out", [D, T], F32, kind="ExternalOutput")
    dr["xo_sp"] = nc.dram_tensor("xo_sp", [D, T], F32, kind="Internal")
    dr["y0_sp"] = nc.dram_tensor("y0_sp", [D, T], F32, kind="Internal")
    shapes = dict(
        wq_t=[8, 128, 8, 128], wdkv_t=[2, 128, 8, 128], wuk_t=[8, 128, 2, 128],
        wuv_s=[2, 128, D], wo_t=[8, 128, 8, 128], gam_c=[128, 2], bet_c=[128, 2],
        wqkv_qk_t=[DEPTH, 16, 128, 8, 128], wqkv_v_s=[DEPTH, 8, 128, D],
        wo2_t=[DEPTH, 8, 128, 8, 128], w1_t=[DEPTH, NKT2, 128, 2, 8, 128],
        b1a_c=[DEPTH, 128, NKT2], b1g_c=[DEPTH, 128, NKT2],
        w2_t=[DEPTH, 8, 128, NKT2, 128], b2_c=[DEPTH, 128, 8],
        g_c=[DEPTH, 128, 8], bln_c=[DEPTH, 128, 8], qsk_c=[DEPTH, 128, 4],
        mcos=[128, T], msin=[128, T], tabc=[128, T], tabs=[128, T],
        tabci=[128, T], tabsi=[128, T], mtri=[128, 4 * 512], win=[128, 1024],
        win0=[128, 512], onesk=[128, 128], bo32=[128, 128],
    )
    for n, s in shapes.items():
        dr[n] = nc.dram_tensor(n, s, F32, kind="ExternalInput")
    tap_out = {}
    for name in taps:
        tap_out[name] = nc.dram_tensor("tap_" + name, [D, T], F32, kind="ExternalOutput")

    tapstage = [None]
    if tap_out:
        tapstage[0] = top.enter_context(nc.sbuf_tensor("tapstage", [128, T], F32))

    def tapT(name, tens, w):
        if name not in tap_out:
            return
        dt = tap_out[name]
        k.barrier()
        dvv = k.op("vector", lambda v, w=w: v.tensor_copy(tapstage[0][:, 0:w], tens[:, 0:w]), "dv")
        k.wait("sync", "dv", dvv)
        wsv = k.dma("sync", dt[0:128, 0:w], tapstage[0][:, 0:w], "tap")
        k.wait("vector", "tap", wsv)
        k.barrier()

    def tap(name, fmb, col0=0, w=T):
        if name not in tap_out:
            return
        dt = tap_out[name]
        k.barrier()
        for p in range(8):
            dvv = k.op("vector", lambda v, p=p, w=w, c0=col0:
                       v.tensor_copy(tapstage[0][:, 0:w], fmb.sl(p, c0, w)), "dv")
            k.wait("sync", "dv", dvv)
            wsv = k.dma("sync", dt[128 * p:128 * p + 128, 0:w], tapstage[0][:, 0:w], "tap")
            k.wait("vector", "tap", wsv)
        k.barrier()

    psA = top.enter_context(nc.psum_tensor("psA", [128, 512], F32))
    psB = top.enter_context(nc.psum_tensor("psB", [128, 512], F32))
    psS0 = top.enter_context(nc.psum_tensor("psS0", [128, 1024], F32))
    psS1 = top.enter_context(nc.psum_tensor("psS1", [128, 1024], F32))
    psV0 = top.enter_context(nc.psum_tensor("psV0", [128, 512], F32))
    psV1 = top.enter_context(nc.psum_tensor("psV1", [128, 512], F32))
    psG = [psA, psB]
    psS = [psS0, psS1]
    psV = [psV0, psV1]

    onesk = top.enter_context(nc.sbuf_tensor("onesk_b", [128, 128], BF16))
    bo32 = top.enter_context(nc.sbuf_tensor("bo32_b", [128, 128], BF16))
    tabs4 = top.enter_context(nc.sbuf_tensor("tabs4", [128, 4 * T], BF16))
    mask_b = top.enter_context(nc.sbuf_tensor("mask_b", [128, 2048], BF16))
    smalls = top.enter_context(nc.sbuf_tensor("smalls", [128, 80], F32))

    epsc = top.enter_context(nc.sbuf_tensor("epsc", [128, 1], F32))
    onesf = top.enter_context(nc.sbuf_tensor("onesf", [33, 64], F32))
    k.dma("gpsimd", onesk[:, :], dr["onesk"][:, :], "cst")
    k.dma("gpsimd", bo32[:, :], dr["bo32"][:, :], "cst")
    k.dma("sync", smalls[:, 0:2], dr["gam_c"][:, :], "sml")
    k.dma("sync", smalls[:, 2:4], dr["bet_c"][:, :], "sml")
    k.op("vector", lambda v: v.memset(epsc[:, :], 1e-5), "dv")
    k.op("vector", lambda v: v.memset(onesf[:, :], 1.0), "dv")

    gemm_ev = {}

    def gemm_wait_bank(b):
        k.wait("tensor", "dv", gemm_ev.get(("dv", b), 0))
        k.wait("tensor", "ac", gemm_ev.get(("ac", b), 0))

    def mark(kind, b, v):
        gemm_ev[(kind, b)] = max(gemm_ev.get((kind, b), 0), v)

    def ln_from_sums(mu_ap, rstd_ap, st1, st2, n, wdt):
        k.op("vector", lambda v: v.tensor_scalar(mu_ap, psG[0][:, 0:wdt],
                                                 1.0 / n, None, ALU.mult), "dv")
        dva = k.op("vector", lambda v: v.tensor_scalar(st1[:, 0:wdt], psG[1][:, 0:wdt],
                                                       1.0 / n, None, ALU.mult), "dv")
        mark("dv", 0, dva)
        mark("dv", 1, dva)
        k.op("vector", lambda v: v.tensor_mul(st2[:, 0:wdt], mu_ap, mu_ap), "dv")
        dvv = k.op("vector", lambda v: v.tensor_sub(st1[:, 0:wdt], st1[:, 0:wdt],
                                                    st2[:, 0:wdt]), "dv")
        k.wait("scalar", "dv", dvv)
        k.op("scalar", lambda s: s.activation(st2[:, 0:wdt], st1[:, 0:wdt], AF.Ln,
                                              bias=epsc[:, 0:1]), "ac")
        return k.op("scalar", lambda s: s.activation(rstd_ap, st2[:, 0:wdt],
                                                     AF.Exp, scale=-0.5), "ac")

    def pn_copies(denf, doff, jw, voff):
        k.wait("scalar", "pe", k.cnt["pe"])
        k.op("scalar", lambda s: s.copy(denf[0:1, doff:doff + jw],
                                        psV[0][64:65, voff:voff + jw]), "ac")
        return k.op("scalar", lambda s: s.copy(denf[32:33, doff:doff + jw],
                                               psV[1][64:65, voff:voff + jw]), "ac")

    def pn_finish(obc_ap0, obc_ap1, denf, rcpd, st1, jw, acv, voff, doff):
        k.wait("tensor", "ac", acv)
        gemm_wait_bank(1)
        k.op("tensor", lambda t: t.matmul(psG[1][0:64, 0:jw], onesf[0:1, 0:64],
                                          denf[0:1, doff:doff + jw],
                                          start=True, stop=True), "pe")
        pev = k.op("tensor", lambda t: t.matmul(psG[1][64:128, 0:jw], onesf[32:33, 0:64],
                                                denf[32:33, doff:doff + jw],
                                                start=True, stop=True,
                                                tile_position=(32, 64)), "pe")
        k.wait("scalar", "pe", pev)
        k.wait("scalar", "dv", gemm_ev.get(("pnlast",), 0))
        k.op("scalar", lambda s: s.activation(st1[:, 0:jw], psG[1][:, 0:jw], AF.Ln), "ac")
        acv = k.op("scalar", lambda s: s.activation(rcpd[:, 0:jw], st1[:, 0:jw],
                                                    AF.Exp, scale=-1.0), "ac")
        mark("ac", 1, acv)
        k.wait("vector", "ac", acv)
        k.op("vector", lambda v: v.tensor_mul(obc_ap0, psV[0][0:64, voff:voff + jw],
                                              rcpd[0:64, 0:jw]), "dv")
        dvv = k.op("vector", lambda v: v.tensor_mul(obc_ap1, psV[1][0:64, voff:voff + jw],
                                                    rcpd[64:128, 0:jw]), "dv")
        gemm_ev[("pnlast",)] = dvv
        return dvv

    def pair_norm_evac(obc_ap0, obc_ap1, denf, rcpd, st1, jw):
        acv = pn_copies(denf, 0, jw, 0)
        return pn_finish(obc_ap0, obc_ap1, denf, rcpd, st1, jw, acv, 0, 0)

    def emit_mla():
        with ExitStack() as ph:
            sb = lambda n_, s_, d_: ph.enter_context(nc.sbuf_tensor(n_, s_, d_))
            qp = FMB(sb("qp", [128, 8 * T], BF16), T)
            kp = FMB(sb("kp", [128, 8 * T], BF16), T)
            vsl = sb("vsl", [128, 16 * VSW], BF16)
            latnb = FMB(sb("latnb", [128, 2 * T], BF16), T)
            wqb = [sb("wqb0", [128, 8 * 128], BF16), sb("wqb1", [128, 8 * 128], BF16)]

            k.dma("gpsimd", tabs4[:, 0:T], dr["mcos"][:, :], "cst")
            k.dma("gpsimd", tabs4[:, T:2 * T], dr["msin"][:, :], "cst")
            k.dma("gpsimd", mask_b[:, 0:4 * 512], dr["mtri"][:, :], "cst")
            vset = k.op("vector", lambda v: v.memset(vsl[:, :], 1.0), "dv")
            k.barrier()

            with ExitStack() as p2:
                s2 = lambda n_, s_, d_: p2.enter_context(nc.sbuf_tensor(n_, s_, d_))
                xbc = FMB(s2("xbc", [128, 8 * CH], BF16), CH)
                sqc = FMB(s2("sqc", [128, 2 * CH], BF16), CH)
                xfc = FMB(s2("xfc", [128, 8 * CH], F32), CH)
                xnc = FMB(s2("xnc", [128, 8 * CH], BF16), CH)
                mu = s2("mu", [128, CH], F32)
                rstd = s2("rstd", [128, CH], F32)
                st1 = s2("st1", [128, CH], F32)
                st2 = s2("st2", [128, CH], F32)
                tA = s2("tA", [128, CH], F32)
                tB = s2("tB", [128, CH], F32)
                qrp = FMB(s2("qrp", [128, 2 * CH], BF16), CH)
                latfc = FMB(s2("latfc", [128, 2 * CH], F32), CH)

                def rope_pair(pr, dstf, dstoff, c, co, so):
                    for (dp, sA, sB, sub) in ((pr, 0, 1, True), (pr + 4, 1, 0, False)):
                        k.op("vector", lambda v, sA=sA, co=co, c=c:
                             v.tensor_mul(tA[:, :], qrp.sl(sA, 0, CH),
                                          tabs4[:, co + c * CH:co + c * CH + CH]), "dv")
                        k.op("vector", lambda v, sB=sB, so=so, c=c:
                             v.tensor_mul(tB[:, :], qrp.sl(sB, 0, CH),
                                          tabs4[:, so + c * CH:so + c * CH + CH]), "dv")
                        if sub:
                            dvv = k.op("vector", lambda v, dp=dp:
                                       v.tensor_sub(dstf.sl(dp, dstoff, CH), tA[:, :], tB[:, :]), "dv")
                        else:
                            dvv = k.op("vector", lambda v, dp=dp:
                                       v.tensor_add(dstf.sl(dp, dstoff, CH), tA[:, :], tB[:, :]), "dv")
                    return dvv

                def stats_pass(src_fn, nf, do_square, sq_fn=None):
                    lastA = {}
                    sqv = {}
                    for p in range(nf):
                        if do_square:
                            k.wait("scalar", "pe", lastA.get(p - 2, 0))
                            sqv[p] = k.op("scalar", lambda s, p=p:
                                          s.activation(sqc.sl(p % 2, 0, CH), src_fn(p),
                                                       AF.Square), "ac")
                        k.op("tensor", lambda t, p=p, s=(p == 0), e=(p == nf - 1):
                             t.matmul(psG[0][:, :], onesk[:, :], src_fn(p),
                                      start=s, stop=e), "pe")
                        if do_square:
                            k.wait("tensor", "ac", sqv[p])
                            src2 = lambda p=p: sqc.sl(p % 2, 0, CH)
                        else:
                            src2 = lambda p=p: sq_fn(p)
                        pev = k.op("tensor", lambda t, p=p, s=(p == 0), e=(p == nf - 1), s2=src2:
                                   t.matmul(psG[1][:, :], onesk[:, :], s2(),
                                            start=s, stop=e), "pe")
                        lastA[p] = pev
                    return pev

                for c in range(NCH):
                    xbv = None
                    xfv = None
                    for p in range(8):
                        xbv = k.dma("gpsimd", xbc.sl(p, 0, CH),
                                    dr["xT"][128 * p:128 * p + 128, c * CH:c * CH + CH], "xbc")
                        xfv = k.dma("sync", xfc.sl(p, 0, CH),
                                    dr["xT"][128 * p:128 * p + 128, c * CH:c * CH + CH], "xfc")
                    k.wait("scalar", "xbc", xbv)
                    k.wait("tensor", "xbc", xbv)
                    gemm_wait_bank(0)
                    gemm_wait_bank(1)
                    pev = stats_pass(lambda p: xbc.sl(p, 0, CH), 8, True)
                    k.wait("vector", "pe", pev)
                    acv = ln_from_sums(mu[:, :], rstd[:, :], st1, st2, D, CH)
                    k.wait("vector", "xfc", xfv)
                    k.wait("vector", "ac", acv)
                    if c == 3:
                        tapT("mu_dbg", mu, CH)
                        tapT("rstd_dbg", rstd, CH)
                    xnv = None
                    for p in range(8):
                        k.op("vector", lambda v, p=p:
                             v.tensor_sub(st2[:, :], xfc.sl(p, 0, CH), mu[:, :]), "dv")
                        xnv = k.op("vector", lambda v, p=p:
                                   v.tensor_mul(xnc.sl(p, 0, CH), st2[:, :],
                                                rstd[:, :]), "dv")
                    k.wait("tensor", "dv", xnv)

                    import os as _o
                    for pr in range(0 if _o.environ.get("SKIP_Q") else 4):
                        for half, p in ((0, pr), (1, pr + 4)):
                            wv = k.dma("gpsimd", wqb[half][:, :],
                                       dr["wq_t"][p].rearrange("p k j -> p (k j)"),
                                       "wqb%d" % half, war="wqb%d" % half)
                            k.wait("tensor", "wqb%d" % half, wv)
                            gemm_wait_bank(half)
                            pev = None
                            for kk in range(8):
                                pev = k.op("tensor", lambda t, h2=half, kk=kk,
                                           s=(kk == 0), e=(kk == 7):
                                           t.matmul(psG[h2][:, :], wqb[h2][:, 128 * kk:128 * kk + 128],
                                                    xnc.sl(kk, 0, CH), start=s, stop=e), "pe")
                            k.use("wqb%d" % half, "pe", pev)
                            k.wait("vector", "pe", pev)
                            dvv = k.op("vector", lambda v, h2=half:
                                       v.tensor_copy(qrp.sl(h2, 0, CH), psG[h2][:, :]), "dv")
                            mark("dv", half, dvv)
                        rope_pair(pr, qp, c * CH, c, 0, T)

                    LP = int(_o.environ.get("LATPART", "5"))
                    for lp in range(0 if _o.environ.get("SKIP_LAT") else 2):
                        wv = k.dma("gpsimd", wqb[lp][:, :],
                                   dr["wdkv_t"][lp].rearrange("p k j -> p (k j)"),
                                   "wqb%d" % lp, war="wqb%d" % lp)
                        k.wait("tensor", "wqb%d" % lp, wv)
                        gemm_wait_bank(lp)
                        pev = None
                        for kk in range(8):
                            pev = k.op("tensor", lambda t, lp=lp, kk=kk,
                                       s=(kk == 0), e=(kk == 7):
                                       t.matmul(psG[lp][:, :], wqb[lp][:, 128 * kk:128 * kk + 128],
                                                xnc.sl(kk, 0, CH), start=s, stop=e), "pe")
                        k.use("wqb%d" % lp, "pe", pev)
                        if LP >= 2:
                            k.wait("vector", "pe", pev)
                            dvv = k.op("vector", lambda v, lp=lp:
                                       v.tensor_copy(latfc.sl(lp, 0, CH), psG[lp][:, :]), "dv")
                            mark("dv", lp, dvv)
                        if LP >= 3:
                            k.wait("scalar", "dv", dvv)
                            acv = k.op("scalar", lambda s, lp=lp:
                                       s.copy(qrp.sl(lp, 0, CH), latfc.sl(lp, 0, CH)), "ac")
                    if not _o.environ.get("SKIP_LAT") and LP >= 4:
                        k.wait("tensor", "ac", k.cnt["ac"])
                        gemm_wait_bank(0)
                        gemm_wait_bank(1)
                        pev = stats_pass(lambda p: qrp.sl(p, 0, CH), 2, True)
                        k.wait("vector", "pe", pev)
                        acv = ln_from_sums(mu[:, :], rstd[:, :], st1, st2, LAT, CH)
                        if LP >= 5:
                            k.wait("vector", "ac", acv)
                            lnv = None
                            for lp in range(2):
                                k.op("vector", lambda v, lp=lp:
                                     v.tensor_sub(st2[:, :], latfc.sl(lp, 0, CH), mu[:, :]), "dv")
                                k.op("vector", lambda v, lp=lp:
                                     v.tensor_mul(latfc.sl(lp, 0, CH), st2[:, :], rstd[:, :]), "dv")
                                lnv = k.op("vector", lambda v, lp=lp, c=c:
                                           v.tensor_scalar(latnb.sl(lp, c * CH, CH), latfc.sl(lp, 0, CH),
                                                           smalls[:, lp:lp + 1], smalls[:, 2 + lp:3 + lp],
                                                           ALU.mult, ALU.add), "dv")
                            k.wait("tensor", "dv", lnv)

                    for pr in range(0 if (_o.environ.get("SKIP_K") or _o.environ.get("SKIP_LAT")) else 4):
                        for half, p in ((0, pr), (1, pr + 4)):
                            wv = k.dma("gpsimd", wqb[half][:, 0:2 * 128],
                                       dr["wuk_t"][p].rearrange("p k j -> p (k j)"),
                                       "wqb%d" % half, war="wqb%d" % half)
                            k.wait("tensor", "wqb%d" % half, wv)
                            gemm_wait_bank(half)
                            pev = None
                            for kk in range(2):
                                pev = k.op("tensor", lambda t, h2=half, kk=kk, c=c,
                                           s=(kk == 0), e=(kk == 1):
                                           t.matmul(psG[h2][:, :], wqb[h2][:, 128 * kk:128 * kk + 128],
                                                    latnb.sl(kk, c * CH, CH), start=s, stop=e), "pe")
                            k.use("wqb%d" % half, "pe", pev)
                            k.wait("vector", "pe", pev)
                            dvv = k.op("vector", lambda v, h2=half:
                                       v.tensor_copy(qrp.sl(h2, 0, CH), psG[h2][:, :]), "dv")
                            mark("dv", half, dvv)
                        rope_pair(pr, kp, c * CH, c, 0, T)

                    if _o.environ.get("SKIP_V") or _o.environ.get("SKIP_LAT"):
                        k.barrier()
                        continue
                    for lk in range(2):
                        k.dma("gpsimd", wqb[lk][:, 0:512], dr["wuv_s"][lk][:, 0:512],
                              "wqb%d" % lk, war="wqb%d" % lk)
                        wvv = k.dma("gpsimd", wqb[lk][:, 512:1024], dr["wuv_s"][lk][:, 512:1024],
                                    "wqb%d" % lk)
                        k.wait("tensor", "wqb%d" % lk, wvv)
                    k.wait("tensor", "dv", k.cnt["dv"])
                    for ti in range(4):
                        tt = 4 * c + ti
                        for jh in range(2):
                            b = (ti * 2 + jh) % 2
                            gemm_wait_bank(b)
                            pev = None
                            for lk in range(2):
                                pev = k.op("tensor", lambda t, lk=lk, ti=ti, jh=jh, b=b, c=c,
                                           s=(lk == 0), e=(lk == 1):
                                           t.matmul(psG[b][:, :],
                                                    latnb.sl(lk, c * CH + ti * 128, 128),
                                                    wqb[lk][:, jh * 512:jh * 512 + 512],
                                                    start=s, stop=e), "pe")
                            k.wait("vector", "pe", pev)
                            k.wait("vector", "dv", vset)
                            dst = bass.AP(vsl, tt * VSW + jh * 8 * 66,
                                          [[16 * VSW, 128], [66, 8], [1, 64]])
                            src = bass.AP(psG[b], 0, [[512, 128], [64, 8], [1, 64]])
                            dvv = k.op("vector", lambda v, d=dst, s3=src: v.tensor_copy(d, s3), "dv")
                            mark("dv", b, dvv)
                    k.use("wqb0", "pe", k.cnt["pe"])
                    k.use("wqb1", "pe", k.cnt["pe"])
                    k.barrier()
                tap("xn_mla", xnc, 0, CH)
                k.flush()
            k.barrier()
            tap("qp", qp)
            tap("kp", kp)
            tapT("vsl_dbg", vsl, 2048)
            tapT("latnb_dbg", latnb.t, 2048)

            import os as _os
            if (_os.environ.get("KSTOP") or "") == "mlaproj":
                k.barrier()
                k.flush()
                return
            with ExitStack() as p3:
                s3 = lambda n_, s_, d_: p3.enter_context(nc.sbuf_tensor(n_, s_, d_))
                ptb = [s3("pt%d" % i, [128, 512], BF16) for i in range(4)]
                stg = [s3("stg%d" % i, [128, 512], F32) for i in range(2)]
                obc = FMB(s3("obc", [128, 8 * 512], BF16), 512)
                rcpd = s3("rcpd", [128, 512], F32)
                denf = s3("denf", [33, 512], F32)
                st1 = s3("st1b", [128, 512], F32)
                st2 = s3("st2b", [128, CH], F32)
                xfc = FMB(s3("xfc2", [128, 8 * CH], F32), CH)
                xoc = FMB(s3("xoc", [128, 8 * CH], F32), CH)
                xob = FMB(s3("xob", [128, 4 * CH], BF16), CH)
                mu = s3("mu2", [128, CH], F32)
                rstd = s3("rstd2", [128, CH], F32)
                wqb2 = [s3("wqc0", [128, 8 * 128], BF16), s3("wqc1", [128, 8 * 128], BF16)]

                AP_ = int(os.environ.get("ATTPART", "5"))
                for qc in range(NCH):
                    nkt = 4 * qc + 4
                    for gq in range(4):
                        for sp in range(2):
                            h0 = 4 * gq + 2 * sp
                            heads = ((0, h0, 64 * sp), (1, h0 + 1, 64 * sp + 32))
                            expv = {0: 0, 1: 0}
                            for kt in range(nkt):
                                par = kt % 2
                                for hi, h, a in heads:
                                    sl = psS[hi][:, par * 512:par * 512 + 512]
                                    k.wait("tensor", "ac", expv[hi])
                                    k.op("tensor", lambda t, gq=gq, a=a, kt=kt, qc=qc, sl=sl:
                                         t.matmul(sl, kp.sl(gq, kt * 128, 128, a, a + 32),
                                                  qp.sl(gq, qc * CH, CH, a, a + 32),
                                                  start=True, stop=False, tile_position=(a, 0)), "pe")
                                    pev = k.op("tensor", lambda t, gq=gq, a=a, kt=kt, qc=qc, sl=sl:
                                               t.matmul(sl, kp.sl(gq + 4, kt * 128, 128, a, a + 32),
                                                        qp.sl(gq + 4, qc * CH, CH, a, a + 32),
                                                        start=False, stop=True,
                                                        tile_position=(a, 0)), "pe")
                                    r = kt - 4 * qc
                                    pt = ptb[2 * hi + par]
                                    if r >= 0:
                                        k.wait("vector", "pe", pev)
                                        k.wait("vector", "ac", gemm_ev.get(("stg", hi), 0))
                                        mdv = k.op("vector", lambda v, sl=sl, r=r, hi=hi:
                                                   v.tensor_add(stg[hi][:, :], sl,
                                                                mask_b[:, r * 512:r * 512 + 512]), "dv")
                                        k.wait("scalar", "dv", mdv)
                                        acv = k.op("scalar", lambda s, hi=hi, pt=pt:
                                                   s.activation(pt[:, :], stg[hi][:, :],
                                                                AF.Exp, scale=0.125), "ac")
                                        gemm_ev[("stg", hi)] = acv
                                        expv[hi] = acv
                                        if AP_ < 2:
                                            continue
                                        k.wait("tensor", "ac", acv)
                                        if kt == 0:
                                            k.wait("tensor", "dv", gemm_ev.get(("pnorm", 0), 0))
                                        vap2 = bass.AP(vsl, kt * VSW + 66 * h,
                                                       [[16 * VSW, 128], [1, 65]])
                                        k.op("tensor", lambda t, va=vap2, pt=pt, hi=hi,
                                             s=(kt == 0), e=(kt == nkt - 1):
                                             t.matmul(psV[hi][0:65, :], va, pt[:, :],
                                                      start=s, stop=e), "pe")
                                        continue
                                    k.wait("scalar", "pe", pev)
                                    acv = k.op("scalar", lambda s, sl=sl, pt=pt:
                                               s.activation(pt[:, :], sl, AF.Exp, scale=0.125), "ac")
                                    expv[hi] = acv
                                    if AP_ < 2:
                                        continue
                                    k.wait("tensor", "ac", acv)
                                    if kt == 0:
                                        k.wait("tensor", "dv", gemm_ev.get(("pnorm", 0), 0))
                                    vap2 = bass.AP(vsl, kt * VSW + 66 * h,
                                                   [[16 * VSW, 128], [1, 65]])
                                    k.op("tensor", lambda t, va=vap2, pt=pt, hi=hi,
                                         s=(kt == 0), e=(kt == nkt - 1):
                                         t.matmul(psV[hi][0:65, :], va, pt[:, :],
                                                  start=s, stop=e), "pe")
                            pev = k.cnt["pe"]
                            k.wait("vector", "pe", pev)
                            if AP_ < 3:
                                continue
                            p = h0 // 2
                            dvv = pair_norm_evac(obc.sl(p, 0, 512, 0, 64),
                                                 obc.sl(p, 0, 512, 64, 128),
                                                 denf, rcpd, st1, 512)
                            k.wait("tensor", "dv", dvv)
                    obv = k.cnt["dv"]
                    if qc == NCH - 1:
                        tap("obc_mla", obc, 0, 512)
                    if AP_ < 4:
                        continue

                    xfv = None
                    for p in range(8):
                        xfv = k.dma("sync", xfc.sl(p, 0, CH),
                                    dr["xT"][128 * p:128 * p + 128, qc * CH:qc * CH + CH], "xfc")
                    for p in range(8):
                        wv = k.dma("gpsimd", wqb2[p % 2][:, :],
                                   dr["wo_t"][p].rearrange("p k j -> p (k j)"),
                                   "wqc%d" % (p % 2), war="wqc%d" % (p % 2))
                        k.wait("tensor", "wqc%d" % (p % 2), wv)
                        k.wait("tensor", "dv", obv)
                        b = p % 2
                        gemm_wait_bank(b)
                        pev = None
                        for kk in range(8):
                            pev = k.op("tensor", lambda t, b=b, kk=kk, s=(kk == 0), e=(kk == 7):
                                       t.matmul(psG[b][:, :], wqb2[b][:, 128 * kk:128 * kk + 128],
                                                obc.sl(kk, 0, 512), start=s, stop=e), "pe")
                        k.use("wqc%d" % b, "pe", pev)
                        k.wait("vector", "pe", pev)
                        k.wait("vector", "xfc", xfv)
                        dvv = k.op("vector", lambda v, p=p, b=b:
                                   v.tensor_add(xoc.sl(p, 0, CH), psG[b][:, :],
                                                xfc.sl(p, 0, CH)), "dv")
                        mark("dv", b, dvv)
                        k.wait("sync", "dv", dvv)
                        k.dma("sync", dr["xo_sp"][128 * p:128 * p + 128, qc * CH:qc * CH + CH],
                              xoc.sl(p, 0, CH), "spill")
                    xonv = k.cnt["dv"]
                    spillv = k.cnt["spill"]
                    gemm_wait_bank(0)
                    gemm_wait_bank(1)
                    lastA = {}
                    for p in range(8):
                        k.wait("scalar", "dv", xonv)
                        k.wait("scalar", "pe", lastA.get(p - 2, 0))
                        cpv = k.op("scalar", lambda s, p=p:
                                   s.activation(xob.sl(p % 2, 0, CH), xoc.sl(p, 0, CH),
                                                AF.Identity), "ac")
                        k.wait("tensor", "ac", cpv)
                        k.op("tensor", lambda t, p=p, s=(p == 0), e=(p == 7):
                             t.matmul(psG[0][:, :], onesk[:, :], xob.sl(p % 2, 0, CH),
                                      start=s, stop=e), "pe")
                        sqv = k.op("scalar", lambda s, p=p:
                                   s.activation(xob.sl(2 + p % 2, 0, CH), xob.sl(p % 2, 0, CH),
                                                AF.Square), "ac")
                        k.wait("tensor", "ac", sqv)
                        pev = k.op("tensor", lambda t, p=p, s=(p == 0), e=(p == 7):
                                   t.matmul(psG[1][:, :], onesk[:, :], xob.sl(2 + p % 2, 0, CH),
                                            start=s, stop=e), "pe")
                        lastA[p] = pev
                    k.wait("vector", "pe", pev)
                    acv = ln_from_sums(mu[:, :], rstd[:, :], st1, st2, D, CH)
                    k.wait("vector", "ac", acv)
                    k.wait("vector", "spill", spillv)
                    dvv = None
                    for p in range(8):
                        k.op("vector", lambda v, p=p:
                             v.tensor_sub(st2[:, :], xoc.sl(p, 0, CH), mu[:, :]), "dv")
                        dvv = k.op("vector", lambda v, p=p:
                                   v.tensor_mul(xoc.sl(p, 0, CH), st2[:, :],
                                                rstd[:, :]), "dv")
                        k.wait("sync", "dv", dvv)
                        k.dma("sync", dr["y0_sp"][128 * p:128 * p + 128, qc * CH:qc * CH + CH],
                              xoc.sl(p, 0, CH), "spill")
                    k.barrier()
                k.barrier()
                k.flush()
            tap("qp2", qp)
            tap("kp2", kp)
            tapT("vsl2_dbg", vsl, 2048)
            k.flush()

    def emit_layer(l, yfm):
        k.dma("sync", smalls[:, 4:8], dr["qsk_c"][l], "sml")
        k.dma("sync", smalls[:, 8:16], dr["g_c"][l], "sml")
        k.dma("sync", smalls[:, 16:24], dr["bln_c"][l], "sml")
        k.dma("sync", smalls[:, 24:32], dr["b2_c"][l], "sml")
        k.dma("sync", smalls[:, 32:54], dr["b1a_c"][l], "sml")
        k.dma("sync", smalls[:, 54:76], dr["b1g_c"][l], "sml")
        if l == 0:
            k.dma("gpsimd", tabs4[:, 0:T], dr["tabc"][:, :], "cst")
            k.dma("gpsimd", tabs4[:, T:2 * T], dr["tabs"][:, :], "cst")
            k.dma("gpsimd", tabs4[:, 2 * T:3 * T], dr["tabci"][:, :], "cst")
            k.dma("gpsimd", tabs4[:, 3 * T:4 * T], dr["tabsi"][:, :], "cst")
            k.dma("gpsimd", mask_b[:, 0:1024], dr["win"][:, :], "cst")
            k.dma("gpsimd", mask_b[:, 1024:1536], dr["win0"][:, :], "cst")
        k.barrier()

        with ExitStack() as ph:
            sb = lambda n_, s_, d_: ph.enter_context(nc.sbuf_tensor("%s_L%d" % (n_, l), s_, d_))
            kpb = FMB(sb("kpb", [128, 8 * 768], BF16), 768)
            vbl = sb("vbl", [128, 6 * VSW], BF16)
            ybc = FMB(sb("ybc", [128, 8 * 513], BF16), 513)
            qpc = FMB(sb("qpc", [128, 8 * CH], BF16), CH)
            qrp = FMB(sb("qrp2", [128, 2 * CH], BF16), CH)
            qsq = FMB(sb("qsq", [128, 2 * CH], BF16), CH)
            rsqf = sb("rsqf", [128, CH], F32)
            tA = sb("tA2", [128, CH], F32)
            tB = sb("tB2", [128, CH], F32)
            tC = sb("tC2", [128, CH], F32)
            st1 = sb("st1c", [128, CH], F32)
            wqb = [sb("wqd0", [128, 8 * 128], BF16), sb("wqd1", [128, 8 * 128], BF16)]
            wvb = FMB(sb("wvb", [128, 8 * 1024], BF16), 1024)
            ptb2 = [[sb("ptA0", [128, 1024], BF16), sb("ptA1", [128, 1024], BF16)],
                    [sb("ptB0", [128, 1024], BF16), sb("ptB1", [128, 1024], BF16)]]
            stg2 = [sb("stgA", [128, 1024], F32), sb("stgB", [128, 1024], F32)]
            obc = FMB(sb("obc2", [128, 8 * 512], BF16), 512)
            rcpd = sb("rcpd2", [128, 256], F32)
            denf = sb("denf2", [33, 512], F32)

            vset = k.op("vector", lambda v: v.memset(vbl[:, :], 1.0), "dv")

            for c in range(NCH):
                if c > 0:
                    for p in range(8):
                        k.op("vector", lambda v, p=p:
                             v.tensor_copy(kpb.sl(p, 0, 256), kpb.sl(p, 512, 256)), "dv")
                    for s4 in range(2):
                        k.op("vector", lambda v, s4=s4:
                             v.tensor_copy(vbl[:, s4 * VSW:(s4 + 1) * VSW],
                                           vbl[:, (4 + s4) * VSW:(5 + s4) * VSW]), "dv")
                for p in range(8):
                    if c == 0:
                        k.op("vector", lambda v, p=p: v.memset(ybc.sl(p, 0, 1), 0.0), "dv")
                    else:
                        k.op("vector", lambda v, p=p:
                             v.tensor_copy(ybc.sl(p, 0, 1), ybc.sl(p, 512, 1)), "dv")
                ycv = None
                for p in range(8):
                    ycv = k.op("vector", lambda v, p=p, c=c:
                               v.tensor_copy(ybc.sl(p, 1, 512), yfm.sl(p, 1 + c * CH, 512)), "dv")
                k.wait("tensor", "dv", ycv)

                def rhs_sh(kk, off, n):
                    return ybc.sl(kk, off + (0 if kk >= 4 else 1), n)

                qrp_rd = {0: 0, 1: 0}
                rope_dv = 0
                for (ofb, co, so, c1, c2) in ((0, 0, T, 4, 5), (8, 2 * T, 3 * T, 6, 7)):
                    for pr in range(4):
                        for half, p in ((0, pr), (1, pr + 4)):
                            wv = k.dma("gpsimd", wqb[half][:, :],
                                       dr["wqkv_qk_t"][l, ofb + p].rearrange("p k j -> p (k j)"),
                                       "wqd%d" % half, war="wqd%d" % half)
                            k.wait("tensor", "wqd%d" % half, wv)
                            gemm_wait_bank(half)
                            pev = None
                            for kk in range(8):
                                pev = k.op("tensor", lambda t, h2=half, kk=kk,
                                           s=(kk == 0), e=(kk == 7):
                                           t.matmul(psG[h2][:, :],
                                                    wqb[h2][:, 128 * kk:128 * kk + 128],
                                                    rhs_sh(kk, 0, CH), start=s, stop=e), "pe")
                            k.use("wqd%d" % half, "pe", pev)
                            k.wait("scalar", "pe", pev)
                            k.wait("scalar", "dv", qrp_rd[half])
                            k.op("scalar", lambda s, h2=half:
                                 s.copy(qrp.sl(h2, 0, CH), psG[h2][:, :]), "ac")
                            acv = k.op("scalar", lambda s, h2=half:
                                       s.activation(qsq.sl(h2, 0, CH), qrp.sl(h2, 0, CH),
                                                    AF.Square), "ac")
                            mark("ac", half, acv)
                        k.wait("tensor", "ac", k.cnt["ac"])
                        gemm_wait_bank(0)
                        k.op("tensor", lambda t: t.matmul(psG[0][:, :], bo32[:, :],
                                                          qsq.sl(0, 0, CH),
                                                          start=True, stop=False), "pe")
                        pev = k.op("tensor", lambda t: t.matmul(psG[0][:, :], bo32[:, :],
                                                                qsq.sl(1, 0, CH),
                                                                start=False, stop=True), "pe")
                        k.wait("scalar", "pe", pev)
                        k.wait("scalar", "dv", rope_dv)
                        k.op("scalar", lambda s: s.activation(st1[:, :], psG[0][:, :], AF.Ln), "ac")
                        acv = k.op("scalar", lambda s: s.activation(tC[:, :], st1[:, :],
                                                                    AF.Exp, scale=-0.5), "ac")
                        mark("ac", 0, acv)
                        k.wait("vector", "ac", acv)
                        k.op("vector", lambda v: v.tensor_scalar_min(rsqf[:, :], tC[:, :],
                                                                     1e12), "dv")
                        qrp_rd[0] = k.op("vector", lambda v, c1=c1: v.scalar_tensor_tensor(
                            tA[:, :], qrp.sl(0, 0, CH), smalls[:, c1:c1 + 1], rsqf[:, :],
                            ALU.mult, ALU.mult), "dv")
                        dvv = k.op("vector", lambda v, c2=c2: v.scalar_tensor_tensor(
                            tB[:, :], qrp.sl(1, 0, CH), smalls[:, c2:c2 + 1], rsqf[:, :],
                            ALU.mult, ALU.mult), "dv")
                        qrp_rd[1] = dvv
                        for (dp, tX, tY, sub) in ((pr, tA, tB, True), (pr + 4, tB, tA, False)):
                            k.op("vector", lambda v, tX=tX, co=co, c=c:
                                 v.tensor_mul(tC[:, :], tX[:, :],
                                              tabs4[:, co + c * CH:co + c * CH + CH]), "dv")
                            k.op("vector", lambda v, tY=tY, so=so, c=c:
                                 v.tensor_mul(st1[:, :], tY[:, :],
                                              tabs4[:, so + c * CH:so + c * CH + CH]), "dv")
                            if ofb == 0:
                                outap = qpc.sl(dp, 0, CH)
                            else:
                                outap = kpb.sl(dp, 256, CH)
                            if sub:
                                k.op("vector", lambda v, o=outap:
                                     v.tensor_sub(o, tC[:, :], st1[:, :]), "dv")
                            else:
                                k.op("vector", lambda v, o=outap:
                                     v.tensor_add(o, tC[:, :], st1[:, :]), "dv")
                        rope_dv = k.cnt["dv"]

                wvv = None
                for kk in range(8):
                    wvv = k.dma("gpsimd", wvb.sl(kk, 0, 1024),
                                dr["wqkv_v_s"][l, kk][:, :], "wvb",
                                war="wvb" if kk == 0 else None)
                k.wait("tensor", "wvb", wvv)
                k.wait("tensor", "dv", k.cnt["dv"])
                for ti in range(4):
                    for jh in range(2):
                        b = (ti * 2 + jh) % 2
                        gemm_wait_bank(b)
                        pev = None
                        for kk in range(8):
                            pev = k.op("tensor", lambda t, kk=kk, ti=ti, jh=jh, b=b,
                                       s=(kk == 0), e=(kk == 7):
                                       t.matmul(psG[b][:, :], rhs_sh(kk, ti * 128, 128),
                                                wvb.sl(kk, jh * 512, 512), start=s, stop=e), "pe")
                        k.wait("vector", "pe", pev)
                        k.wait("vector", "dv", vset)
                        dst = bass.AP(vbl, (2 + ti) * VSW + jh * 8 * 66,
                                      [[6 * VSW, 128], [66, 8], [1, 64]])
                        src = bass.AP(psG[b], 0, [[512, 128], [64, 8], [1, 64]])
                        dvv = k.op("vector", lambda v, d=dst, s5=src: v.tensor_copy(d, s5), "dv")
                        mark("dv", b, dvv)
                k.use("wvb", "pe", k.cnt["pe"])

                pairs = []
                for w in (2 * c, 2 * c + 1):
                    kts = [2, 3] if (w == 0 and c == 0) else [0, 1, 2, 3]
                    qoff = (w - 2 * c) * 256
                    for gq in range(4):
                        for sp in range(2):
                            pairs.append((w, tuple(kts), qoff, gq, sp))

                def emit_av1(rec):
                    idx, heads, kts, qoff, w = rec
                    par = idx % 2
                    voff = par * 256
                    nk = len(kts)
                    k.wait("tensor", "ac", gemm_ev.get(("dencp",), 0))
                    for hi, h, a in heads:
                        k.wait("tensor", "ac", gemm_ev[("exp", (par, hi))])
                        k.wait("tensor", "dv", gemm_ev.get(("pnorm", par), 0))
                        pev = None
                        for ki, kt in enumerate(kts):
                            slot = 2 * (w - 2 * c) + kt
                            vap2 = bass.AP(vbl, slot * VSW + 66 * h,
                                           [[6 * VSW, 128], [1, 65]])
                            pev = k.op("tensor", lambda t, va=vap2, hi=hi, ki=ki, par=par,
                                       voff=voff, s=(ki == 0), e=(ki == nk - 1):
                                       t.matmul(psV[hi][0:65, voff:voff + 256], va,
                                                ptb2[par][hi][:, ki * 256:ki * 256 + 256],
                                                start=s, stop=e), "pe")
                        gemm_ev[("ptfree", (par, hi))] = pev
                    acv = pn_copies(denf, voff, 256, voff)
                    gemm_ev[("dencp",)] = acv
                    return acv

                def emit_av2(rec, acv):
                    idx, heads, kts, qoff, w = rec
                    par = idx % 2
                    voff = par * 256
                    p2h = heads[0][1] // 2
                    dvv = pn_finish(obc.sl(p2h, qoff, 256, 0, 64),
                                    obc.sl(p2h, qoff, 256, 64, 128),
                                    denf, rcpd, st1, 256, acv, voff, voff)
                    gemm_ev[("pnorm", par)] = dvv

                prev = None
                pend = None
                for idx, (w, kts, qoff, gq, sp) in enumerate(pairs):
                    nk = len(kts)
                    par = idx % 2
                    h0 = 4 * gq + 2 * sp
                    heads = ((0, h0, 64 * sp), (1, h0 + 1, 64 * sp + 32))
                    for hi, h, a in heads:
                        k.wait("tensor", "dv", gemm_ev.get(("maskdv", hi), 0))
                        pev = None
                        for ki, kt in enumerate(kts):
                            k.op("tensor", lambda t, gq=gq, a=a, kt=kt, ki=ki, hi=hi, qoff=qoff, w=w, c=c:
                                 t.matmul(psS[hi][:, ki * 256:ki * 256 + 256],
                                          kpb.sl(gq, (w - 2 * c) * 256 + kt * 128, 128, a, a + 32),
                                          qpc.sl(gq, qoff, 256, a, a + 32),
                                          start=True, stop=False, tile_position=(a, 0)), "pe")
                            pev = k.op("tensor", lambda t, gq=gq, a=a, kt=kt, ki=ki, hi=hi, qoff=qoff, w=w, c=c:
                                       t.matmul(psS[hi][:, ki * 256:ki * 256 + 256],
                                                kpb.sl(gq + 4, (w - 2 * c) * 256 + kt * 128, 128, a, a + 32),
                                                qpc.sl(gq + 4, qoff, 256, a, a + 32),
                                                start=False, stop=True,
                                                tile_position=(a, 0)), "pe")
                        k.wait("vector", "pe", pev)
                        moff = 1024 if (w == 0) else 0
                        k.wait("vector", "ac", gemm_ev.get(("stg2", hi), 0))
                        mdv = k.op("vector", lambda v, hi=hi, nk=nk, moff=moff:
                                   v.tensor_add(stg2[hi][:, 0:nk * 256], psS[hi][:, 0:nk * 256],
                                                mask_b[:, moff:moff + nk * 256]), "dv")
                        gemm_ev[("maskdv", hi)] = mdv
                        k.wait("scalar", "dv", mdv)
                        k.wait("scalar", "pe", gemm_ev.get(("ptfree", (par, hi)), 0))
                        acv = k.op("scalar", lambda s, hi=hi, nk=nk, par=par:
                                   s.activation(ptb2[par][hi][:, 0:nk * 256], stg2[hi][:, 0:nk * 256],
                                                AF.Exp, scale=8.0), "ac")
                        gemm_ev[("exp", (par, hi))] = acv
                        gemm_ev[("stg2", hi)] = acv
                    if prev is not None:
                        acv1 = emit_av1(prev)
                        if pend is not None:
                            emit_av2(*pend)
                        pend = (prev, acv1)
                    prev = (idx, heads, kts, qoff, w)
                acv1 = emit_av1(prev)
                if pend is not None:
                    emit_av2(*pend)
                emit_av2(prev, acv1)
                obv = k.cnt["dv"]

                for p in range(8):
                    wv = k.dma("gpsimd", wqb[p % 2][:, :],
                               dr["wo2_t"][l, p].rearrange("p k j -> p (k j)"),
                               "wqd%d" % (p % 2), war="wqd%d" % (p % 2))
                    k.wait("tensor", "wqd%d" % (p % 2), wv)
                    k.wait("tensor", "dv", obv)
                    b = p % 2
                    gemm_wait_bank(b)
                    pev = None
                    for kk in range(8):
                        pev = k.op("tensor", lambda t, b=b, kk=kk, s=(kk == 0), e=(kk == 7):
                                   t.matmul(psG[b][:, :], wqb[b][:, 128 * kk:128 * kk + 128],
                                            obc.sl(kk, 0, 512), start=s, stop=e), "pe")
                    k.use("wqd%d" % b, "pe", pev)
                    k.wait("vector", "pe", pev)
                    dvv = k.op("vector", lambda v, p=p, b=b, c=c:
                               v.tensor_add(yfm.sl(p, 1 + c * CH, CH), psG[b][:, :],
                                            yfm.sl(p, 1 + c * CH, CH)), "dv")
                    mark("dv", b, dvv)
                k.barrier()
                k.flush()
            k.barrier()
        k.flush()

        with ExitStack() as ph:
            sb = lambda n_, s_, d_: ph.enter_context(nc.sbuf_tensor("%s_F%d" % (n_, l), s_, d_))
            ybc1 = FMB(sb("ybc1", [128, 8 * 1025], BF16), 1025)
            sqp = FMB(sb("sqp", [128, 2 * 513], BF16), 513)
            ybk = sb("ybk", [128, 8], F32)
            xnbc = FMB(sb("xnbc", [128, 8 * 1024], BF16), 1024)
            gg = FMB(sb("gg", [128, NKT2 * 1024], BF16), 1024)
            w1b = [sb("w1b0", [128, 2 * 1024], BF16), sb("w1b1", [128, 2 * 1024], BF16)]
            w2b = [sb("w2b0", [128, NKT2 * 128], BF16), sb("w2b1", [128, NKT2 * 128], BF16)]
            gel = [sb("gel0", [128, 512], BF16), sb("gel1", [128, 512], BF16)]
            mu = sb("mu3", [128, 1024], F32)
            rstd = sb("rstd3", [128, 1024], F32)
            st1 = sb("st1d", [128, 512], F32)
            st2 = sb("st2d", [128, 512], F32)
            st3 = sb("st3d", [128, 512], F32)

            for F in range(2):
                F0 = F * 1024
                for p in range(8):
                    if F == 0:
                        k.op("vector", lambda v, p=p: v.memset(ybc1.sl(p, 0, 1), 0.0), "dv")
                    else:
                        k.op("vector", lambda v, p=p:
                             v.tensor_copy(ybc1.sl(p, 0, 1), ybc1.sl(p, 1024, 1)), "dv")
                ycv = None
                for p in range(8):
                    ycv = k.op("vector", lambda v, p=p, F0=F0:
                               v.tensor_copy(ybc1.sl(p, 1, 1024), yfm.sl(p, F0 + 1, 1024)), "dv")
                if F == 0:
                    for p in range(8):
                        ycv = k.op("vector", lambda v, p=p:
                                   v.tensor_copy(ybk[:, p:p + 1], yfm.sl(p, 1024, 1)), "dv")
                k.wait("tensor", "dv", ycv)
                k.wait("scalar", "dv", ycv)

                def rhs_sh1(kk, off, n):
                    return ybc1.sl(kk, off + (0 if kk >= 4 else 1), n)

                for jc in range(2):
                    gemm_wait_bank(0)
                    gemm_wait_bank(1)
                    lastA = {}
                    sqv = {}
                    for p in range(8):
                        k.wait("scalar", "pe", lastA.get(p - 2, 0))
                        sqv[p] = k.op("scalar", lambda s, p=p, jc=jc:
                                      s.activation(sqp.sl(p % 2, 0, 513),
                                                   ybc1.sl(p, jc * 512, 513), AF.Square), "ac")
                        k.op("tensor", lambda t, p=p, jc=jc, s=(p == 0), e=(p == 7):
                             t.matmul(psG[0][:, :], onesk[:, :], rhs_sh1(p, jc * 512, 512),
                                      start=s, stop=e), "pe")
                        k.wait("tensor", "ac", sqv[p])
                        pev = k.op("tensor", lambda t, p=p, jc=jc, s=(p == 0), e=(p == 7):
                                   t.matmul(psG[1][:, :], onesk[:, :],
                                            sqp.sl(p % 2, (0 if p >= 4 else 1), 512),
                                            start=s, stop=e), "pe")
                        lastA[p] = pev
                    k.wait("vector", "pe", pev)
                    acv = ln_from_sums(mu[:, jc * 512:jc * 512 + 512],
                                       rstd[:, jc * 512:jc * 512 + 512], st1, st2, D, 512)
                k.wait("vector", "ac", acv)
                xnv = [0, 0]
                for jc in range(2):
                    for p in range(8):
                        src = yfm.sl(p, F0 + jc * 512 + (0 if p >= 4 else 1), 512)
                        k.op("vector", lambda v, p=p, jc=jc, src=src:
                             v.tensor_sub(st3[:, :], src, mu[:, jc * 512:jc * 512 + 512]), "dv")
                        if p >= 4 and F == 1 and jc == 0:
                            k.op("vector", lambda v, p=p:
                                 v.tensor_sub(st3[:, 0:1], ybk[:, p:p + 1], mu[:, 0:1]), "dv")
                        k.op("vector", lambda v, jc=jc:
                             v.tensor_mul(st1[:, :], st3[:, :],
                                          rstd[:, jc * 512:jc * 512 + 512]), "dv")
                        xnv[jc] = k.op("vector", lambda v, p=p, jc=jc:
                                       v.tensor_scalar(xnbc.sl(p, jc * 512, 512), st1[:, :],
                                                       smalls[:, 8 + p:9 + p],
                                                       smalls[:, 16 + p:17 + p],
                                                       ALU.mult, ALU.add), "dv")

                psFF = [psG[0], psG[1], psV[0], psV[1]]
                for i in range(NKT2):
                    wv = k.dma("gpsimd", w1b[i % 2][:, :],
                               dr["w1_t"][l, i].rearrange("p o k j -> p (o k j)"),
                               "w1b%d" % (i % 2), war="w1b%d" % (i % 2))
                    k.wait("tensor", "w1b%d" % (i % 2), wv)
                    for jc in range(2):
                        k.wait("tensor", "dv", xnv[jc])
                        ba = 2 * (i % 2)
                        bg = ba + 1
                        gemm_wait_bank(ba)
                        gemm_wait_bank(bg)
                        pev = None
                        for kk in range(8):
                            k.op("tensor", lambda t, i=i, kk=kk, jc=jc, ba=ba, s=(kk == 0), e=(kk == 7):
                                 t.matmul(psFF[ba][:, :], w1b[i % 2][:, 128 * kk:128 * kk + 128],
                                          xnbc.sl(kk, jc * 512, 512), start=s, stop=e), "pe")
                            pev = k.op("tensor", lambda t, i=i, kk=kk, jc=jc, bg=bg,
                                       s=(kk == 0), e=(kk == 7):
                                       t.matmul(psFF[bg][:, :],
                                                w1b[i % 2][:, 1024 + 128 * kk:1024 + 128 * kk + 128],
                                                xnbc.sl(kk, jc * 512, 512), start=s, stop=e), "pe")
                        k.use("w1b%d" % (i % 2), "pe", pev)
                        k.wait("scalar", "pe", pev)
                        k.wait("scalar", "dv", gemm_ev.get(("gel", jc), 0))
                        gcv = k.op("scalar", lambda s, i=i, jc=jc, bg=bg:
                                   s.activation(gel[jc][:, :], psFF[bg][:, :], AF.Gelu,
                                                bias=smalls[:, 54 + i:55 + i]), "ac")
                        mark("ac", bg, gcv)
                        k.wait("vector", "ac", gcv)
                        dvv = k.op("vector", lambda v, i=i, jc=jc, ba=ba:
                                   v.scalar_tensor_tensor(gg.sl(i, jc * 512, 512), psFF[ba][:, :],
                                                          smalls[:, 32 + i:33 + i], gel[jc][:, :],
                                                          ALU.add, ALU.mult), "dv")
                        mark("dv", ba, dvv)
                        gemm_ev[("gel", jc)] = dvv
                ggv = k.cnt["dv"]

                for p in range(8):
                    wv = k.dma("gpsimd", w2b[p % 2][:, :],
                               dr["w2_t"][l, p].rearrange("p k j -> p (k j)"),
                               "w2b%d" % (p % 2), war="w2b%d" % (p % 2))
                    k.wait("tensor", "w2b%d" % (p % 2), wv)
                    k.wait("tensor", "dv", ggv)
                    for jc in range(2):
                        b = (2 * p + jc) % 4
                        gemm_wait_bank(b)
                        pev = None
                        for kk in range(NKT2):
                            pev = k.op("tensor", lambda t, p=p, kk=kk, jc=jc, b=b,
                                       s=(kk == 0), e=(kk == NKT2 - 1):
                                       t.matmul(psFF[b][:, :], w2b[p % 2][:, 128 * kk:128 * kk + 128],
                                                gg.sl(kk, jc * 512, 512), start=s, stop=e), "pe")
                        k.wait("vector", "pe", pev)
                        dvv = k.op("vector", lambda v, p=p, jc=jc, b=b, F0=F0:
                                   v.scalar_tensor_tensor(yfm.sl(p, F0 + 1 + jc * 512, 512),
                                                          psFF[b][:, :], smalls[:, 24 + p:25 + p],
                                                          yfm.sl(p, F0 + 1 + jc * 512, 512),
                                                          ALU.add, ALU.add), "dv")
                        mark("dv", b, dvv)
                        k.use("w2b%d" % (p % 2), "pe", pev)
                k.barrier()
                k.flush()
            k.barrier()
        k.flush()

    import os
    stop = stop or os.environ.get("KSTOP") or None
    emit_mla()

    if os.environ.get("NOYS"):
        with ExitStack() as ph:
            dum = FMB(ph.enter_context(nc.sbuf_tensor("dumout", [128, 8 * CH], F32)), CH)
            for c in range(NCH):
                dvv = None
                for p in range(8):
                    dvv = k.op("vector", lambda v, p=p: v.memset(dum.sl(p, 0, CH), 0.0), "dv")
                k.wait("sync", "dv", dvv)
                for p in range(8):
                    k.dma("sync", dr["y_out"][128 * p:128 * p + 128, c * CH:c * CH + CH],
                          dum.sl(p, 0, CH), "out")
            k.barrier()
            k.flush()
        return nc, top, k, tap_out

    with ExitStack() as rest:
        yfm = FMB(rest.enter_context(nc.sbuf_tensor("ystream", [128, 8 * (T + 1)], F32)),
                  T + 1)
        ylv = None
        for p in range(8):
            k.op("vector", lambda v, p=p: v.memset(yfm.sl(p, 0, 1), 0.0), "dv")
            ylv = k.dma("sync", yfm.sl(p, 1, T), dr["y0_sp"][128 * p:128 * p + 128, :], "y0ld")
        k.barrier()
        tap("y0", yfm, 1)

        nlay = DEPTH
        if stop in ("mla", "mlaproj"):
            nlay = 0
        elif stop and stop.startswith("l"):
            nlay = int(stop[1:]) + 1
        for l in range(nlay):
            emit_layer(l, yfm)
            tap("y_l%d" % l, yfm, 1)

        with ExitStack() as ph:
            sb = lambda n_, s_, d_: ph.enter_context(nc.sbuf_tensor(n_, s_, d_))
            xfc = FMB(sb("xfc3", [128, 8 * CH], F32), CH)
            outc = FMB(sb("outc", [128, 8 * CH], F32), CH)
            for c in range(NCH):
                xfv = None
                for p in range(8):
                    xfv = k.dma("sync", xfc.sl(p, 0, CH),
                                dr["xo_sp"][128 * p:128 * p + 128, c * CH:c * CH + CH], "xfc",
                                war="xfcbuf" if p == 0 else None)
                k.wait("vector", "xfc", xfv)
                dvv = None
                for p in range(8):
                    dvv = k.op("vector", lambda v, p=p, c=c:
                               v.tensor_add(outc.sl(p, 0, CH), xfc.sl(p, 0, CH),
                                            yfm.sl(p, 1 + c * CH, CH)), "dv")
                k.use("xfcbuf", "dv", dvv)

                k.wait("sync", "dv", dvv)
                for p in range(8):
                    k.dma("sync", dr["y_out"][128 * p:128 * p + 128, c * CH:c * CH + CH],
                          outc.sl(p, 0, CH), "out")
                k.wait("vector", "out", k.cnt["out"])
            k.barrier()
            k.flush()

    return nc, top, k, tap_out


def kernel(**ins):
    prep = _prep(ins)
    x = np.asarray(ins["L0_attn_x"])
    nc, top, k, _ = build_nc()
    with top:
        in_maps = []
        for b in range(8):
            m = {"xT": np.ascontiguousarray(x[b].T)}
            m.update(prep)
            in_maps.append(m)
        res = run_bass_kernel_spmd(nc, in_maps, core_ids=list(range(8)))
    return np.stack([np.ascontiguousarray(r["y_out"].T) for r in res.results]).astype(np.float32)


# version 07 (final)
# speedup vs baseline: 1.0681x; 1.0449x over previous
import numpy as np
from contextlib import ExitStack

import concourse.bass as bass
import concourse.mybir as mybir
from concourse.bass_utils import run_bass_kernel_spmd

F32 = mybir.dt.float32
BF16 = mybir.dt.bfloat16
AF = mybir.ActivationFunctionType
ALU = mybir.AluOpType

T = 2048
D = 1024
LAT = 256
DEPTH = 6
CH = 512
NCH = T // CH
DI = 2730
NKT2 = 22
WIN = 256
VSW = 66 * 16
ENGS = ("tensor", "vector", "scalar", "gpsimd", "sync")


def _perm():
    p = np.zeros(D, np.int64)
    for r in range(512):
        h, d = r // 32, r % 32
        p[r] = 64 * h + d
        p[512 + r] = 64 * h + 32 + d
    return p


PERM = _perm()


def _tile_major(w):
    K, M = w.shape
    return np.ascontiguousarray(
        w.reshape(K // 128, 128, M // 128, 128).transpose(2, 1, 0, 3))


def _slab(w):
    K, M = w.shape
    return np.ascontiguousarray(w.reshape(K // 128, 128, M))


def _host_tables():
    t = np.arange(T, dtype=np.float64)
    i = np.arange(32, dtype=np.float64)
    invf = 1.0 / 10000.0 ** (i / 32.0)
    ang = np.outer(invf, t)
    mcos = np.tile(np.cos(ang), (4, 1)).astype(np.float32)
    msin = np.tile(np.sin(ang), (4, 1)).astype(np.float32)

    half = np.arange(0, 64, 2, dtype=np.float64)
    invf2 = 1.0 / 10000.0 ** (half / 64.0)
    freqs = np.outer(invf2, t)
    base = (half + 0.4 * 64) / (1.4 * 64)
    power = (t - T // 2) / (WIN // 2)
    scale = base[:, None] ** power[None, :]
    xc, xs = np.cos(freqs), np.sin(freqs)
    tabc = np.tile(xc * scale, (4, 1)).astype(np.float32)
    tabs = np.tile(xs * scale, (4, 1)).astype(np.float32)
    tabci = np.tile(xc / scale, (4, 1)).astype(np.float32)
    tabsi = np.tile(xs / scale, (4, 1)).astype(np.float32)

    NEG = np.float32(-1e30)
    kp = np.arange(128)[:, None]
    qf = np.arange(512)[None, :]
    mtri = np.concatenate(
        [((128 * r + kp) > qf) * NEG for r in range(4)], axis=1).astype(np.float32)
    qr = np.arange(256)[None, :]
    win = np.zeros((128, 1024), np.float32)
    for kt in range(4):
        kk = 128 * kt + kp
        win[:, 256 * kt:256 * kt + 256] = ((kk < qr) | (kk > qr + 256)) * NEG
    win0 = np.zeros((128, 512), np.float32)
    for kt in range(2):
        kk = 128 * kt + kp
        win0[:, 256 * kt:256 * kt + 256] = (kk > qr) * NEG

    onesk = np.ones((128, 128), np.float32)
    bo32 = np.zeros((128, 128), np.float32)
    for b in range(4):
        bo32[32 * b:32 * b + 32, 32 * b:32 * b + 32] = 1.0
    return dict(mcos=mcos, msin=msin, tabc=tabc, tabs=tabs, tabci=tabci,
                tabsi=tabsi, mtri=mtri, win=win, win0=win0, onesk=onesk,
                bo32=bo32)


def _prep(ins):
    o = {}
    o.update(_host_tables())
    o["wq_t"] = _tile_major(np.asarray(ins["L0_attn_W_q"])[:, PERM])
    o["wdkv_t"] = _tile_major(np.asarray(ins["L0_attn_W_dkv"]))
    o["wuk_t"] = _tile_major(np.asarray(ins["L0_attn_W_uk"])[:, PERM])
    o["wuv_s"] = _slab(np.asarray(ins["L0_attn_W_uv"]))
    o["wo_t"] = _tile_major(np.asarray(ins["L0_attn_W_o"]))
    o["gam_c"] = np.ascontiguousarray(np.asarray(ins["L0_attn_ln_gamma"]).reshape(2, 128).T)
    o["bet_c"] = np.ascontiguousarray(np.asarray(ins["L0_attn_ln_beta"]).reshape(2, 128).T)

    wqkv = np.asarray(ins["L0_ffn_Wqkv"])
    qk = np.concatenate([wqkv[:, :, 0:D][:, :, PERM],
                         wqkv[:, :, D:2 * D][:, :, PERM]], axis=2)
    o["wqkv_qk_t"] = np.stack([_tile_major(qk[l]) for l in range(DEPTH)])
    o["wqkv_v_s"] = np.stack([_slab(wqkv[l][:, 2 * D:3 * D]) for l in range(DEPTH)])
    o["wo2_t"] = np.stack([_tile_major(np.asarray(ins["L0_ffn_Wo"])[l]) for l in range(DEPTH)])

    w1 = np.asarray(ins["L0_ffn_W1"])
    b1 = np.asarray(ins["L0_ffn_b1"])
    w1p = np.zeros((DEPTH, D, 44 * 128), np.float32)
    b1a = np.zeros((DEPTH, 128, NKT2), np.float32)
    b1g = np.zeros((DEPTH, 128, NKT2), np.float32)
    for i in range(NKT2):
        c0 = 128 * i
        n = min(128, DI - c0)
        w1p[:, :, 256 * i:256 * i + n] = w1[:, :, c0:c0 + n]
        w1p[:, :, 256 * i + 128:256 * i + 128 + n] = w1[:, :, DI + c0:DI + c0 + n]
        b1a[:, :n, i] = b1[:, c0:c0 + n]
        b1g[:, :n, i] = b1[:, DI + c0:DI + c0 + n]
    w1t = np.stack([_tile_major(w1p[l]) for l in range(DEPTH)])
    o["w1_t"] = np.ascontiguousarray(
        w1t.reshape(DEPTH, NKT2, 2, 128, 8, 128).transpose(0, 1, 3, 2, 4, 5))
    o["b1a_c"] = b1a
    o["b1g_c"] = b1g
    w2 = np.asarray(ins["L0_ffn_W2"])
    w2p = np.zeros((DEPTH, NKT2 * 128, D), np.float32)
    w2p[:, :DI, :] = w2
    o["w2_t"] = np.stack([_tile_major(w2p[l]) for l in range(DEPTH)])
    o["b2_c"] = np.ascontiguousarray(np.asarray(ins["L0_ffn_b2"]).reshape(DEPTH, 8, 128).transpose(0, 2, 1))
    o["g_c"] = np.ascontiguousarray(np.asarray(ins["L0_ffn_ln_g"]).reshape(DEPTH, 8, 128).transpose(0, 2, 1))
    o["bln_c"] = np.ascontiguousarray(np.asarray(ins["L0_ffn_ln_b"]).reshape(DEPTH, 8, 128).transpose(0, 2, 1))
    qs = np.asarray(ins["L0_ffn_q_scale"])
    ks = np.asarray(ins["L0_ffn_k_scale"])
    o["qsk_c"] = np.stack([
        np.stack([np.tile(qs[l, 0:32], 4), np.tile(qs[l, 32:64], 4),
                  np.tile(ks[l, 0:32], 4), np.tile(ks[l, 32:64], 4)], axis=1)
        for l in range(DEPTH)]).astype(np.float32)
    return o


class K:
    def __init__(self, nc, es):
        self.nc = nc
        self.ops = {e: [] for e in ENGS}
        self.sems = {}
        self.cnt = {}
        self.lastw = {e: {} for e in ENGS}
        self.bufev = {}
        self.es = es
        self.trace = {e: [] for e in ENGS}
        for s in ("pe", "dv", "ac"):
            self.sems[s] = es.enter_context(nc.semaphore(s))
            self.cnt[s] = 0

    def emit(self, eng, fn):
        self.ops[eng].append(fn)

    def wait(self, eng, sem, val):
        if val is None or val <= 0 or self.lastw[eng].get(sem, 0) >= val:
            return
        self.lastw[eng][sem] = val
        s = self.sems[sem]
        self.trace[eng].append(("w", sem, val))
        self.emit(eng, lambda e, s=s, v=val: e.wait_ge(s, v))

    def op(self, eng, fn, sem=None, dn=1):
        if sem is None:
            self.emit(eng, fn)
            return None
        if sem in ("dv", "ac"):
            self.wait(eng, sem, self.cnt[sem])
        self.cnt[sem] += dn
        s = self.sems[sem]
        self.trace[eng].append(("i", sem, dn))
        self.emit(eng, lambda e, fn=fn, s=s, d=dn: fn(e).then_inc(s, d))
        return self.cnt[sem]

    def check_deadlock(self):
        pos = {e: 0 for e in ENGS}
        cur = {s: 0 for s in self.cnt}
        tr = self.trace
        progress = True
        while progress:
            progress = False
            for e in ENGS:
                while pos[e] < len(tr[e]):
                    kind, sem, val = tr[e][pos[e]]
                    if kind == "i":
                        cur[sem] += val
                        pos[e] += 1
                        progress = True
                    else:
                        if cur[sem] >= val:
                            pos[e] += 1
                            progress = True
                        else:
                            break
        stuck = {e: (pos[e], len(tr[e])) for e in ENGS if pos[e] < len(tr[e])}
        if stuck:
            for e, (p, n) in stuck.items():
                print("DEADLOCK", e, "at", p, "/", n, "on", tr[e][p],
                      "cur=", {s: cur[s] for s in cur})
            return False
        return True

    def chan(self, name):
        if name not in self.sems:
            self.sems[name] = self.es.enter_context(self.nc.semaphore(name))
            self.cnt[name] = 0
        return name

    def dma(self, eng, out, in_, sem, war=None):
        self.chan(sem)
        if war is not None:
            for (s2, v2) in self.bufev.get(war, ()):
                self.wait(eng, s2, v2)
        return self.op(eng, lambda e, o=out, i=in_: e.dma_start(o, i), sem, 16)

    def use(self, key, sem, val):
        self.bufev[key] = ((sem, val),)

    def barrier(self):
        snap = dict(self.cnt)
        for eng in ENGS:
            for s, v in snap.items():
                self.wait(eng, s, v)

    def flush(self):
        nonempty = [e for e in ENGS if self.ops[e]]
        if not nonempty:
            return
        with self.nc.Block() as block:
            for eng in nonempty:
                lst = self.ops[eng]

                def body(e, lst=lst):
                    for f in lst:
                        f(e)
                getattr(block, eng)(body)
        self.ops = {e: [] for e in ENGS}


class FMB:
    def __init__(self, t, w):
        self.t = t
        self.w = w

    def sl(self, p, o, n, r0=0, r1=128):
        return self.t[r0:r1, p * self.w + o: p * self.w + o + n]


def build_nc(taps=(), stop=None):
    nc = bass.Bass("TRN2", target_bir_lowering=False)
    top = ExitStack()
    k = K(nc, top)

    dr = {}
    dr["xT"] = nc.dram_tensor("xT", [D, T], F32, kind="ExternalInput")
    dr["y_out"] = nc.dram_tensor("y_out", [D, T], F32, kind="ExternalOutput")
    dr["xo_sp"] = nc.dram_tensor("xo_sp", [D, T], F32, kind="Internal")
    dr["y0_sp"] = nc.dram_tensor("y0_sp", [D, T], F32, kind="Internal")
    shapes = dict(
        wq_t=[8, 128, 8, 128], wdkv_t=[2, 128, 8, 128], wuk_t=[8, 128, 2, 128],
        wuv_s=[2, 128, D], wo_t=[8, 128, 8, 128], gam_c=[128, 2], bet_c=[128, 2],
        wqkv_qk_t=[DEPTH, 16, 128, 8, 128], wqkv_v_s=[DEPTH, 8, 128, D],
        wo2_t=[DEPTH, 8, 128, 8, 128], w1_t=[DEPTH, NKT2, 128, 2, 8, 128],
        b1a_c=[DEPTH, 128, NKT2], b1g_c=[DEPTH, 128, NKT2],
        w2_t=[DEPTH, 8, 128, NKT2, 128], b2_c=[DEPTH, 128, 8],
        g_c=[DEPTH, 128, 8], bln_c=[DEPTH, 128, 8], qsk_c=[DEPTH, 128, 4],
        mcos=[128, T], msin=[128, T], tabc=[128, T], tabs=[128, T],
        tabci=[128, T], tabsi=[128, T], mtri=[128, 4 * 512], win=[128, 1024],
        win0=[128, 512], onesk=[128, 128], bo32=[128, 128],
    )
    for n, s in shapes.items():
        dr[n] = nc.dram_tensor(n, s, F32, kind="ExternalInput")
    tap_out = {}
    for name in taps:
        tap_out[name] = nc.dram_tensor("tap_" + name, [D, T], F32, kind="ExternalOutput")

    tapstage = [None]
    if tap_out:
        tapstage[0] = top.enter_context(nc.sbuf_tensor("tapstage", [128, T], F32))

    def tapT(name, tens, w):
        if name not in tap_out:
            return
        dt = tap_out[name]
        k.barrier()
        dvv = k.op("vector", lambda v, w=w: v.tensor_copy(tapstage[0][:, 0:w], tens[:, 0:w]), "dv")
        k.wait("sync", "dv", dvv)
        wsv = k.dma("sync", dt[0:128, 0:w], tapstage[0][:, 0:w], "tap")
        k.wait("vector", "tap", wsv)
        k.barrier()

    def tap(name, fmb, col0=0, w=T):
        if name not in tap_out:
            return
        dt = tap_out[name]
        k.barrier()
        for p in range(8):
            dvv = k.op("vector", lambda v, p=p, w=w, c0=col0:
                       v.tensor_copy(tapstage[0][:, 0:w], fmb.sl(p, c0, w)), "dv")
            k.wait("sync", "dv", dvv)
            wsv = k.dma("sync", dt[128 * p:128 * p + 128, 0:w], tapstage[0][:, 0:w], "tap")
            k.wait("vector", "tap", wsv)
        k.barrier()

    psA = top.enter_context(nc.psum_tensor("psA", [128, 512], F32))
    psB = top.enter_context(nc.psum_tensor("psB", [128, 512], F32))
    psS0 = top.enter_context(nc.psum_tensor("psS0", [128, 1024], F32))
    psS1 = top.enter_context(nc.psum_tensor("psS1", [128, 1024], F32))
    psV0 = top.enter_context(nc.psum_tensor("psV0", [128, 512], F32))
    psV1 = top.enter_context(nc.psum_tensor("psV1", [128, 512], F32))
    psG = [psA, psB]
    psS = [psS0, psS1]
    psV = [psV0, psV1]

    onesk = top.enter_context(nc.sbuf_tensor("onesk_b", [128, 128], BF16))
    bo32 = top.enter_context(nc.sbuf_tensor("bo32_b", [128, 128], BF16))
    tabs4 = top.enter_context(nc.sbuf_tensor("tabs4", [128, 4 * T], BF16))
    mask_b = top.enter_context(nc.sbuf_tensor("mask_b", [128, 2048], BF16))
    smalls = top.enter_context(nc.sbuf_tensor("smalls", [128, 80], F32))

    epsc = top.enter_context(nc.sbuf_tensor("epsc", [128, 1], F32))
    onesf = top.enter_context(nc.sbuf_tensor("onesf", [33, 64], F32))
    k.dma("gpsimd", onesk[:, :], dr["onesk"][:, :], "cst")
    k.dma("gpsimd", bo32[:, :], dr["bo32"][:, :], "cst")
    k.dma("sync", smalls[:, 0:2], dr["gam_c"][:, :], "sml")
    k.dma("sync", smalls[:, 2:4], dr["bet_c"][:, :], "sml")
    k.op("vector", lambda v: v.memset(epsc[:, :], 1e-5), "dv")
    k.op("vector", lambda v: v.memset(onesf[:, :], 1.0), "dv")

    gemm_ev = {}

    def gemm_wait_bank(b):
        k.wait("tensor", "dv", gemm_ev.get(("dv", b), 0))
        k.wait("tensor", "ac", gemm_ev.get(("ac", b), 0))

    def mark(kind, b, v):
        gemm_ev[(kind, b)] = max(gemm_ev.get((kind, b), 0), v)

    def ln_from_sums(mu_ap, rstd_ap, st1, st2, n, wdt):
        k.op("vector", lambda v: v.tensor_scalar(mu_ap, psG[0][:, 0:wdt],
                                                 1.0 / n, None, ALU.mult), "dv")
        dva = k.op("vector", lambda v: v.tensor_scalar(st1[:, 0:wdt], psG[1][:, 0:wdt],
                                                       1.0 / n, None, ALU.mult), "dv")
        mark("dv", 0, dva)
        mark("dv", 1, dva)
        k.op("vector", lambda v: v.tensor_mul(st2[:, 0:wdt], mu_ap, mu_ap), "dv")
        dvv = k.op("vector", lambda v: v.tensor_sub(st1[:, 0:wdt], st1[:, 0:wdt],
                                                    st2[:, 0:wdt]), "dv")
        k.wait("scalar", "dv", dvv)
        k.op("scalar", lambda s: s.activation(st2[:, 0:wdt], st1[:, 0:wdt], AF.Ln,
                                              bias=epsc[:, 0:1]), "ac")
        return k.op("scalar", lambda s: s.activation(rstd_ap, st2[:, 0:wdt],
                                                     AF.Exp, scale=-0.5), "ac")

    def pn_copies(denf, doff, jw, voff):
        k.wait("scalar", "pe", k.cnt["pe"])
        k.op("scalar", lambda s: s.copy(denf[0:1, doff:doff + jw],
                                        psV[0][64:65, voff:voff + jw]), "ac")
        return k.op("scalar", lambda s: s.copy(denf[32:33, doff:doff + jw],
                                               psV[1][64:65, voff:voff + jw]), "ac")

    def pn_finish(obc_ap0, obc_ap1, denf, rcpd, st1, jw, acv, voff, doff):
        k.wait("tensor", "ac", acv)
        gemm_wait_bank(1)
        k.op("tensor", lambda t: t.matmul(psG[1][0:64, 0:jw], onesf[0:1, 0:64],
                                          denf[0:1, doff:doff + jw],
                                          start=True, stop=True), "pe")
        pev = k.op("tensor", lambda t: t.matmul(psG[1][64:128, 0:jw], onesf[32:33, 0:64],
                                                denf[32:33, doff:doff + jw],
                                                start=True, stop=True,
                                                tile_position=(32, 64)), "pe")
        k.wait("scalar", "pe", pev)
        k.wait("scalar", "dv", gemm_ev.get(("pnlast",), 0))
        k.op("scalar", lambda s: s.activation(st1[:, 0:jw], psG[1][:, 0:jw], AF.Ln), "ac")
        acv = k.op("scalar", lambda s: s.activation(rcpd[:, 0:jw], st1[:, 0:jw],
                                                    AF.Exp, scale=-1.0), "ac")
        mark("ac", 1, acv)
        k.wait("vector", "ac", acv)
        k.op("vector", lambda v: v.tensor_mul(obc_ap0, psV[0][0:64, voff:voff + jw],
                                              rcpd[0:64, 0:jw]), "dv")
        dvv = k.op("vector", lambda v: v.tensor_mul(obc_ap1, psV[1][0:64, voff:voff + jw],
                                                    rcpd[64:128, 0:jw]), "dv")
        gemm_ev[("pnlast",)] = dvv
        return dvv

    def pair_norm_evac(obc_ap0, obc_ap1, denf, rcpd, st1, jw):
        acv = pn_copies(denf, 0, jw, 0)
        return pn_finish(obc_ap0, obc_ap1, denf, rcpd, st1, jw, acv, 0, 0)

    def emit_mla():
        with ExitStack() as ph:
            sb = lambda n_, s_, d_: ph.enter_context(nc.sbuf_tensor(n_, s_, d_))
            qp = FMB(sb("qp", [128, 8 * T], BF16), T)
            kp = FMB(sb("kp", [128, 8 * T], BF16), T)
            vsl = sb("vsl", [128, 16 * VSW], BF16)
            latnb = FMB(sb("latnb", [128, 2 * T], BF16), T)
            wqb = [sb("wqb0", [128, 8 * 128], BF16), sb("wqb1", [128, 8 * 128], BF16)]

            k.dma("gpsimd", tabs4[:, 0:T], dr["mcos"][:, :], "cst")
            k.dma("gpsimd", tabs4[:, T:2 * T], dr["msin"][:, :], "cst")
            k.dma("gpsimd", mask_b[:, 0:4 * 512], dr["mtri"][:, :], "cst")
            vset = k.op("vector", lambda v: v.memset(vsl[:, :], 1.0), "dv")
            k.barrier()

            with ExitStack() as p2:
                s2 = lambda n_, s_, d_: p2.enter_context(nc.sbuf_tensor(n_, s_, d_))
                xbc = FMB(s2("xbc", [128, 8 * CH], BF16), CH)
                sqc = FMB(s2("sqc", [128, 2 * CH], BF16), CH)
                xfc = FMB(s2("xfc", [128, 8 * CH], F32), CH)
                xnc = FMB(s2("xnc", [128, 8 * CH], BF16), CH)
                mu = s2("mu", [128, CH], F32)
                rstd = s2("rstd", [128, CH], F32)
                st1 = s2("st1", [128, CH], F32)
                st2 = s2("st2", [128, CH], F32)
                tA = s2("tA", [128, CH], F32)
                tB = s2("tB", [128, CH], F32)
                qrp = FMB(s2("qrp", [128, 2 * CH], BF16), CH)
                latfc = FMB(s2("latfc", [128, 2 * CH], F32), CH)

                def rope_pair(pr, dstf, dstoff, c, co, so):
                    for (dp, sA, sB, sub) in ((pr, 0, 1, True), (pr + 4, 1, 0, False)):
                        k.op("vector", lambda v, sA=sA, co=co, c=c:
                             v.tensor_mul(tA[:, :], qrp.sl(sA, 0, CH),
                                          tabs4[:, co + c * CH:co + c * CH + CH]), "dv")
                        k.op("vector", lambda v, sB=sB, so=so, c=c:
                             v.tensor_mul(tB[:, :], qrp.sl(sB, 0, CH),
                                          tabs4[:, so + c * CH:so + c * CH + CH]), "dv")
                        if sub:
                            dvv = k.op("vector", lambda v, dp=dp:
                                       v.tensor_sub(dstf.sl(dp, dstoff, CH), tA[:, :], tB[:, :]), "dv")
                        else:
                            dvv = k.op("vector", lambda v, dp=dp:
                                       v.tensor_add(dstf.sl(dp, dstoff, CH), tA[:, :], tB[:, :]), "dv")
                    return dvv

                def stats_pass(src_fn, nf, do_square, sq_fn=None):
                    lastA = {}
                    sqv = {}
                    for p in range(nf):
                        if do_square:
                            k.wait("scalar", "pe", lastA.get(p - 2, 0))
                            sqv[p] = k.op("scalar", lambda s, p=p:
                                          s.activation(sqc.sl(p % 2, 0, CH), src_fn(p),
                                                       AF.Square), "ac")
                        k.op("tensor", lambda t, p=p, s=(p == 0), e=(p == nf - 1):
                             t.matmul(psG[0][:, :], onesk[:, :], src_fn(p),
                                      start=s, stop=e), "pe")
                        if do_square:
                            k.wait("tensor", "ac", sqv[p])
                            src2 = lambda p=p: sqc.sl(p % 2, 0, CH)
                        else:
                            src2 = lambda p=p: sq_fn(p)
                        pev = k.op("tensor", lambda t, p=p, s=(p == 0), e=(p == nf - 1), s2=src2:
                                   t.matmul(psG[1][:, :], onesk[:, :], s2(),
                                            start=s, stop=e), "pe")
                        lastA[p] = pev
                    return pev

                for c in range(NCH):
                    xbv = None
                    xfv = None
                    for p in range(8):
                        xbv = k.dma("gpsimd", xbc.sl(p, 0, CH),
                                    dr["xT"][128 * p:128 * p + 128, c * CH:c * CH + CH], "xbc")
                        xfv = k.dma("sync", xfc.sl(p, 0, CH),
                                    dr["xT"][128 * p:128 * p + 128, c * CH:c * CH + CH], "xfc")
                    k.wait("scalar", "xbc", xbv)
                    k.wait("tensor", "xbc", xbv)
                    gemm_wait_bank(0)
                    gemm_wait_bank(1)
                    pev = stats_pass(lambda p: xbc.sl(p, 0, CH), 8, True)
                    k.wait("vector", "pe", pev)
                    acv = ln_from_sums(mu[:, :], rstd[:, :], st1, st2, D, CH)
                    k.wait("vector", "xfc", xfv)
                    k.wait("vector", "ac", acv)
                    if c == 3:
                        tapT("mu_dbg", mu, CH)
                        tapT("rstd_dbg", rstd, CH)
                    xnv = None
                    for p in range(8):
                        k.op("vector", lambda v, p=p:
                             v.tensor_sub(st2[:, :], xfc.sl(p, 0, CH), mu[:, :]), "dv")
                        xnv = k.op("vector", lambda v, p=p:
                                   v.tensor_mul(xnc.sl(p, 0, CH), st2[:, :],
                                                rstd[:, :]), "dv")
                    k.wait("tensor", "dv", xnv)

                    import os as _o
                    for pr in range(0 if _o.environ.get("SKIP_Q") else 4):
                        for half, p in ((0, pr), (1, pr + 4)):
                            wv = k.dma("gpsimd", wqb[half][:, :],
                                       dr["wq_t"][p].rearrange("p k j -> p (k j)"),
                                       "wqb%d" % half, war="wqb%d" % half)
                            k.wait("tensor", "wqb%d" % half, wv)
                            gemm_wait_bank(half)
                            pev = None
                            for kk in range(8):
                                pev = k.op("tensor", lambda t, h2=half, kk=kk,
                                           s=(kk == 0), e=(kk == 7):
                                           t.matmul(psG[h2][:, :], wqb[h2][:, 128 * kk:128 * kk + 128],
                                                    xnc.sl(kk, 0, CH), start=s, stop=e), "pe")
                            k.use("wqb%d" % half, "pe", pev)
                            k.wait("vector", "pe", pev)
                            dvv = k.op("vector", lambda v, h2=half:
                                       v.tensor_copy(qrp.sl(h2, 0, CH), psG[h2][:, :]), "dv")
                            mark("dv", half, dvv)
                        rope_pair(pr, qp, c * CH, c, 0, T)

                    LP = int(_o.environ.get("LATPART", "5"))
                    for lp in range(0 if _o.environ.get("SKIP_LAT") else 2):
                        wv = k.dma("gpsimd", wqb[lp][:, :],
                                   dr["wdkv_t"][lp].rearrange("p k j -> p (k j)"),
                                   "wqb%d" % lp, war="wqb%d" % lp)
                        k.wait("tensor", "wqb%d" % lp, wv)
                        gemm_wait_bank(lp)
                        pev = None
                        for kk in range(8):
                            pev = k.op("tensor", lambda t, lp=lp, kk=kk,
                                       s=(kk == 0), e=(kk == 7):
                                       t.matmul(psG[lp][:, :], wqb[lp][:, 128 * kk:128 * kk + 128],
                                                xnc.sl(kk, 0, CH), start=s, stop=e), "pe")
                        k.use("wqb%d" % lp, "pe", pev)
                        if LP >= 2:
                            k.wait("vector", "pe", pev)
                            dvv = k.op("vector", lambda v, lp=lp:
                                       v.tensor_copy(latfc.sl(lp, 0, CH), psG[lp][:, :]), "dv")
                            mark("dv", lp, dvv)
                        if LP >= 3:
                            k.wait("scalar", "dv", dvv)
                            acv = k.op("scalar", lambda s, lp=lp:
                                       s.copy(qrp.sl(lp, 0, CH), latfc.sl(lp, 0, CH)), "ac")
                    if not _o.environ.get("SKIP_LAT") and LP >= 4:
                        k.wait("tensor", "ac", k.cnt["ac"])
                        gemm_wait_bank(0)
                        gemm_wait_bank(1)
                        pev = stats_pass(lambda p: qrp.sl(p, 0, CH), 2, True)
                        k.wait("vector", "pe", pev)
                        acv = ln_from_sums(mu[:, :], rstd[:, :], st1, st2, LAT, CH)
                        if LP >= 5:
                            k.wait("vector", "ac", acv)
                            lnv = None
                            for lp in range(2):
                                k.op("vector", lambda v, lp=lp:
                                     v.tensor_sub(st2[:, :], latfc.sl(lp, 0, CH), mu[:, :]), "dv")
                                k.op("vector", lambda v, lp=lp:
                                     v.tensor_mul(latfc.sl(lp, 0, CH), st2[:, :], rstd[:, :]), "dv")
                                lnv = k.op("vector", lambda v, lp=lp, c=c:
                                           v.tensor_scalar(latnb.sl(lp, c * CH, CH), latfc.sl(lp, 0, CH),
                                                           smalls[:, lp:lp + 1], smalls[:, 2 + lp:3 + lp],
                                                           ALU.mult, ALU.add), "dv")
                            k.wait("tensor", "dv", lnv)

                    for pr in range(0 if (_o.environ.get("SKIP_K") or _o.environ.get("SKIP_LAT")) else 4):
                        for half, p in ((0, pr), (1, pr + 4)):
                            wv = k.dma("gpsimd", wqb[half][:, 0:2 * 128],
                                       dr["wuk_t"][p].rearrange("p k j -> p (k j)"),
                                       "wqb%d" % half, war="wqb%d" % half)
                            k.wait("tensor", "wqb%d" % half, wv)
                            gemm_wait_bank(half)
                            pev = None
                            for kk in range(2):
                                pev = k.op("tensor", lambda t, h2=half, kk=kk, c=c,
                                           s=(kk == 0), e=(kk == 1):
                                           t.matmul(psG[h2][:, :], wqb[h2][:, 128 * kk:128 * kk + 128],
                                                    latnb.sl(kk, c * CH, CH), start=s, stop=e), "pe")
                            k.use("wqb%d" % half, "pe", pev)
                            k.wait("vector", "pe", pev)
                            dvv = k.op("vector", lambda v, h2=half:
                                       v.tensor_copy(qrp.sl(h2, 0, CH), psG[h2][:, :]), "dv")
                            mark("dv", half, dvv)
                        rope_pair(pr, kp, c * CH, c, 0, T)

                    if _o.environ.get("SKIP_V") or _o.environ.get("SKIP_LAT"):
                        k.barrier()
                        continue
                    for lk in range(2):
                        k.dma("gpsimd", wqb[lk][:, 0:512], dr["wuv_s"][lk][:, 0:512],
                              "wqb%d" % lk, war="wqb%d" % lk)
                        wvv = k.dma("gpsimd", wqb[lk][:, 512:1024], dr["wuv_s"][lk][:, 512:1024],
                                    "wqb%d" % lk)
                        k.wait("tensor", "wqb%d" % lk, wvv)
                    k.wait("tensor", "dv", k.cnt["dv"])
                    for ti in range(4):
                        tt = 4 * c + ti
                        for jh in range(2):
                            b = (ti * 2 + jh) % 2
                            gemm_wait_bank(b)
                            pev = None
                            for lk in range(2):
                                pev = k.op("tensor", lambda t, lk=lk, ti=ti, jh=jh, b=b, c=c,
                                           s=(lk == 0), e=(lk == 1):
                                           t.matmul(psG[b][:, :],
                                                    latnb.sl(lk, c * CH + ti * 128, 128),
                                                    wqb[lk][:, jh * 512:jh * 512 + 512],
                                                    start=s, stop=e), "pe")
                            k.wait("vector", "pe", pev)
                            k.wait("vector", "dv", vset)
                            dst = bass.AP(vsl, tt * VSW + jh * 8 * 66,
                                          [[16 * VSW, 128], [66, 8], [1, 64]])
                            src = bass.AP(psG[b], 0, [[512, 128], [64, 8], [1, 64]])
                            dvv = k.op("vector", lambda v, d=dst, s3=src: v.tensor_copy(d, s3), "dv")
                            mark("dv", b, dvv)
                    k.use("wqb0", "pe", k.cnt["pe"])
                    k.use("wqb1", "pe", k.cnt["pe"])
                    k.barrier()
                tap("xn_mla", xnc, 0, CH)
                k.flush()
            k.barrier()
            tap("qp", qp)
            tap("kp", kp)
            tapT("vsl_dbg", vsl, 2048)
            tapT("latnb_dbg", latnb.t, 2048)

            import os as _os
            if (_os.environ.get("KSTOP") or "") == "mlaproj":
                k.barrier()
                k.flush()
                return
            with ExitStack() as p3:
                s3 = lambda n_, s_, d_: p3.enter_context(nc.sbuf_tensor(n_, s_, d_))
                ptb = [s3("pt%d" % i, [128, 512], BF16) for i in range(4)]
                stg = [s3("stg%d" % i, [128, 512], F32) for i in range(2)]
                obc = FMB(s3("obc", [128, 8 * 512], BF16), 512)
                rcpd = s3("rcpd", [128, 512], F32)
                denf = s3("denf", [33, 512], F32)
                st1 = s3("st1b", [128, 512], F32)
                st2 = s3("st2b", [128, CH], F32)
                xfc = FMB(s3("xfc2", [128, 8 * CH], F32), CH)
                xoc = FMB(s3("xoc", [128, 8 * CH], F32), CH)
                xob = FMB(s3("xob", [128, 4 * CH], BF16), CH)
                mu = s3("mu2", [128, CH], F32)
                rstd = s3("rstd2", [128, CH], F32)
                wqb2 = [s3("wqc0", [128, 8 * 128], BF16), s3("wqc1", [128, 8 * 128], BF16)]

                AP_ = int(os.environ.get("ATTPART", "5"))
                for qc in range(NCH):
                    nkt = 4 * qc + 4
                    for gq in range(4):
                        for sp in range(2):
                            h0 = 4 * gq + 2 * sp
                            heads = ((0, h0, 64 * sp), (1, h0 + 1, 64 * sp + 32))
                            expv = {0: 0, 1: 0}
                            for kt in range(nkt):
                                par = kt % 2
                                for hi, h, a in heads:
                                    sl = psS[hi][:, par * 512:par * 512 + 512]
                                    k.wait("tensor", "ac", expv[hi])
                                    k.op("tensor", lambda t, gq=gq, a=a, kt=kt, qc=qc, sl=sl:
                                         t.matmul(sl, kp.sl(gq, kt * 128, 128, a, a + 32),
                                                  qp.sl(gq, qc * CH, CH, a, a + 32),
                                                  start=True, stop=False, tile_position=(a, 0)), "pe")
                                    pev = k.op("tensor", lambda t, gq=gq, a=a, kt=kt, qc=qc, sl=sl:
                                               t.matmul(sl, kp.sl(gq + 4, kt * 128, 128, a, a + 32),
                                                        qp.sl(gq + 4, qc * CH, CH, a, a + 32),
                                                        start=False, stop=True,
                                                        tile_position=(a, 0)), "pe")
                                    r = kt - 4 * qc
                                    pt = ptb[2 * hi + par]
                                    if r >= 0:
                                        k.wait("vector", "pe", pev)
                                        k.wait("vector", "ac", gemm_ev.get(("stg", hi), 0))
                                        mdv = k.op("vector", lambda v, sl=sl, r=r, hi=hi:
                                                   v.tensor_add(stg[hi][:, :], sl,
                                                                mask_b[:, r * 512:r * 512 + 512]), "dv")
                                        k.wait("scalar", "dv", mdv)
                                        acv = k.op("scalar", lambda s, hi=hi, pt=pt:
                                                   s.activation(pt[:, :], stg[hi][:, :],
                                                                AF.Exp, scale=0.125), "ac")
                                        gemm_ev[("stg", hi)] = acv
                                        expv[hi] = acv
                                        if AP_ < 2:
                                            continue
                                        k.wait("tensor", "ac", acv)
                                        if kt == 0:
                                            k.wait("tensor", "dv", gemm_ev.get(("pnorm", 0), 0))
                                        vap2 = bass.AP(vsl, kt * VSW + 66 * h,
                                                       [[16 * VSW, 128], [1, 65]])
                                        k.op("tensor", lambda t, va=vap2, pt=pt, hi=hi,
                                             s=(kt == 0), e=(kt == nkt - 1):
                                             t.matmul(psV[hi][0:65, :], va, pt[:, :],
                                                      start=s, stop=e), "pe")
                                        continue
                                    k.wait("scalar", "pe", pev)
                                    acv = k.op("scalar", lambda s, sl=sl, pt=pt:
                                               s.activation(pt[:, :], sl, AF.Exp, scale=0.125), "ac")
                                    expv[hi] = acv
                                    if AP_ < 2:
                                        continue
                                    k.wait("tensor", "ac", acv)
                                    if kt == 0:
                                        k.wait("tensor", "dv", gemm_ev.get(("pnorm", 0), 0))
                                    vap2 = bass.AP(vsl, kt * VSW + 66 * h,
                                                   [[16 * VSW, 128], [1, 65]])
                                    k.op("tensor", lambda t, va=vap2, pt=pt, hi=hi,
                                         s=(kt == 0), e=(kt == nkt - 1):
                                         t.matmul(psV[hi][0:65, :], va, pt[:, :],
                                                  start=s, stop=e), "pe")
                            pev = k.cnt["pe"]
                            k.wait("vector", "pe", pev)
                            if AP_ < 3:
                                continue
                            p = h0 // 2
                            dvv = pair_norm_evac(obc.sl(p, 0, 512, 0, 64),
                                                 obc.sl(p, 0, 512, 64, 128),
                                                 denf, rcpd, st1, 512)
                            k.wait("tensor", "dv", dvv)
                    obv = k.cnt["dv"]
                    if qc == NCH - 1:
                        tap("obc_mla", obc, 0, 512)
                    if AP_ < 4:
                        continue

                    xfv = None
                    for p in range(8):
                        xfv = k.dma("sync", xfc.sl(p, 0, CH),
                                    dr["xT"][128 * p:128 * p + 128, qc * CH:qc * CH + CH], "xfc")
                    for p in range(8):
                        wv = k.dma("gpsimd", wqb2[p % 2][:, :],
                                   dr["wo_t"][p].rearrange("p k j -> p (k j)"),
                                   "wqc%d" % (p % 2), war="wqc%d" % (p % 2))
                        k.wait("tensor", "wqc%d" % (p % 2), wv)
                        k.wait("tensor", "dv", obv)
                        b = p % 2
                        gemm_wait_bank(b)
                        pev = None
                        for kk in range(8):
                            pev = k.op("tensor", lambda t, b=b, kk=kk, s=(kk == 0), e=(kk == 7):
                                       t.matmul(psG[b][:, :], wqb2[b][:, 128 * kk:128 * kk + 128],
                                                obc.sl(kk, 0, 512), start=s, stop=e), "pe")
                        k.use("wqc%d" % b, "pe", pev)
                        k.wait("vector", "pe", pev)
                        k.wait("vector", "xfc", xfv)
                        dvv = k.op("vector", lambda v, p=p, b=b:
                                   v.tensor_add(xoc.sl(p, 0, CH), psG[b][:, :],
                                                xfc.sl(p, 0, CH)), "dv")
                        mark("dv", b, dvv)
                        k.wait("sync", "dv", dvv)
                        k.dma("sync", dr["xo_sp"][128 * p:128 * p + 128, qc * CH:qc * CH + CH],
                              xoc.sl(p, 0, CH), "spill")
                    xonv = k.cnt["dv"]
                    spillv = k.cnt["spill"]
                    gemm_wait_bank(0)
                    gemm_wait_bank(1)
                    lastA = {}
                    for p in range(8):
                        k.wait("scalar", "dv", xonv)
                        k.wait("scalar", "pe", lastA.get(p - 2, 0))
                        cpv = k.op("scalar", lambda s, p=p:
                                   s.activation(xob.sl(p % 2, 0, CH), xoc.sl(p, 0, CH),
                                                AF.Identity), "ac")
                        k.wait("tensor", "ac", cpv)
                        k.op("tensor", lambda t, p=p, s=(p == 0), e=(p == 7):
                             t.matmul(psG[0][:, :], onesk[:, :], xob.sl(p % 2, 0, CH),
                                      start=s, stop=e), "pe")
                        sqv = k.op("scalar", lambda s, p=p:
                                   s.activation(xob.sl(2 + p % 2, 0, CH), xob.sl(p % 2, 0, CH),
                                                AF.Square), "ac")
                        k.wait("tensor", "ac", sqv)
                        pev = k.op("tensor", lambda t, p=p, s=(p == 0), e=(p == 7):
                                   t.matmul(psG[1][:, :], onesk[:, :], xob.sl(2 + p % 2, 0, CH),
                                            start=s, stop=e), "pe")
                        lastA[p] = pev
                    k.wait("vector", "pe", pev)
                    acv = ln_from_sums(mu[:, :], rstd[:, :], st1, st2, D, CH)
                    k.wait("vector", "ac", acv)
                    k.wait("vector", "spill", spillv)
                    dvv = None
                    for p in range(8):
                        k.op("vector", lambda v, p=p:
                             v.tensor_sub(st2[:, :], xoc.sl(p, 0, CH), mu[:, :]), "dv")
                        dvv = k.op("vector", lambda v, p=p:
                                   v.tensor_mul(xoc.sl(p, 0, CH), st2[:, :],
                                                rstd[:, :]), "dv")
                        k.wait("sync", "dv", dvv)
                        k.dma("sync", dr["y0_sp"][128 * p:128 * p + 128, qc * CH:qc * CH + CH],
                              xoc.sl(p, 0, CH), "spill")
                    k.barrier()
                k.barrier()
                k.flush()
            tap("qp2", qp)
            tap("kp2", kp)
            tapT("vsl2_dbg", vsl, 2048)
            k.flush()

    def emit_layer(l, yfm):
        k.dma("sync", smalls[:, 4:8], dr["qsk_c"][l], "sml")
        k.dma("sync", smalls[:, 8:16], dr["g_c"][l], "sml")
        k.dma("sync", smalls[:, 16:24], dr["bln_c"][l], "sml")
        k.dma("sync", smalls[:, 24:32], dr["b2_c"][l], "sml")
        k.dma("sync", smalls[:, 32:54], dr["b1a_c"][l], "sml")
        k.dma("sync", smalls[:, 54:76], dr["b1g_c"][l], "sml")
        if l == 0:
            k.dma("gpsimd", tabs4[:, 0:T], dr["tabc"][:, :], "cst")
            k.dma("gpsimd", tabs4[:, T:2 * T], dr["tabs"][:, :], "cst")
            k.dma("gpsimd", tabs4[:, 2 * T:3 * T], dr["tabci"][:, :], "cst")
            k.dma("gpsimd", tabs4[:, 3 * T:4 * T], dr["tabsi"][:, :], "cst")
            k.dma("gpsimd", mask_b[:, 0:1024], dr["win"][:, :], "cst")
            k.dma("gpsimd", mask_b[:, 1024:1536], dr["win0"][:, :], "cst")
        k.barrier()

        with ExitStack() as ph:
            sb = lambda n_, s_, d_: ph.enter_context(nc.sbuf_tensor("%s_L%d" % (n_, l), s_, d_))
            kpb = FMB(sb("kpb", [128, 8 * 768], BF16), 768)
            vbl = sb("vbl", [128, 6 * VSW], BF16)
            ybc = FMB(sb("ybc", [128, 8 * 513], BF16), 513)
            qpc = FMB(sb("qpc", [128, 8 * CH], BF16), CH)
            qrp = FMB(sb("qrp2", [128, 2 * CH], BF16), CH)
            qsq = FMB(sb("qsq", [128, 2 * CH], BF16), CH)
            rsqf = sb("rsqf", [128, CH], F32)
            tA = sb("tA2", [128, CH], F32)
            tB = sb("tB2", [128, CH], F32)
            tC = sb("tC2", [128, CH], F32)
            st1 = sb("st1c", [128, CH], F32)
            wqb = [sb("wqd0", [128, 8 * 128], BF16), sb("wqd1", [128, 8 * 128], BF16)]
            wvb = FMB(sb("wvb", [128, 8 * 1024], BF16), 1024)
            ptb2 = [[sb("ptA0", [128, 1024], BF16), sb("ptA1", [128, 1024], BF16)],
                    [sb("ptB0", [128, 1024], BF16), sb("ptB1", [128, 1024], BF16)]]
            stg2 = [sb("stgA", [128, 1024], F32), sb("stgB", [128, 1024], F32)]
            obc = FMB(sb("obc2", [128, 8 * 512], BF16), 512)
            rcpd = sb("rcpd2", [128, 256], F32)
            denf = sb("denf2", [33, 512], F32)

            vset = k.op("vector", lambda v: v.memset(vbl[:, :], 1.0), "dv")

            for c in range(NCH):
                if c > 0:
                    for p in range(8):
                        k.op("vector", lambda v, p=p:
                             v.tensor_copy(kpb.sl(p, 0, 256), kpb.sl(p, 512, 256)), "dv")
                    for s4 in range(2):
                        k.op("vector", lambda v, s4=s4:
                             v.tensor_copy(vbl[:, s4 * VSW:(s4 + 1) * VSW],
                                           vbl[:, (4 + s4) * VSW:(5 + s4) * VSW]), "dv")
                for p in range(8):
                    if c == 0:
                        k.op("vector", lambda v, p=p: v.memset(ybc.sl(p, 0, 1), 0.0), "dv")
                    else:
                        k.op("vector", lambda v, p=p:
                             v.tensor_copy(ybc.sl(p, 0, 1), ybc.sl(p, 512, 1)), "dv")
                ycv = None
                for p in range(8):
                    ycv = k.op("vector", lambda v, p=p, c=c:
                               v.tensor_copy(ybc.sl(p, 1, 512), yfm.sl(p, 1 + c * CH, 512)), "dv")
                k.wait("tensor", "dv", ycv)

                def rhs_sh(kk, off, n):
                    return ybc.sl(kk, off + (0 if kk >= 4 else 1), n)

                qrp_rd = {0: 0, 1: 0}
                rope_dv = 0
                for (ofb, co, so, c1, c2) in ((0, 0, T, 4, 5), (8, 2 * T, 3 * T, 6, 7)):
                    for pr in range(4):
                        for half, p in ((0, pr), (1, pr + 4)):
                            wv = k.dma("gpsimd", wqb[half][:, :],
                                       dr["wqkv_qk_t"][l, ofb + p].rearrange("p k j -> p (k j)"),
                                       "wqd%d" % half, war="wqd%d" % half)
                            k.wait("tensor", "wqd%d" % half, wv)
                            gemm_wait_bank(half)
                            pev = None
                            for kk in range(8):
                                pev = k.op("tensor", lambda t, h2=half, kk=kk,
                                           s=(kk == 0), e=(kk == 7):
                                           t.matmul(psG[h2][:, :],
                                                    wqb[h2][:, 128 * kk:128 * kk + 128],
                                                    rhs_sh(kk, 0, CH), start=s, stop=e), "pe")
                            k.use("wqd%d" % half, "pe", pev)
                            k.wait("scalar", "pe", pev)
                            k.wait("scalar", "dv", qrp_rd[half])
                            k.op("scalar", lambda s, h2=half:
                                 s.copy(qrp.sl(h2, 0, CH), psG[h2][:, :]), "ac")
                            acv = k.op("scalar", lambda s, h2=half:
                                       s.activation(qsq.sl(h2, 0, CH), qrp.sl(h2, 0, CH),
                                                    AF.Square), "ac")
                            mark("ac", half, acv)
                        k.wait("tensor", "ac", k.cnt["ac"])
                        gemm_wait_bank(0)
                        k.op("tensor", lambda t: t.matmul(psG[0][:, :], bo32[:, :],
                                                          qsq.sl(0, 0, CH),
                                                          start=True, stop=False), "pe")
                        pev = k.op("tensor", lambda t: t.matmul(psG[0][:, :], bo32[:, :],
                                                                qsq.sl(1, 0, CH),
                                                                start=False, stop=True), "pe")
                        k.wait("scalar", "pe", pev)
                        k.wait("scalar", "dv", rope_dv)
                        k.op("scalar", lambda s: s.activation(st1[:, :], psG[0][:, :], AF.Ln), "ac")
                        acv = k.op("scalar", lambda s: s.activation(tC[:, :], st1[:, :],
                                                                    AF.Exp, scale=-0.5), "ac")
                        mark("ac", 0, acv)
                        k.wait("vector", "ac", acv)
                        k.op("vector", lambda v: v.tensor_scalar_min(rsqf[:, :], tC[:, :],
                                                                     1e12), "dv")
                        qrp_rd[0] = k.op("vector", lambda v, c1=c1: v.scalar_tensor_tensor(
                            tA[:, :], qrp.sl(0, 0, CH), smalls[:, c1:c1 + 1], rsqf[:, :],
                            ALU.mult, ALU.mult), "dv")
                        dvv = k.op("vector", lambda v, c2=c2: v.scalar_tensor_tensor(
                            tB[:, :], qrp.sl(1, 0, CH), smalls[:, c2:c2 + 1], rsqf[:, :],
                            ALU.mult, ALU.mult), "dv")
                        qrp_rd[1] = dvv
                        for (dp, tX, tY, sub) in ((pr, tA, tB, True), (pr + 4, tB, tA, False)):
                            k.op("vector", lambda v, tX=tX, co=co, c=c:
                                 v.tensor_mul(tC[:, :], tX[:, :],
                                              tabs4[:, co + c * CH:co + c * CH + CH]), "dv")
                            k.op("vector", lambda v, tY=tY, so=so, c=c:
                                 v.tensor_mul(st1[:, :], tY[:, :],
                                              tabs4[:, so + c * CH:so + c * CH + CH]), "dv")
                            if ofb == 0:
                                outap = qpc.sl(dp, 0, CH)
                            else:
                                outap = kpb.sl(dp, 256, CH)
                            if sub:
                                k.op("vector", lambda v, o=outap:
                                     v.tensor_sub(o, tC[:, :], st1[:, :]), "dv")
                            else:
                                k.op("vector", lambda v, o=outap:
                                     v.tensor_add(o, tC[:, :], st1[:, :]), "dv")
                        rope_dv = k.cnt["dv"]

                wvv = None
                for kk in range(8):
                    wvv = k.dma("gpsimd", wvb.sl(kk, 0, 1024),
                                dr["wqkv_v_s"][l, kk][:, :], "wvb",
                                war="wvb" if kk == 0 else None)
                k.wait("tensor", "wvb", wvv)
                k.wait("tensor", "dv", k.cnt["dv"])
                for ti in range(4):
                    for jh in range(2):
                        b = (ti * 2 + jh) % 2
                        gemm_wait_bank(b)
                        pev = None
                        for kk in range(8):
                            pev = k.op("tensor", lambda t, kk=kk, ti=ti, jh=jh, b=b,
                                       s=(kk == 0), e=(kk == 7):
                                       t.matmul(psG[b][:, :], rhs_sh(kk, ti * 128, 128),
                                                wvb.sl(kk, jh * 512, 512), start=s, stop=e), "pe")
                        k.wait("vector", "pe", pev)
                        k.wait("vector", "dv", vset)
                        dst = bass.AP(vbl, (2 + ti) * VSW + jh * 8 * 66,
                                      [[6 * VSW, 128], [66, 8], [1, 64]])
                        src = bass.AP(psG[b], 0, [[512, 128], [64, 8], [1, 64]])
                        dvv = k.op("vector", lambda v, d=dst, s5=src: v.tensor_copy(d, s5), "dv")
                        mark("dv", b, dvv)
                k.use("wvb", "pe", k.cnt["pe"])

                pairs = []
                for w in (2 * c, 2 * c + 1):
                    kts = [2, 3] if (w == 0 and c == 0) else [0, 1, 2, 3]
                    qoff = (w - 2 * c) * 256
                    for gq in range(4):
                        for sp in range(2):
                            pairs.append((w, tuple(kts), qoff, gq, sp))

                def emit_av1(rec):
                    idx, heads, kts, qoff, w = rec
                    par = idx % 2
                    voff = par * 256
                    nk = len(kts)
                    k.wait("tensor", "ac", gemm_ev.get(("dencp",), 0))
                    for hi, h, a in heads:
                        k.wait("tensor", "ac", gemm_ev[("exp", (par, hi))])
                        k.wait("tensor", "dv", gemm_ev.get(("pnorm", par), 0))
                        pev = None
                        for ki, kt in enumerate(kts):
                            slot = 2 * (w - 2 * c) + kt
                            vap2 = bass.AP(vbl, slot * VSW + 66 * h,
                                           [[6 * VSW, 128], [1, 65]])
                            pev = k.op("tensor", lambda t, va=vap2, hi=hi, ki=ki, par=par,
                                       voff=voff, s=(ki == 0), e=(ki == nk - 1):
                                       t.matmul(psV[hi][0:65, voff:voff + 256], va,
                                                ptb2[par][hi][:, ki * 256:ki * 256 + 256],
                                                start=s, stop=e), "pe")
                        gemm_ev[("ptfree", (par, hi))] = pev
                    acv = pn_copies(denf, voff, 256, voff)
                    gemm_ev[("dencp",)] = acv
                    return acv

                def emit_av2(rec, acv):
                    idx, heads, kts, qoff, w = rec
                    par = idx % 2
                    voff = par * 256
                    p2h = heads[0][1] // 2
                    dvv = pn_finish(obc.sl(p2h, qoff, 256, 0, 64),
                                    obc.sl(p2h, qoff, 256, 64, 128),
                                    denf, rcpd, st1, 256, acv, voff, voff)
                    gemm_ev[("pnorm", par)] = dvv

                prev = None
                pend = None
                for idx, (w, kts, qoff, gq, sp) in enumerate(pairs):
                    nk = len(kts)
                    par = idx % 2
                    h0 = 4 * gq + 2 * sp
                    heads = ((0, h0, 64 * sp), (1, h0 + 1, 64 * sp + 32))
                    for hi, h, a in heads:
                        k.wait("tensor", "dv", gemm_ev.get(("maskdv", hi), 0))
                        pev = None
                        for ki, kt in enumerate(kts):
                            k.op("tensor", lambda t, gq=gq, a=a, kt=kt, ki=ki, hi=hi, qoff=qoff, w=w, c=c:
                                 t.matmul(psS[hi][:, ki * 256:ki * 256 + 256],
                                          kpb.sl(gq, (w - 2 * c) * 256 + kt * 128, 128, a, a + 32),
                                          qpc.sl(gq, qoff, 256, a, a + 32),
                                          start=True, stop=False, tile_position=(a, 0)), "pe")
                            pev = k.op("tensor", lambda t, gq=gq, a=a, kt=kt, ki=ki, hi=hi, qoff=qoff, w=w, c=c:
                                       t.matmul(psS[hi][:, ki * 256:ki * 256 + 256],
                                                kpb.sl(gq + 4, (w - 2 * c) * 256 + kt * 128, 128, a, a + 32),
                                                qpc.sl(gq + 4, qoff, 256, a, a + 32),
                                                start=False, stop=True,
                                                tile_position=(a, 0)), "pe")
                        k.wait("vector", "pe", pev)
                        moff = 1024 if (w == 0) else 0
                        k.wait("vector", "ac", gemm_ev.get(("stg2", hi), 0))
                        mdv = k.op("vector", lambda v, hi=hi, nk=nk, moff=moff:
                                   v.tensor_add(stg2[hi][:, 0:nk * 256], psS[hi][:, 0:nk * 256],
                                                mask_b[:, moff:moff + nk * 256]), "dv")
                        gemm_ev[("maskdv", hi)] = mdv
                        k.wait("scalar", "dv", mdv)
                        k.wait("scalar", "pe", gemm_ev.get(("ptfree", (par, hi)), 0))
                        acv = k.op("scalar", lambda s, hi=hi, nk=nk, par=par:
                                   s.activation(ptb2[par][hi][:, 0:nk * 256], stg2[hi][:, 0:nk * 256],
                                                AF.Exp, scale=8.0), "ac")
                        gemm_ev[("exp", (par, hi))] = acv
                        gemm_ev[("stg2", hi)] = acv
                    if prev is not None:
                        acv1 = emit_av1(prev)
                        if pend is not None:
                            emit_av2(*pend)
                        pend = (prev, acv1)
                    prev = (idx, heads, kts, qoff, w)
                acv1 = emit_av1(prev)
                if pend is not None:
                    emit_av2(*pend)
                emit_av2(prev, acv1)
                obv = k.cnt["dv"]

                for p in range(8):
                    wv = k.dma("gpsimd", wqb[p % 2][:, :],
                               dr["wo2_t"][l, p].rearrange("p k j -> p (k j)"),
                               "wqd%d" % (p % 2), war="wqd%d" % (p % 2))
                    k.wait("tensor", "wqd%d" % (p % 2), wv)
                    k.wait("tensor", "dv", obv)
                    b = p % 2
                    gemm_wait_bank(b)
                    pev = None
                    for kk in range(8):
                        pev = k.op("tensor", lambda t, b=b, kk=kk, s=(kk == 0), e=(kk == 7):
                                   t.matmul(psG[b][:, :], wqb[b][:, 128 * kk:128 * kk + 128],
                                            obc.sl(kk, 0, 512), start=s, stop=e), "pe")
                    k.use("wqd%d" % b, "pe", pev)
                    k.wait("vector", "pe", pev)
                    dvv = k.op("vector", lambda v, p=p, b=b, c=c:
                               v.tensor_add(yfm.sl(p, 1 + c * CH, CH), psG[b][:, :],
                                            yfm.sl(p, 1 + c * CH, CH)), "dv")
                    mark("dv", b, dvv)
                k.barrier()
                k.flush()
            k.barrier()
        k.flush()

        with ExitStack() as ph:
            sb = lambda n_, s_, d_: ph.enter_context(nc.sbuf_tensor("%s_F%d" % (n_, l), s_, d_))
            ybc1 = FMB(sb("ybc1", [128, 8 * 1025], BF16), 1025)
            sqp = FMB(sb("sqp", [128, 2 * 513], BF16), 513)
            ybk = sb("ybk", [128, 8], F32)
            xnbc = FMB(sb("xnbc", [128, 8 * 1024], BF16), 1024)
            gg = FMB(sb("gg", [128, NKT2 * 1024], BF16), 1024)
            w1b = [sb("w1b0", [128, 2 * 1024], BF16), sb("w1b1", [128, 2 * 1024], BF16)]
            w2b = [sb("w2b0", [128, NKT2 * 128], BF16), sb("w2b1", [128, NKT2 * 128], BF16)]
            gel = [sb("gel0", [128, 512], BF16), sb("gel1", [128, 512], BF16)]
            mu = sb("mu3", [128, 1024], F32)
            rstd = sb("rstd3", [128, 1024], F32)
            st1 = sb("st1d", [128, 512], F32)
            st2 = sb("st2d", [128, 512], F32)
            st3 = sb("st3d", [128, 512], F32)

            for F in range(2):
                F0 = F * 1024
                for p in range(8):
                    if F == 0:
                        k.op("vector", lambda v, p=p: v.memset(ybc1.sl(p, 0, 1), 0.0), "dv")
                    else:
                        k.op("vector", lambda v, p=p:
                             v.tensor_copy(ybc1.sl(p, 0, 1), ybc1.sl(p, 1024, 1)), "dv")
                ycv = None
                for p in range(8):
                    ycv = k.op("vector", lambda v, p=p, F0=F0:
                               v.tensor_copy(ybc1.sl(p, 1, 1024), yfm.sl(p, F0 + 1, 1024)), "dv")
                if F == 0:
                    for p in range(8):
                        ycv = k.op("vector", lambda v, p=p:
                                   v.tensor_copy(ybk[:, p:p + 1], yfm.sl(p, 1024, 1)), "dv")
                k.wait("tensor", "dv", ycv)
                k.wait("scalar", "dv", ycv)

                def rhs_sh1(kk, off, n):
                    return ybc1.sl(kk, off + (0 if kk >= 4 else 1), n)

                for jc in range(2):
                    gemm_wait_bank(0)
                    gemm_wait_bank(1)
                    lastA = {}
                    sqv = {}
                    for p in range(8):
                        k.wait("scalar", "pe", lastA.get(p - 2, 0))
                        sqv[p] = k.op("scalar", lambda s, p=p, jc=jc:
                                      s.activation(sqp.sl(p % 2, 0, 513),
                                                   ybc1.sl(p, jc * 512, 513), AF.Square), "ac")
                        k.op("tensor", lambda t, p=p, jc=jc, s=(p == 0), e=(p == 7):
                             t.matmul(psG[0][:, :], onesk[:, :], rhs_sh1(p, jc * 512, 512),
                                      start=s, stop=e), "pe")
                        k.wait("tensor", "ac", sqv[p])
                        pev = k.op("tensor", lambda t, p=p, jc=jc, s=(p == 0), e=(p == 7):
                                   t.matmul(psG[1][:, :], onesk[:, :],
                                            sqp.sl(p % 2, (0 if p >= 4 else 1), 512),
                                            start=s, stop=e), "pe")
                        lastA[p] = pev
                    k.wait("vector", "pe", pev)
                    acv = ln_from_sums(mu[:, jc * 512:jc * 512 + 512],
                                       rstd[:, jc * 512:jc * 512 + 512], st1, st2, D, 512)
                k.wait("vector", "ac", acv)
                xnv = [0, 0]
                for jc in range(2):
                    for p in range(8):
                        src = yfm.sl(p, F0 + jc * 512 + (0 if p >= 4 else 1), 512)
                        k.op("vector", lambda v, p=p, jc=jc, src=src:
                             v.tensor_sub(st3[:, :], src, mu[:, jc * 512:jc * 512 + 512]), "dv")
                        if p >= 4 and F == 1 and jc == 0:
                            k.op("vector", lambda v, p=p:
                                 v.tensor_sub(st3[:, 0:1], ybk[:, p:p + 1], mu[:, 0:1]), "dv")
                        k.op("vector", lambda v, jc=jc:
                             v.tensor_mul(st1[:, :], st3[:, :],
                                          rstd[:, jc * 512:jc * 512 + 512]), "dv")
                        xnv[jc] = k.op("vector", lambda v, p=p, jc=jc:
                                       v.tensor_scalar(xnbc.sl(p, jc * 512, 512), st1[:, :],
                                                       smalls[:, 8 + p:9 + p],
                                                       smalls[:, 16 + p:17 + p],
                                                       ALU.mult, ALU.add), "dv")

                psFF = [psG[0], psG[1], psV[0], psV[1]]
                for i in range(NKT2):
                    wv = k.dma("gpsimd", w1b[i % 2][:, :],
                               dr["w1_t"][l, i].rearrange("p o k j -> p (o k j)"),
                               "w1b%d" % (i % 2), war="w1b%d" % (i % 2))
                    k.wait("tensor", "w1b%d" % (i % 2), wv)
                    for jc in range(2):
                        k.wait("tensor", "dv", xnv[jc])
                        ba = 2 * (i % 2)
                        bg = ba + 1
                        gemm_wait_bank(ba)
                        gemm_wait_bank(bg)
                        pev_g = None
                        for kk in range(8):
                            pev_g = k.op("tensor", lambda t, i=i, kk=kk, jc=jc, bg=bg,
                                         s=(kk == 0), e=(kk == 7):
                                         t.matmul(psFF[bg][:, :],
                                                  w1b[i % 2][:, 1024 + 128 * kk:1024 + 128 * kk + 128],
                                                  xnbc.sl(kk, jc * 512, 512), start=s, stop=e), "pe")
                        pev = None
                        for kk in range(8):
                            pev = k.op("tensor", lambda t, i=i, kk=kk, jc=jc, ba=ba,
                                       s=(kk == 0), e=(kk == 7):
                                       t.matmul(psFF[ba][:, :], w1b[i % 2][:, 128 * kk:128 * kk + 128],
                                                xnbc.sl(kk, jc * 512, 512), start=s, stop=e), "pe")
                        k.use("w1b%d" % (i % 2), "pe", pev)
                        k.wait("scalar", "pe", pev_g)
                        k.wait("scalar", "dv", gemm_ev.get(("gel", jc), 0))
                        gcv = k.op("scalar", lambda s, i=i, jc=jc, bg=bg:
                                   s.activation(gel[jc][:, :], psFF[bg][:, :], AF.Gelu,
                                                bias=smalls[:, 54 + i:55 + i]), "ac")
                        mark("ac", bg, gcv)
                        k.wait("vector", "ac", gcv)
                        k.wait("vector", "pe", pev)
                        dvv = k.op("vector", lambda v, i=i, jc=jc, ba=ba:
                                   v.scalar_tensor_tensor(gg.sl(i, jc * 512, 512), psFF[ba][:, :],
                                                          smalls[:, 32 + i:33 + i], gel[jc][:, :],
                                                          ALU.add, ALU.mult), "dv")
                        mark("dv", ba, dvv)
                        gemm_ev[("gel", jc)] = dvv
                ggv = k.cnt["dv"]

                for p in range(8):
                    wv = k.dma("gpsimd", w2b[p % 2][:, :],
                               dr["w2_t"][l, p].rearrange("p k j -> p (k j)"),
                               "w2b%d" % (p % 2), war="w2b%d" % (p % 2))
                    k.wait("tensor", "w2b%d" % (p % 2), wv)
                    k.wait("tensor", "dv", ggv)
                    for jc in range(2):
                        b = (2 * p + jc) % 4
                        gemm_wait_bank(b)
                        pev = None
                        for kk in range(NKT2):
                            pev = k.op("tensor", lambda t, p=p, kk=kk, jc=jc, b=b,
                                       s=(kk == 0), e=(kk == NKT2 - 1):
                                       t.matmul(psFF[b][:, :], w2b[p % 2][:, 128 * kk:128 * kk + 128],
                                                gg.sl(kk, jc * 512, 512), start=s, stop=e), "pe")
                        k.wait("vector", "pe", pev)
                        dvv = k.op("vector", lambda v, p=p, jc=jc, b=b, F0=F0:
                                   v.scalar_tensor_tensor(yfm.sl(p, F0 + 1 + jc * 512, 512),
                                                          psFF[b][:, :], smalls[:, 24 + p:25 + p],
                                                          yfm.sl(p, F0 + 1 + jc * 512, 512),
                                                          ALU.add, ALU.add), "dv")
                        mark("dv", b, dvv)
                        k.use("w2b%d" % (p % 2), "pe", pev)
                k.barrier()
                k.flush()
            k.barrier()
        k.flush()

    import os
    stop = stop or os.environ.get("KSTOP") or None
    emit_mla()

    if os.environ.get("NOYS"):
        with ExitStack() as ph:
            dum = FMB(ph.enter_context(nc.sbuf_tensor("dumout", [128, 8 * CH], F32)), CH)
            for c in range(NCH):
                dvv = None
                for p in range(8):
                    dvv = k.op("vector", lambda v, p=p: v.memset(dum.sl(p, 0, CH), 0.0), "dv")
                k.wait("sync", "dv", dvv)
                for p in range(8):
                    k.dma("sync", dr["y_out"][128 * p:128 * p + 128, c * CH:c * CH + CH],
                          dum.sl(p, 0, CH), "out")
            k.barrier()
            k.flush()
        return nc, top, k, tap_out

    with ExitStack() as rest:
        yfm = FMB(rest.enter_context(nc.sbuf_tensor("ystream", [128, 8 * (T + 1)], F32)),
                  T + 1)
        ylv = None
        for p in range(8):
            k.op("vector", lambda v, p=p: v.memset(yfm.sl(p, 0, 1), 0.0), "dv")
            ylv = k.dma("sync", yfm.sl(p, 1, T), dr["y0_sp"][128 * p:128 * p + 128, :], "y0ld")
        k.barrier()
        tap("y0", yfm, 1)

        nlay = DEPTH
        if stop in ("mla", "mlaproj"):
            nlay = 0
        elif stop and stop.startswith("l"):
            nlay = int(stop[1:]) + 1
        for l in range(nlay):
            emit_layer(l, yfm)
            tap("y_l%d" % l, yfm, 1)

        with ExitStack() as ph:
            sb = lambda n_, s_, d_: ph.enter_context(nc.sbuf_tensor(n_, s_, d_))
            xfc = FMB(sb("xfc3", [128, 8 * CH], F32), CH)
            outc = FMB(sb("outc", [128, 8 * CH], F32), CH)
            for c in range(NCH):
                xfv = None
                for p in range(8):
                    xfv = k.dma("sync", xfc.sl(p, 0, CH),
                                dr["xo_sp"][128 * p:128 * p + 128, c * CH:c * CH + CH], "xfc",
                                war="xfcbuf" if p == 0 else None)
                k.wait("vector", "xfc", xfv)
                dvv = None
                for p in range(8):
                    dvv = k.op("vector", lambda v, p=p, c=c:
                               v.tensor_add(outc.sl(p, 0, CH), xfc.sl(p, 0, CH),
                                            yfm.sl(p, 1 + c * CH, CH)), "dv")
                k.use("xfcbuf", "dv", dvv)

                k.wait("sync", "dv", dvv)
                for p in range(8):
                    k.dma("sync", dr["y_out"][128 * p:128 * p + 128, c * CH:c * CH + CH],
                          outc.sl(p, 0, CH), "out")
                k.wait("vector", "out", k.cnt["out"])
            k.barrier()
            k.flush()

    return nc, top, k, tap_out


def kernel(**ins):
    prep = _prep(ins)
    x = np.asarray(ins["L0_attn_x"])
    nc, top, k, _ = build_nc()
    with top:
        in_maps = []
        for b in range(8):
            m = {"xT": np.ascontiguousarray(x[b].T)}
            m.update(prep)
            in_maps.append(m)
        res = run_bass_kernel_spmd(nc, in_maps, core_ids=list(range(8)))
    return np.stack([np.ascontiguousarray(r["y_out"].T) for r in res.results]).astype(np.float32)
